# version 12
# baseline (speedup 1.0000x reference)
import numpy as np
import ml_dtypes
from contextlib import ExitStack
import concourse.bass as bass
import concourse.mybir as mybir
from concourse.bass_utils import run_bass_kernel_spmd

F32 = mybir.dt.float32
BF16 = mybir.dt.bfloat16
F16 = mybir.dt.float16
AF = mybir.ActivationFunctionType
ALU = mybir.AluOpType
AXX = mybir.AxisListType.X
BF = ml_dtypes.bfloat16

D = 2048
NCD = 16
HID = 1792
NHC = 14
NH = 16
SEQ = 4096
TOWN = 2048
NKT = 32
QG_NKT = [8, 16, 24, 32]
RS2 = float(1.0 / np.sqrt(128.0))
EPS_LN = 1e-5
EPS_QK = 1e-6
INV_D = 1.0 / 2048.0
TP = 512
NT = TP // 128


def positions_for_core(c):
    if c % 2 == 0:
        return [i for i in range(32) if i % 4 in (0, 3)]
    return [i for i in range(32) if i % 4 in (1, 2)]


class Sem:
    def __init__(self, nc, name):
        self.h = nc.alloc_semaphore(name=name)
        self.n = 0

    def inc(self, inst, k=1):
        inst.then_inc(self.h, k)
        self.n += k
        return self.n


class Builder:
    def __init__(self):
        self.nc = bass.Bass("TRN2", target_bir_lowering=False)
        self.sems = {}

    def sem(self, name):
        if name not in self.sems:
            self.sems[name] = Sem(self.nc, name)
        return self.sems[name]

    def w(self, eng, *names):
        for n in names:
            s = self.sem(n)
            if s.n > 0:
                eng.wait_ge(s.h, s.n)

    def barrier(self):
        nc = self.nc
        b = self.sem("bar")
        engs = [nc.tensor, nc.vector, nc.scalar, nc.sync, nc.gpsimd]
        for nm, s in list(self.sems.items()):
            if nm != "bar" and s.n > 0:
                nc.sync.wait_ge(s.h, s.n)
        for e in engs:
            e.drain()
            e.sem_inc(b.h, 1)
        b.n += 5
        for e in engs:
            e.wait_ge(b.h, b.n)

    def declare(self):
        nc = self.nc
        ap = lambda t: t.ap()
        self.i_xq = ap(nc.dram_tensor("xq", [TOWN, D], F32, kind="ExternalInput"))
        self.i_xk = ap(nc.dram_tensor("xk", [SEQ, D], F32, kind="ExternalInput"))
        self.i_masks = ap(nc.dram_tensor("masks", [32, 128, 512], BF16, kind="ExternalInput"))
        self.i_wqkv = ap(nc.dram_tensor("wqkv", [D, 3 * D], F32, kind="ExternalInput"))
        self.i_qln = ap(nc.dram_tensor("qln", [D], F32, kind="ExternalInput"))
        self.i_kln = ap(nc.dram_tensor("kln", [D], F32, kind="ExternalInput"))
        self.i_wout = ap(nc.dram_tensor("wout", [D, D], F32, kind="ExternalInput"))
        self.i_gw = ap(nc.dram_tensor("gw", [D, 8], F32, kind="ExternalInput"))
        self.i_eb = ap(nc.dram_tensor("eb", [8], F32, kind="ExternalInput"))
        self.i_w1 = ap(nc.dram_tensor("w1", [8, D, HID], F32, kind="ExternalInput"))
        self.i_w2 = ap(nc.dram_tensor("w2", [8, HID, D], F32, kind="ExternalInput"))
        self.i_w3 = ap(nc.dram_tensor("w3", [8, D, HID], F32, kind="ExternalInput"))
        self.i_sw1 = ap(nc.dram_tensor("sw1", [D, HID], F32, kind="ExternalInput"))
        self.i_sw2 = ap(nc.dram_tensor("sw2", [HID, D], F32, kind="ExternalInput"))
        self.i_sw3 = ap(nc.dram_tensor("sw3", [D, HID], F32, kind="ExternalInput"))
        self.o_out = ap(nc.dram_tensor("out", [TOWN, D], F32, kind="ExternalOutput"))
        self.s_k = ap(nc.dram_tensor("kscr", [D, SEQ], BF16, kind="Internal"))
        self.s_kl = ap(nc.dram_tensor("kscrl", [D, SEQ], BF16, kind="Internal"))
        self.s_v = ap(nc.dram_tensor("vscr", [SEQ, D], F32, kind="Internal"))
        self.s_q = ap(nc.dram_tensor("qscr", [D, TOWN], BF16, kind="Internal"))
        self.s_ql = ap(nc.dram_tensor("qscrl", [D, TOWN], BF16, kind="Internal"))
        self.s_ctx = ap(nc.dram_tensor("ctxscr", [D, TOWN], F32, kind="Internal"))
        self.s_x2 = ap(nc.dram_tensor("x2scr", [TOWN, D], F32, kind="Internal"))
        self.s_xb2 = ap(nc.dram_tensor("xb2scr", [D, TOWN], BF16, kind="Internal"))

        self.c_eye_bf = ap(nc.inline_tensor(np.eye(128, dtype=BF), name="eyebf"))
        self.c_eye_f = ap(nc.inline_tensor(np.eye(128, dtype=np.float32), name="eyef"))
        self.c_ones_row_f = ap(nc.inline_tensor(np.ones((1, 128), np.float32), name="onesrowf"))
        self.c_ones_col_f = ap(nc.inline_tensor(np.ones((128, 1), np.float32), name="onescolf"))
        self.c_ones_col_h = ap(nc.inline_tensor(np.ones((128, 1), np.float16), name="onescolh"))
        eh = np.zeros((128, 256), np.float32)
        for kc in range(16):
            eh[:, 16 * kc + kc] = 1.0
        self.c_ehead = ap(nc.inline_tensor(eh.astype(np.float16), name="ehead"))
        s8 = np.zeros((8, 8 * 128), np.float32)
        for e in range(8):
            s8[e, e * 128:(e + 1) * 128] = 1.0
        self.c_sel8 = ap(nc.inline_tensor(s8.astype(BF), name="sel8"))
        cp = np.zeros((128, 4), np.float32)
        cp[:, 0] = EPS_LN; cp[:, 1] = EPS_QK; cp[:, 2] = 0.0; cp[:, 3] = 1.0
        self.c_pool = ap(nc.inline_tensor(cp, name="cpool"))

    def alloc_globals(self):
        nc = self.nc
        A = nc.alloc_sbuf_tensor
        self.g_wslab_a = A("wslab_a", [128, NCD * 512], BF16)
        self.g_wslab_b = A("wslab_b", [128, NCD * 512], BF16)
        self.g_sqdump = A("sqdump", [128, D], BF16)
        self.g_eye_bf = A("eye_bf", [128, 128], BF16)
        self.g_eye_f = A("eye_f", [128, 128], F32)
        self.g_ones_row_f = A("ones_row_f", [1, 128], F32)
        self.g_ones_col_f = A("ones_col_f", [128, 1], F32)
        self.g_ones_col_h = A("ones_col_h", [128, 1], F16)
        self.g_ehead = A("ehead_s", [128, 256], F16)
        self.g_sel8 = A("sel8_s", [8, 1024], BF16)
        self.g_pool = A("pool_s", [128, 4], F32)
        self.g_gq = A("gq_s", [128, 16], F32)
        self.g_gk = A("gk_s", [128, 16], F32)
        self.g_Bneg = A("Bneg", [128, 16], F32)
        self.g_wt = A("wtslab", [128, 16 * 8], F32)
        self.g_wtT8 = A("wtT8", [8, TOWN], BF16)
        self.g_ebrow = A("ebrow", [1, 8], F32)
        self.g_ebb = A("ebb", [128, 8], F32)
        self.g_ghi = A("ghi", [128, 16 * 8], BF16)
        self.g_glo = A("glo", [128, 16 * 8], BF16)
        self.g_cols = A("cols", [128, 32], F32)
        self.g_km2 = A("kmax2", [16, 2], F32)
        self.g_bh = A("bh16", [16, 1], F32)
        self.g_bhrow = A("bhrow", [1, 16], F32)
        self.g_gl = A("glg", [128, 8], F32)
        self.g_ge = A("geg", [128, 8], F32)
        self.g_pr = A("prg", [128, 8], F32)
        self.g_b8 = A("b8g", [128, 8], F32)
        self.g_s1 = A("s1g", [128, 8], F32)
        nc.const_aps.register(F32, 0.0, self.g_pool[:, 2:3])
        nc.const_aps.register(F32, 1.0, self.g_pool[:, 3:4])

    def col(self, i):
        return self.g_cols[:, i:i + 1]

    def prologue(self):
        nc = self.nc
        sl = self.sem("sl"); pe = self.sem("pe"); ve = self.sem("ve")
        ctx = ExitStack()
        gw16 = ctx.enter_context(nc.sbuf_tensor("gw16", [16, 1024], F32))
        gwhi16 = ctx.enter_context(nc.sbuf_tensor("gwhi16", [16, 1024], BF16))
        gwlo16 = ctx.enter_context(nc.sbuf_tensor("gwlo16", [16, 1024], BF16))
        g16t = ctx.enter_context(nc.sbuf_tensor("g16t", [16, 128], BF16))
        g16q = ctx.enter_context(nc.sbuf_tensor("g16q", [16, 128], F32))
        g16k = ctx.enter_context(nc.sbuf_tensor("g16k", [16, 128], F32))
        for dst, src in [
            (self.g_eye_bf, self.c_eye_bf), (self.g_eye_f, self.c_eye_f),
            (self.g_ones_row_f, self.c_ones_row_f), (self.g_ones_col_f, self.c_ones_col_f),
            (self.g_ones_col_h, self.c_ones_col_h),
            (self.g_ehead, self.c_ehead), (self.g_sel8, self.c_sel8),
            (self.g_pool, self.c_pool),
        ]:
            sl.inc(nc.sync.dma_start(out=dst[:, :], in_=src), 16)
        sl.inc(nc.sync.dma_start(out=g16q[:, :],
                                 in_=self.i_qln.rearrange("(c p) -> c p", p=128)), 16)
        sl.inc(nc.sync.dma_start(out=g16k[:, :],
                                 in_=self.i_kln.rearrange("(c p) -> c p", p=128)), 16)
        sl.inc(nc.sync.dma_start(out=gw16[:, :],
                                 in_=self.i_gw.rearrange("(c p) e -> c (p e)", p=128)), 16)
        sl.inc(nc.sync.dma_start(out=self.g_ebrow[:, :], in_=self.i_eb[None, :]), 16)
        with nc.psum_tensor("ps_pro", [128, 128], F32) as ps:
            self.w(nc.tensor, "sl")
            pe.inc(nc.tensor.transpose(ps[:, 0:16], g16q[:, :], self.g_eye_f[0:16, 0:16]))
            self.w(nc.vector, "pe")
            ve.inc(nc.vector.tensor_copy(self.g_gq[:, :], ps[:, 0:16]))
            self.w(nc.tensor, "ve")
            pe.inc(nc.tensor.transpose(ps[:, 0:16], g16k[:, :], self.g_eye_f[0:16, 0:16]))
            self.w(nc.vector, "pe")
            ve.inc(nc.vector.tensor_copy(self.g_gk[:, :], ps[:, 0:16]))
            self.w(nc.tensor, "ve")
            pe.inc(nc.tensor.matmul(ps[:, 0:8], self.g_ones_row_f[0:1, :],
                                    self.g_ebrow[0:1, :], start=True, stop=True))
            self.w(nc.vector, "pe")
            ve.inc(nc.vector.tensor_copy(self.g_ebb[:, :], ps[:, 0:8]))
        ve.inc(nc.vector.tensor_copy(gwhi16[:, :], gw16[:, :]))
        self.w(nc.vector, "ve")
        ve.inc(nc.vector.tensor_tensor(gwlo16[:, :], gw16[:, :], gwhi16[:, :], ALU.subtract))
        with nc.psum_tensor("ps_pg", [128, 128], BF16) as psb:
            for j in range(8):
                for (src16, dst) in [(gwhi16, self.g_ghi), (gwlo16, self.g_glo)]:
                    self.w(nc.vector, "pe")
                    ve.inc(nc.vector.tensor_copy(
                        g16t[:, :],
                        src16.ap().rearrange("c (p e) -> c p e", e=8)[:, :, j]))
                    self.w(nc.tensor, "ve")
                    pe.inc(nc.tensor.transpose(psb[:, 0:16], g16t[:, :],
                                               self.g_eye_bf[0:16, 0:16]))
                    self.w(nc.vector, "pe")
                    ve.inc(nc.vector.tensor_copy(
                        dst.ap().rearrange("p (c e) -> p c e", e=8)[:, :, j],
                        psb[:, 0:16]))
        self.barrier()
        ctx.close()

    def p1(self):
        nc = self.nc
        with ExitStack() as ctx:
            E = ctx.enter_context
            st = dict(
                xbTh=E(nc.sbuf_tensor("xbTh", [128, NCD * TP], BF16)),
                xbTl=E(nc.sbuf_tensor("xbTl", [128, NCD * TP], BF16)),
                kraw=E(nc.sbuf_tensor("kraw", [128, NCD * TP], F32)),
                wfh=E(nc.sbuf_tensor("wfh", [128, NCD * 512], BF16)),
                wfl=E(nc.sbuf_tensor("wfl", [128, NCD * 512], BF16)),
                xsh=E(nc.sbuf_tensor("xsh", [128, D], BF16)),
                xsl=E(nc.sbuf_tensor("xsl", [128, D], BF16)),
                khl=E(nc.sbuf_tensor("khl", [128, 512], BF16)),
                kll=E(nc.sbuf_tensor("kll", [128, 512], BF16)),
                xt0=E(nc.sbuf_tensor("xt0", [128, D], F32)),
                xt1=E(nc.sbuf_tensor("xt1", [128, D], F32)),
                xbtok=E(nc.sbuf_tensor("xbtok", [128, D], F32)),
                sq0=E(nc.sbuf_tensor("sq0", [128, 512], F16)),
                sq1=E(nc.sbuf_tensor("sq1", [128, 512], F16)),
                t1f=E(nc.sbuf_tensor("t1f", [128, 512], F32)),
                t2f=E(nc.sbuf_tensor("t2f", [128, 512], F32)),
                sA=E(nc.sbuf_tensor("sArow", [1, TP], F32)),
                sB=E(nc.sbuf_tensor("sBrow", [1, TP], F32)),
                mrow=E(nc.sbuf_tensor("mrow", [1, TP], F32)),
                rrow=E(nc.sbuf_tensor("rrow", [1, TP], F32)),
                trow=E(nc.sbuf_tensor("trow", [1, TP], F32)),
                kn2=E(nc.sbuf_tensor("kn2", [16, SEQ], F16)),
                qn2=E(nc.sbuf_tensor("qn2", [16, TOWN], F16)),
                vtmp=E(nc.sbuf_tensor("vtmp", [128, 512], F32)),
                vtok=E(nc.sbuf_tensor("vtok", [128, TP], F32)),
                ps_p0=E(nc.psum_tensor("ps_p0", [128, 512], F32)),
                ps_p1=E(nc.psum_tensor("ps_p1", [128, 512], F32)),
                ps_sA=E(nc.psum_tensor("ps_sA", [1, 512], F32)),
                ps_sB=E(nc.psum_tensor("ps_sB", [1, 512], F32)),
                ps_s16=E(nc.psum_tensor("ps_s16", [16, 512], F32)),
                ps_tb=E(nc.psum_tensor("ps_tb1", [128, 1024], BF16)),
                ps_b0=E(nc.psum_tensor("ps_b0", [128, 512], F32)),
                ps_b1=E(nc.psum_tensor("ps_b1", [128, 512], F32)),
            )
            self._wf = {"tick": 0, "used": [0, 0], "q": []}
            for p in range(8):
                self._p1_stepA(p, "kv", st)
                self._p1_proj(p, "k", st)
                self._p1_proj_v(p, st)
            for p in range(4):
                self._p1_stepA(p, "q", st)
                self._p1_proj(p, "q", st)
            self._bounds_ps = st["ps_b0"]
            self._p1_bounds(st["kn2"], st["qn2"])
        self.barrier()

    def _xbT_rhs(self, slab, ci):
        v = slab.ap()[:, :]
        return v.rearrange("p (i ct) -> p i ct", i=NT)[:, :, ci * 128:(ci + 1) * 128]

    def _p1_stepA(self, pidx, kind, st):
        nc = self.nc
        xl = self.sem("xl"); pe = self.sem("pe"); ve = self.sem("ve"); ae = self.sem("ae")
        src = self.i_xk if kind == "kv" else self.i_xq
        row0 = pidx * TP
        xbtok = st["xbtok"]
        for i in range(NT):
            xt = st[f"xt{i % 2}"]
            if i >= 2:
                self.w(nc.sync, "ve", "ae")
            xA = self.sem(f"xA{i % 2}")
            v_xl = xA.inc(nc.sync.dma_start(
                out=xt[:, :], in_=src[row0 + i * 128: row0 + (i + 1) * 128, :]), 16)
            nc.vector.wait_ge(xA.h, v_xl)
            ve.inc(nc.vector.reduce_sum(self.col(0 + (i % 2)), xt[:, :], axis=AXX))
            nc.scalar.wait_ge(xA.h, v_xl)
            self.w(nc.scalar, "ve")
            ae.inc(nc.scalar.activation(self.g_sqdump[:, :], xt[:, :], AF.Square,
                                        accum_out=self.col(2 + (i % 2))))
            c_m = self.col(4 + (i % 2)); c_v = self.col(6)
            c_r = self.col(8 + (i % 2)); c_t = self.col(7)
            self.w(nc.vector, "ae")
            ve.inc(nc.vector.tensor_scalar(c_m, self.col(0 + (i % 2)), INV_D, None, ALU.mult))
            ve.inc(nc.vector.tensor_scalar(c_t, self.col(2 + (i % 2)), INV_D, None, ALU.mult))
            self.w(nc.vector, "ve")
            ve.inc(nc.vector.tensor_tensor(self.col(12), c_m, c_m, ALU.mult))
            self.w(nc.vector, "ve")
            ve.inc(nc.vector.tensor_tensor(c_v, c_t, self.col(12), ALU.subtract))
            self.w(nc.scalar, "ve")
            ae.inc(nc.scalar.activation(self.col(13), c_v, AF.Ln,
                                        bias=self.g_pool[:, 0:1], scale=1.0))
            self.w(nc.scalar, "ae")
            ae.inc(nc.scalar.activation(c_r, self.col(13), AF.Exp, bias=0.0, scale=-0.5))
            self.w(nc.vector, "ae", "pe")
            ve.inc(nc.vector.tensor_scalar(xbtok[:, :], xt[:, :], c_m, c_r,
                                           ALU.subtract, ALU.mult))
            self.w(nc.vector, "ve")
            ve.inc(nc.vector.tensor_copy(st["xsh"][:, :], xbtok[:, :]))
            self.w(nc.vector, "ve")
            ve.inc(nc.vector.tensor_tensor(st["xsl"][:, :], xbtok[:, :], st["xsh"][:, :],
                                           ALU.subtract))
            nc.tensor.wait_ge(ve.h, ve.n)
            for (tok, slab) in [(st["xsh"], st["xbTh"]), (st["xsl"], st["xbTl"])]:
                for c4 in range(4):
                    self.w(nc.tensor, "ve")
                    for j in range(4):
                        ci = c4 * 4 + j
                        mm = nc.tensor.transpose(st["ps_tb"][:, j * 128:(j + 1) * 128],
                                                 tok[:, ci * 128:(ci + 1) * 128],
                                                 self.g_eye_bf[:, :])
                    pe.inc(mm)
                    nc.vector.wait_ge(pe.h, pe.n)
                    ve.inc(nc.vector.tensor_copy(
                        slab[:, i * D + c4 * 512: i * D + (c4 + 1) * 512],
                        st["ps_tb"][:, 0:512]))

    def _proj_half(self, st, colbase):
        nc = self.nc
        ve = self.sem("ve")
        t = self._wf
        b = t["tick"] % 2
        t["tick"] += 1
        ws = self.sem(f"wf{b}")
        nc.gpsimd.wait_ge(self.sem("pe").h, t["used"][b])
        for ci in range(NCD):
            ws.inc(nc.gpsimd.dma_start(
                out=st["wfh"][:, ci * 512 + b * 256: ci * 512 + b * 256 + 256],
                in_=self.i_wqkv[ci * 128:(ci + 1) * 128, colbase:colbase + 256]), 16)
        v_hi = ws.n
        nc.vector.wait_ge(ws.h, v_hi)
        wfh_half = st["wfh"].ap().rearrange("p (c u) -> p c u", u=512)[:, :, b * 256:(b + 1) * 256]
        wfl_half = st["wfl"].ap().rearrange("p (c u) -> p c u", u=512)[:, :, b * 256:(b + 1) * 256]
        v_ng = ve.inc(nc.vector.tensor_scalar(wfl_half, wfh_half, -1.0, None, ALU.mult))
        nc.gpsimd.wait_ge(ve.h, v_ng)
        for ci in range(NCD):
            ws.inc(nc.gpsimd.dma_start(
                out=st["wfl"][:, ci * 512 + b * 256: ci * 512 + b * 256 + 256],
                in_=self.i_wqkv[ci * 128:(ci + 1) * 128, colbase:colbase + 256],
                accum_op=ALU.add), 16)
        t["q"].append((b, ws, ws.n))

    def _p1_proj(self, pidx, kind, st):
        nc = self.nc
        pe = self.sem("pe"); ve = self.sem("ve"); ae = self.sem("ae")
        kraw = st["kraw"]
        colb0 = D if kind == "k" else 0
        self._proj_half(st, colb0)
        evac_done = {}
        for hw in range(8):
            if hw + 1 < 8:
                self._proj_half(st, colb0 + (hw + 1) * 256)
            b, ws, vw = self._wf["q"].pop(0)
            nc.tensor.wait_ge(ws.h, vw)
            for oc2 in range(2):
                kc = hw * 2 + oc2
                psp = (st["ps_p0"], st["ps_p1"])[kc % 2]
                if kc >= 2:
                    nc.tensor.wait_ge(ve.h, evac_done[kc - 2])
                nterm = 3 * NCD
                ti = 0
                for (wsl_, xsl_) in [(st["wfh"], st["xbTh"]), (st["wfh"], st["xbTl"]),
                                     (st["wfl"], st["xbTh"])]:
                    for ci in range(NCD):
                        mm = nc.tensor.matmul(
                            psp[:, :],
                            wsl_[:, ci * 512 + b * 256 + oc2 * 128: ci * 512 + b * 256 + (oc2 + 1) * 128],
                            self._xbT_rhs(xsl_, ci),
                            start=(ti == 0), stop=(ti == nterm - 1))
                        ti += 1
                pe.inc(mm)
                self.w(nc.vector, "pe", "sl")
                dst = kraw[:, kc * TP:(kc + 1) * TP]
                ve.inc(nc.vector.tensor_copy(dst, psp[:, :]))
                evac_done[kc] = ve.n
                if oc2 == 1:
                    self._wf["used"][b] = pe.n
                self.w(nc.scalar, "ve")
                nc.scalar.wait_ge(pe.h, pe.n)
                sq = st[f"sq{kc % 2}"]
                ae.inc(nc.scalar.activation(sq[:, :], dst, AF.Square))
                self.w(nc.tensor, "ae")
                pe.inc(nc.tensor.matmul(st["ps_sA"][:, :], self.g_ones_col_f[:, 0:1], dst,
                                        start=(kc == 0), stop=(kc == NCD - 1)))
                pe.inc(nc.tensor.matmul(st["ps_sB"][:, :], self.g_ones_col_h[:, 0:1],
                                        sq[:, :], start=(kc == 0), stop=(kc == NCD - 1)))
        if True:
            pass
        self.w(nc.vector, "pe")
        ve.inc(nc.vector.tensor_copy(st["sA"][0:1, :], st["ps_sA"][:, :]))
        ve.inc(nc.vector.tensor_copy(st["sB"][0:1, :], st["ps_sB"][:, :]))
        self.w(nc.vector, "ve")
        ve.inc(nc.vector.tensor_scalar(st["mrow"][0:1, :], st["sA"][0:1, :], INV_D, None, ALU.mult))
        ve.inc(nc.vector.tensor_scalar(st["sB"][0:1, :], st["sB"][0:1, :], INV_D, None, ALU.mult))
        self.w(nc.vector, "ve")
        ve.inc(nc.vector.tensor_tensor(st["trow"][0:1, :], st["mrow"][0:1, :],
                                       st["mrow"][0:1, :], ALU.mult))
        self.w(nc.vector, "ve")
        ve.inc(nc.vector.tensor_tensor(st["rrow"][0:1, :], st["sB"][0:1, :],
                                       st["trow"][0:1, :], ALU.subtract))
        self.w(nc.scalar, "ve")
        ae.inc(nc.scalar.activation(st["trow"][0:1, :], st["rrow"][0:1, :], AF.Ln,
                                    bias=self.g_pool[0:1, 1:2], scale=1.0))
        self.w(nc.scalar, "ae")
        ae.inc(nc.scalar.activation(st["rrow"][0:1, :], st["trow"][0:1, :], AF.Exp,
                                    bias=0.0, scale=-0.5))
        self._p1_normalize_flush(pidx, kind, st)

    def _p1_normalize_flush(self, pidx, kind, st):
        nc = self.nc
        pe = self.sem("pe"); ve = self.sem("ve"); ae = self.sem("ae"); sl = self.sem("sl")
        kraw = st["kraw"]; t1f = st["t1f"]; t2f = st["t2f"]
        gam = self.g_gk if kind == "k" else self.g_gq
        scr = (self.s_k, self.s_kl) if kind == "k" else (self.s_q, self.s_ql)
        n2sb = st["kn2"] if kind == "k" else st["qn2"]
        row0 = pidx * TP
        self.w(nc.tensor, "ae", "ve")
        pe.inc(nc.tensor.matmul(st["ps_b0"][:, :], self.g_ones_row_f[0:1, :],
                                st["mrow"][0:1, :], start=True, stop=True))
        pe.inc(nc.tensor.matmul(st["ps_b1"][:, :], self.g_ones_row_f[0:1, :],
                                st["rrow"][0:1, :], start=True, stop=True))
        v_bc = pe.n
        for kc in range(NCD):
            sli = kraw[:, kc * TP:(kc + 1) * TP]
            nc.vector.wait_ge(pe.h, v_bc)
            ve.inc(nc.vector.tensor_tensor(t1f[:, :], sli, st["ps_b0"][:, :], ALU.subtract))
            self.w(nc.vector, "ve")
            ve.inc(nc.vector.tensor_tensor(t2f[:, :], t1f[:, :], st["ps_b1"][:, :], ALU.mult))
            self.w(nc.vector, "ve")
            if kind == "k":
                ve.inc(nc.vector.tensor_scalar(sli, t2f[:, :], gam[:, kc:kc + 1], None, ALU.mult))
            else:
                ve.inc(nc.vector.tensor_scalar(sli, t2f[:, :], gam[:, kc:kc + 1], RS2,
                                               ALU.mult, ALU.mult))
            self.w(nc.scalar, "ve")
            nc.scalar.wait_ge(pe.h, pe.n)
            sq = st[f"sq{kc % 2}"]
            ae.inc(nc.scalar.activation(sq[:, :], sli, AF.Square))
            nc.tensor.wait_ge(ae.h, ae.n)
            pe.inc(nc.tensor.matmul(st["ps_s16"][:, :],
                                    self.g_ehead[:, kc * 16:(kc + 1) * 16],
                                    sq[:, :], start=(kc == 0), stop=(kc == NCD - 1)))
            self.w(nc.vector, "sl")
            ve.inc(nc.vector.tensor_copy(st["khl"][:, :], sli))
            self.w(nc.vector, "ve")
            ve.inc(nc.vector.tensor_tensor(st["kll"][:, :], sli, st["khl"][:, :],
                                           ALU.subtract))
            self.w(nc.sync, "ve")
            sl.inc(nc.sync.dma_start(
                out=scr[0][kc * 128:(kc + 1) * 128, row0:row0 + TP],
                in_=st["khl"][:, :]), 16)
            sl.inc(nc.sync.dma_start(
                out=scr[1][kc * 128:(kc + 1) * 128, row0:row0 + TP],
                in_=st["kll"][:, :]), 16)
        self.w(nc.vector, "pe")
        ve.inc(nc.vector.tensor_copy(n2sb[:, row0:row0 + TP], st["ps_s16"][:, :]))

    def _p1_proj_v(self, pidx, st):
        nc = self.nc
        pe = self.sem("pe"); ve = self.sem("ve"); wl = self.sem("wl"); sl = self.sem("sl")
        row0 = pidx * TP
        vtmp = st["vtmp"]; vtok = st["vtok"]
        self._proj_half(st, 2 * D)
        for hw in range(8):
            if hw + 1 < 8:
                self._proj_half(st, 2 * D + (hw + 1) * 256)
            b, ws, vw = self._wf["q"].pop(0)
            nc.tensor.wait_ge(ws.h, vw)
            for oc2 in range(2):
                vc = hw * 2 + oc2
                psp = (st["ps_p0"], st["ps_p1"])[vc % 2]
                self.w(nc.tensor, "ve")
                nterm = 3 * NCD
                ti = 0
                for (wsl_, xsl_) in [(st["wfh"], st["xbTh"]), (st["wfh"], st["xbTl"]),
                                     (st["wfl"], st["xbTh"])]:
                    for ci in range(NCD):
                        mm = nc.tensor.matmul(
                            psp[:, :],
                            wsl_[:, ci * 512 + b * 256 + oc2 * 128: ci * 512 + b * 256 + (oc2 + 1) * 128],
                            self._xbT_rhs(xsl_, ci),
                            start=(ti == 0), stop=(ti == nterm - 1))
                        ti += 1
                pe.inc(mm)
                if oc2 == 1:
                    self._wf["used"][b] = pe.n
                self.w(nc.vector, "pe")
                ve.inc(nc.vector.tensor_copy(vtmp[:, :], psp[:, :]))
                nc.tensor.wait_ge(ve.h, ve.n)
                for j in range(4):
                    mm = nc.tensor.transpose(st["ps_b0"][:, j * 128:(j + 1) * 128],
                                             vtmp[:, j * 128:(j + 1) * 128],
                                             self.g_eye_f[:, :])
                pe.inc(mm)
                self.w(nc.vector, "pe")
                nc.vector.wait_ge(sl.h, sl.n)
                ve.inc(nc.vector.tensor_copy(vtok[:, :], st["ps_b0"][:, :]))
                self.w(nc.sync, "ve")
                sl.inc(nc.sync.dma_start(
                    out=self.s_v[row0:row0 + TP, vc * 128:(vc + 1) * 128]
                        .rearrange("(tt p) c -> p tt c", p=128),
                    in_=vtok.ap().rearrange("p (tt c) -> p tt c", c=128)), 16)

    def _p1_bounds(self, kn2, qn2):
        nc = self.nc
        pe = self.sem("pe"); ve = self.sem("ve"); ae = self.sem("ae")
        self.w(nc.vector, "pe", "ae", "sl")
        km = self.g_km2
        ve.inc(nc.vector.reduce_max(km[:, 0:1], kn2[:, :], axis=AXX))
        ve.inc(nc.vector.reduce_max(km[:, 1:2], qn2[:, :], axis=AXX))
        self.w(nc.scalar, "ve")
        ae.inc(nc.scalar.activation(km[:, 0:1], km[:, 0:1], AF.Sqrt))
        self.w(nc.scalar, "ae")
        ae.inc(nc.scalar.activation(km[:, 1:2], km[:, 1:2], AF.Sqrt))
        self.w(nc.vector, "ae")
        ve.inc(nc.vector.tensor_tensor(self.g_bh[:, :], km[:, 0:1], km[:, 1:2], ALU.mult))
        psb = self._bounds_ps
        self.w(nc.tensor, "ve")
        pe.inc(nc.tensor.transpose(psb[0:1, 0:16], self.g_bh[:, :],
                                   self.g_eye_f[0:16, 0:16]))
        self.w(nc.vector, "pe")
        ve.inc(nc.vector.tensor_scalar(self.g_bhrow[0:1, :], psb[0:1, 0:16],
                                       -1.0, None, ALU.mult))
        self.w(nc.tensor, "ve")
        pe.inc(nc.tensor.matmul(psb[:, 0:16], self.g_ones_row_f[0:1, :],
                                self.g_bhrow[0:1, :], start=True, stop=True))
        self.w(nc.vector, "pe")
        ve.inc(nc.vector.tensor_copy(self.g_Bneg[:, :], psb[:, 0:16]))

    def p2p3(self):
        nc = self.nc
        with ExitStack() as ctx:
            E = ctx.enter_context
            st = dict(
                masksl=E(nc.sbuf_tensor("masksl", [128, 8 * 512], BF16)),
                qh0=E(nc.sbuf_tensor("qh0", [128, 512], BF16)),
                qh1=E(nc.sbuf_tensor("qh1", [128, 512], BF16)),
                ql0=E(nc.sbuf_tensor("ql0_t", [128, 512], BF16)),
                ql1=E(nc.sbuf_tensor("ql1_t", [128, 512], BF16)),
                kh0=E(nc.sbuf_tensor("kh0", [128, 8 * 128], BF16)),
                kh1=E(nc.sbuf_tensor("kh1", [128, 8 * 128], BF16)),
                kl0=E(nc.sbuf_tensor("kl0", [128, 8 * 128], BF16)),
                kl1=E(nc.sbuf_tensor("kl1", [128, 8 * 128], BF16)),
                vh0=E(nc.sbuf_tensor("vh0", [128, 8 * 128], F32)),
                vh1=E(nc.sbuf_tensor("vh1", [128, 8 * 128], F32)),
                pf0=E(nc.sbuf_tensor("pf0", [128, 1024], F32)),
                pf1=E(nc.sbuf_tensor("pf1", [128, 1024], F32)),
                pbm0=E(nc.sbuf_tensor("pbm0", [128, 512], F32)),
                pbm1=E(nc.sbuf_tensor("pbm1", [128, 512], F32)),
                pbm2=E(nc.sbuf_tensor("pbm2", [128, 512], F32)),
                pbm3=E(nc.sbuf_tensor("pbm3", [128, 512], F32)),
                dsum0=E(nc.sbuf_tensor("dsum0", [128, 512], F32)),
                dsum1=E(nc.sbuf_tensor("dsum1", [128, 512], F32)),
                rbsb=E(nc.sbuf_tensor("rbsb", [128, 512], F32)),
                drow=E(nc.sbuf_tensor("drow", [1, 512], F32)),
                ctxn0=E(nc.sbuf_tensor("ctxn0", [128, 512], F32)),
                ctxn1=E(nc.sbuf_tensor("ctxn1", [128, 512], F32)),
                ctxb0=E(nc.sbuf_tensor("ctxb0", [128, 128], F32)),
                ctxb1=E(nc.sbuf_tensor("ctxb1", [128, 128], F32)),
                wob0=E(nc.sbuf_tensor("wob0", [128, 512], F32)),
                wob1=E(nc.sbuf_tensor("wob1", [128, 512], F32)),
                wob2=E(nc.sbuf_tensor("wob2", [128, 512], F32)),
                wob3=E(nc.sbuf_tensor("wob3", [128, 512], F32)),
                ao=E(nc.sbuf_tensor("aorow", [128, D], F32)),
                rowf=E(nc.sbuf_tensor("rowf", [128, D], F32)),
                xhiT=E(nc.sbuf_tensor("xhiT", [128, D], BF16)),
                xloT=E(nc.sbuf_tensor("xloT", [128, D], BF16)),
                xhitok=E(nc.sbuf_tensor("xhitok", [128, D], BF16)),
                xlotok=E(nc.sbuf_tensor("xlotok", [128, D], BF16)),
                ps_S0=E(nc.psum_tensor("ps_S0", [128, 1024], F32)),
                ps_S1=E(nc.psum_tensor("ps_S1", [128, 1024], F32)),
                ps_ctx=E(nc.psum_tensor("ps_ctx", [128, 512], F32)),
                ps_D=E(nc.psum_tensor("ps_D", [1, 512], F32)),
                ps_tb=E(nc.psum_tensor("ps_tb", [128, 1024], BF16)),
                ps_g=E(nc.psum_tensor("ps_g", [128, 512], F32)),
            )
            self._p3_prev_last = 0
            for g in range(4):
                self._p2_group(g, st)
                self._p3_group(g, st)
        self.barrier()

    def _p2_load_q(self, g, h, st):
        nc = self.nc
        ql = self.sem("ql"); pe = self.sem("pe")
        if not hasattr(self, "_q_loaded"):
            self._q_loaded = [0, 0]
            self._kv_queue = []
            self._kv_tick = 0
            self._pf_cons = [0, 0]
            self._ctx_flushed = [0, 0]
            self._kv_first = True
            self._q_first = True
        if not (self._q_first and h < 2):
            nc.sync.wait_ge(pe.h, pe.n)
        if h >= 1:
            self._q_first = False
        qs = self.sem(f"ql{h % 2}")
        qs.inc(nc.sync.dma_start(
            out=st[f"qh{h % 2}"][:, :],
            in_=self.s_q[h * 128:(h + 1) * 128, g * 512:(g + 1) * 512]), 16)
        self._q_loaded[h % 2] = qs.inc(nc.sync.dma_start(
            out=st[f"ql{h % 2}"][:, :],
            in_=self.s_ql[h * 128:(h + 1) * 128, g * 512:(g + 1) * 512]), 16)

    def _p2_load_kv(self, g, h, cidx, st, nkt):
        nc = self.nc
        hl = self.sem("hl"); pe = self.sem("pe")
        if not self._kv_first:
            nc.sync.wait_ge(pe.h, pe.n)
        self._kv_first = False
        n = min(8, nkt - cidx * 8)
        b = self._kv_tick % 2
        self._kv_tick += 1
        hs = self.sem(f"hl{b}")
        hs.inc(nc.sync.dma_start(
            out=st[f"kh{b}"][:, 0:n * 128],
            in_=self.s_k[h * 128:(h + 1) * 128, cidx * 1024: cidx * 1024 + n * 128]), 16)
        hs.inc(nc.sync.dma_start(
            out=st[f"kl{b}"][:, 0:n * 128],
            in_=self.s_kl[h * 128:(h + 1) * 128, cidx * 1024: cidx * 1024 + n * 128]), 16)
        self._kv_queue.append((b, hs.inc(nc.sync.dma_start(
            out=st[f"vh{b}"].ap()[:, 0:n * 128].rearrange("p (kt c) -> p kt c", c=128),
            in_=self.s_v[cidx * 1024: cidx * 1024 + n * 128, h * 128:(h + 1) * 128]
                .rearrange("(kt p) c -> p kt c", p=128)), 16)))

    def _p2_group(self, g, st):
        nc = self.nc
        nkt = QG_NKT[g]
        pe = self.sem("pe"); ve = self.sem("ve"); ae = self.sem("ae")
        sl = self.sem("sl"); hl = self.sem("hl"); ql = self.sem("ql")
        self.w(nc.sync, "ve", "pe")
        for j in range(8):
            sl.inc(nc.sync.dma_start(out=st["masksl"][:, j * 512:(j + 1) * 512],
                                     in_=self.i_masks[g * 8 + j]), 16)
        v_m = sl.n
        nchk = (nkt + 7) // 8
        self._p2_load_q(g, 0, st)
        self._p2_load_kv(g, 0, 0, st, nkt)
        for h in range(NH):
            if h + 1 < NH:
                self._p2_load_q(g, h + 1, st)
            qh = st[f"qh{h % 2}"]; qlo = st[f"ql{h % 2}"]
            nc.tensor.wait_ge(self.sem(f"ql{h % 2}").h, self._q_loaded[h % 2])
            nc.tensor.wait_ge(sl.h, v_m)
            self.w(nc.tensor, "ve")
            exp_done = []
            cons_done = {}
            presum_done = {}
            band0 = 8 * g
            npair = nkt // 2
            pend = None

            def emit_cons(jj, srcs, ds, koff_, vh_):
                nc.tensor.wait_ge(ve.h, presum_done[jj])
                for half in range(2):
                    kt = jj * 2 + half
                    nc.tensor.matmul(st["ps_ctx"][:, :],
                                     vh_[:, (koff_ + half) * 128:(koff_ + half + 1) * 128],
                                     srcs[half],
                                     start=(kt == 0), stop=(kt == nkt - 1))
                mm_ = nc.tensor.matmul(st["ps_D"][:, :], self.g_ones_col_f[:, 0:1],
                                       ds[:, :], start=(jj == 0), stop=(jj == npair - 1))
                pe.inc(mm_)
                cons_done[jj] = pe.n

            for jp in range(npair):
                cidx = jp // 4
                if jp % 4 == 0:
                    if pend is not None:
                        emit_cons(*pend)
                        pend = None
                    if cidx + 1 < nchk:
                        self._p2_load_kv(g, h, cidx + 1, st, nkt)
                    elif h + 1 < NH:
                        self._p2_load_kv(g, h + 1, 0, st, QG_NKT[g])
                    self._kv_cur = self._kv_queue.pop(0)
                    nc.tensor.wait_ge(self.sem(f"hl{self._kv_cur[0]}").h, self._kv_cur[1])
                kh = st[f"kh{self._kv_cur[0]}"]; vh = st[f"vh{self._kv_cur[0]}"]
                klo = st[f"kl{self._kv_cur[0]}"]
                koff = (jp % 4) * 2
                psS = (st["ps_S0"], st["ps_S1"])[jp % 2]
                if jp >= 2:
                    nc.tensor.wait_ge(ae.h, exp_done[jp - 2])
                for half in range(2):
                    sliA = slice((koff + half) * 128, (koff + half + 1) * 128)
                    for ti, (kk, qq) in enumerate([(kh, qh), (kh, qlo), (klo, qh)]):
                        mm = nc.tensor.matmul(
                            psS[:, half * 512:(half + 1) * 512],
                            kk[:, sliA], qq[:, :], start=(ti == 0), stop=(ti == 2))
                pe.inc(mm)
                v_S = pe.n
                if pend is not None:
                    emit_cons(*pend)
                    pend = None
                nc.scalar.wait_ge(pe.h, v_S)
                if jp >= 2:
                    nc.scalar.wait_ge(ve.h, self._pf_cons[jp % 2])
                pf = st[f"pf{jp % 2}"]
                ae.inc(nc.scalar.activation(pf[:, :], psS[:, :], AF.Exp,
                                            bias=self.g_Bneg[:, h:h + 1], scale=1.0))
                exp_done.append(ae.n)
                srcs = []
                nc.vector.wait_ge(ae.h, exp_done[jp])
                if jp >= 2:
                    nc.vector.wait_ge(pe.h, cons_done[jp - 2])
                for half in range(2):
                    kt = jp * 2 + half
                    src_p = pf[:, half * 512:(half + 1) * 512]
                    if kt >= band0:
                        pbm = st[f"pbm{(jp % 2) * 2 + half}"]
                        ve.inc(nc.vector.tensor_tensor(
                            pbm[:, :], src_p,
                            st["masksl"][:, (kt - band0) * 512:(kt - band0 + 1) * 512],
                            ALU.mult))
                        src_p = pbm[:, :]
                    srcs.append(src_p)
                self.w(nc.vector, "ve")
                ds = st[f"dsum{jp % 2}"]
                ve.inc(nc.vector.tensor_tensor(ds[:, :], srcs[0], srcs[1], ALU.add))
                self._pf_cons[jp % 2] = ve.n
                presum_done[jp] = ve.n
                pend = (jp, srcs, ds, koff, vh)
            emit_cons(*pend)
            self.w(nc.vector, "pe")
            ve.inc(nc.vector.reciprocal(st["drow"][0:1, :], st["ps_D"][:, :]))
            self.w(nc.tensor, "ve")
            pe.inc(nc.tensor.matmul(st["ps_g"][:, 0:512], self.g_ones_row_f[0:1, :],
                                    st["drow"][0:1, :], start=True, stop=True))
            self.w(nc.vector, "pe")
            ve.inc(nc.vector.tensor_copy(st["rbsb"][:, :], st["ps_g"][:, 0:512]))
            self.w(nc.vector, "ve")
            cn = st[f"ctxn{h % 2}"]
            cf = self.sem(f"cf{h % 2}")
            nc.vector.wait_ge(cf.h, self._ctx_flushed[h % 2])
            ve.inc(nc.vector.tensor_tensor(cn[:, :], st["ps_ctx"][:, :], st["rbsb"][:, :],
                                           ALU.mult))
            self.w(nc.sync, "ve")
            self._ctx_flushed[h % 2] = cf.inc(nc.sync.dma_start(
                out=self.s_ctx[h * 128:(h + 1) * 128, g * 512:(g + 1) * 512],
                in_=cn[:, :]), 16)

    def _p3_group(self, g, st):
        nc = self.nc
        pe = self.sem("pe"); ve = self.sem("ve"); ae = self.sem("ae")
        sl = self.sem("sl"); xl = self.sem("xl"); cl = self.sem("cl")
        ao = st["ao"]; rowf = st["rowf"]
        for s in range(4):
            tok0 = g * 512 + s * 128
            views = [st["ps_S0"][:, 0:512], st["ps_S0"][:, 512:1024],
                     st["ps_S1"][:, 0:512], st["ps_S1"][:, 512:1024]]
            self.w(nc.tensor, "ae", "ve")
            pv = {}
            v_w = {}
            v_c = {}
            base_pe = pe.n

            def load_wob(t):
                ci_, eg_ = t // 4, t % 4
                wbs = self.sem(f"wb{t % 4}")
                nc.sync.wait_ge(pe.h, pv.get(t - 4, self._p3_prev_last))
                v_w[t] = (wbs, wbs.inc(nc.sync.dma_start(
                    out=st[f"wob{t % 4}"][:, :],
                    in_=self.i_wout[ci_ * 128:(ci_ + 1) * 128,
                                    eg_ * 512:(eg_ + 1) * 512]), 16))

            def load_cb(ci_):
                cbs = self.sem(f"cb{ci_ % 2}")
                nc.sync.wait_ge(pe.h, pv.get((ci_ - 2) * 4 + 3, self._p3_prev_last))
                v_c[ci_] = (cbs, cbs.inc(nc.sync.dma_start(
                    out=st[f"ctxb{ci_ % 2}"][:, :],
                    in_=self.s_ctx[ci_ * 128:(ci_ + 1) * 128, tok0:tok0 + 128]), 16))

            load_cb(0); load_cb(1)
            for t0 in range(3):
                load_wob(t0)
            for t in range(64):
                ci, eg = t // 4, t % 4
                if t + 3 < 64:
                    load_wob(t + 3)
                sw, vw = v_w[t]
                nc.tensor.wait_ge(sw.h, vw)
                sc, vc = v_c[ci]
                nc.tensor.wait_ge(sc.h, vc)
                pe.inc(nc.tensor.matmul(views[eg], st[f"ctxb{ci % 2}"][:, :],
                                        st[f"wob{t % 4}"][:, :],
                                        start=(ci == 0), stop=(ci == NCD - 1)))
                pv[t] = pe.n
                if eg == 3 and ci + 2 < NCD:
                    load_cb(ci + 2)
            self._p3_prev_last = pe.n
            self.w(nc.vector, "pe")
            for eg in range(4):
                ve.inc(nc.vector.tensor_copy(ao[:, eg * 512:(eg + 1) * 512], views[eg]))
            self.w(nc.vector, "ve")
            ve.inc(nc.vector.reduce_sum(self.col(16), ao[:, :], axis=AXX))
            self.w(nc.scalar, "ve")
            ae.inc(nc.scalar.activation(self.g_sqdump[:, :], ao[:, :], AF.Square,
                                        accum_out=self.col(17)))
            self._ln_cols(16, 17, 18, 19)
            self.w(nc.vector, "ae")
            ve.inc(nc.vector.tensor_scalar(rowf[:, :], ao[:, :], self.col(18), self.col(19),
                                           ALU.subtract, ALU.mult))
            v_ln2 = ve.n
            nc.sync.wait_ge(ve.h, v_ln2)
            v_x = xl.inc(nc.sync.dma_start(out=ao[:, :],
                                           in_=self.i_xq[tok0:tok0 + 128, :]), 16)
            nc.vector.wait_ge(xl.h, v_x)
            self.w(nc.vector, "ve")
            ve.inc(nc.vector.tensor_tensor(ao[:, :], rowf[:, :], ao[:, :], ALU.add))
            self.w(nc.sync, "ve")
            sl.inc(nc.sync.dma_start(out=self.s_x2[tok0:tok0 + 128, :], in_=ao[:, :]), 16)
            self.w(nc.vector, "ve")
            ve.inc(nc.vector.reduce_sum(self.col(20), ao[:, :], axis=AXX))
            self.w(nc.scalar, "ve")
            ae.inc(nc.scalar.activation(self.g_sqdump[:, :], ao[:, :], AF.Square,
                                        accum_out=self.col(21)))
            self._ln_cols(20, 21, 22, 23)
            self.w(nc.vector, "ae")
            ve.inc(nc.vector.tensor_scalar(rowf[:, :], ao[:, :], self.col(22), self.col(23),
                                           ALU.subtract, ALU.mult))
            self.w(nc.vector, "ve")
            ve.inc(nc.vector.tensor_copy(st["xhitok"][:, :], rowf[:, :]))
            self.w(nc.vector, "ve")
            ve.inc(nc.vector.tensor_tensor(st["xlotok"][:, :], rowf[:, :],
                                           st["xhitok"][:, :], ALU.subtract))
            v_hl = ve.n
            nc.tensor.wait_ge(ve.h, v_hl)
            for (tok, dstT) in [(st["xhitok"], st["xhiT"]), (st["xlotok"], st["xloT"])]:
                for c4 in range(4):
                    if not (tok is st["xhitok"] and c4 == 0):
                        nc.tensor.wait_ge(ve.h, ve.n)
                    for j in range(4):
                        ci = c4 * 4 + j
                        mm = nc.tensor.transpose(st["ps_tb"][:, j * 128:(j + 1) * 128],
                                                 tok[:, ci * 128:(ci + 1) * 128],
                                                 self.g_eye_bf[:, :])
                    pe.inc(mm)
                    self.w(nc.vector, "pe")
                    ve.inc(nc.vector.tensor_copy(dstT[:, c4 * 512:(c4 + 1) * 512],
                                                 st["ps_tb"][:, 0:512]))
            self.w(nc.sync, "ve")
            sl.inc(nc.sync.dma_start(
                out=self.s_xb2.rearrange("(c p) t -> p c t", p=128)[:, :, tok0:tok0 + 128],
                in_=st["xhiT"].ap().rearrange("p (c t) -> p c t", c=NCD)), 16)
            self.w(nc.tensor, "ve")
            first = True
            for (xT, gt) in [(st["xhiT"], self.g_ghi), (st["xhiT"], self.g_glo),
                             (st["xloT"], self.g_ghi)]:
                last_op = (xT is st["xloT"])
                for ci in range(NCD):
                    mm = nc.tensor.matmul(
                        st["ps_g"][:, 0:8],
                        xT[:, ci * 128:(ci + 1) * 128],
                        gt[:, ci * 8:(ci + 1) * 8],
                        start=first, stop=(last_op and ci == NCD - 1))
                    first = False
            pe.inc(mm)
            self._p3_gate(g * 4 + s, st["ps_g"])

    def _ln_cols(self, c_sum, c_ss, c_mean, c_rstd):
        nc = self.nc
        ve = self.sem("ve"); ae = self.sem("ae")
        self.w(nc.vector, "ae")
        ve.inc(nc.vector.tensor_scalar(self.col(c_mean), self.col(c_sum), INV_D, None, ALU.mult))
        ve.inc(nc.vector.tensor_scalar(self.col(24), self.col(c_ss), INV_D, None, ALU.mult))
        self.w(nc.vector, "ve")
        ve.inc(nc.vector.tensor_tensor(self.col(25), self.col(c_mean), self.col(c_mean), ALU.mult))
        self.w(nc.vector, "ve")
        ve.inc(nc.vector.tensor_tensor(self.col(26), self.col(24), self.col(25), ALU.subtract))
        self.w(nc.scalar, "ve")
        ae.inc(nc.scalar.activation(self.col(27), self.col(26), AF.Ln,
                                    bias=self.g_pool[:, 0:1], scale=1.0))
        self.w(nc.scalar, "ae")
        ae.inc(nc.scalar.activation(self.col(c_rstd), self.col(27), AF.Exp, bias=0.0, scale=-0.5))

    def _p3_gate(self, qsub, ps_g):
        nc = self.nc
        pe = self.sem("pe"); ve = self.sem("ve"); ae = self.sem("ae")
        gl, ge, pr, b8, s1 = self.g_gl, self.g_ge, self.g_pr, self.g_b8, self.g_s1
        self.w(nc.vector, "pe")
        ve.inc(nc.vector.tensor_copy(gl[:, :], ps_g[:, 0:8]))
        self.w(nc.vector, "ve")
        ve.inc(nc.vector.reduce_max(self.col(28), gl[:, :], axis=AXX))
        self.w(nc.vector, "ve")
        ve.inc(nc.vector.tensor_scalar(self.col(29), self.col(28), -1.0, None, ALU.mult))
        self.w(nc.scalar, "ve")
        ae.inc(nc.scalar.activation(ge[:, :], gl[:, :], AF.Exp, bias=self.col(29), scale=1.0,
                                    accum_out=self.col(30)))
        self.w(nc.vector, "ae")
        ve.inc(nc.vector.reciprocal(self.col(31), self.col(30)))
        self.w(nc.vector, "ve")
        ve.inc(nc.vector.tensor_scalar(pr[:, :], ge[:, :], self.col(31), None, ALU.mult))
        self.w(nc.vector, "ve")
        ve.inc(nc.vector.tensor_tensor(b8[:, :], pr[:, :], self.g_ebb[:, :], ALU.add))
        self.w(nc.vector, "ve")
        ve.inc(nc.vector.reduce_max(self.col(28), b8[:, :], axis=AXX))
        self.w(nc.vector, "ve")
        ve.inc(nc.vector.tensor_scalar(s1[:, :], b8[:, :], self.col(28), None, ALU.is_ge))
        self.w(nc.vector, "ve")
        ve.inc(nc.vector.scalar_tensor_tensor(ge[:, :], s1[:, :], -1e30, b8[:, :],
                                              ALU.mult, ALU.add))
        self.w(nc.vector, "ve")
        ve.inc(nc.vector.reduce_max(self.col(29), ge[:, :], axis=AXX))
        self.w(nc.vector, "ve")
        ve.inc(nc.vector.tensor_scalar(s1[:, :], b8[:, :], self.col(29), None, ALU.is_ge))
        self.w(nc.vector, "ve")
        ve.inc(nc.vector.tensor_tensor(self.g_wt[:, qsub * 8:(qsub + 1) * 8],
                                       pr[:, :], s1[:, :], ALU.mult))
        self.w(nc.tensor, "ve")
        psw = ps_g
        pe.inc(nc.tensor.transpose(psw[0:8, 0:128],
                                   self.g_wt[:, qsub * 8:(qsub + 1) * 8],
                                   self.g_eye_f[:, :]))
        self.w(nc.vector, "pe")
        ve.inc(nc.vector.tensor_copy(self.g_wtT8[:, qsub * 128:(qsub + 1) * 128],
                                     psw[0:8, 0:128]))

    def p4p5(self):
        nc = self.nc
        with ExitStack() as ctx:
            E = ctx.enter_context
            acc = E(nc.sbuf_tensor("acc", [128, 4 * D], F32))
            w2b = E(nc.sbuf_tensor("w2b", [128, NHC * 512], BF16))
            w2c = E(nc.sbuf_tensor("w2c", [128, NHC * 512], BF16))
            wsc = E(nc.sbuf_tensor("wslab_c", [128, NCD * 512], BF16))
            wsd = E(nc.sbuf_tensor("wslab_d", [128, NCD * 512], BF16))
            hsl = E(nc.sbuf_tensor("hsl", [128, NHC * 512], BF16))
            xeb = E(nc.sbuf_tensor("xeb", [128, NCD * 512], BF16))
            xbfb = E(nc.sbuf_tensor("xbfb", [128, NCD * 512], BF16))
            silub = E(nc.sbuf_tensor("silub", [128, 512], BF16))
            wbsb = E(nc.sbuf_tensor("wbsb", [128, 512], BF16))
            xrow = E(nc.sbuf_tensor("p5x", [128, D], F32))
            rowf = E(nc.sbuf_tensor("p5f", [128, D], F32))
            ps_A = E(nc.psum_tensor("ps_A", [128, 512], F32))
            ps_B = E(nc.psum_tensor("ps_B", [128, 512], F32))
            ps_A1 = E(nc.psum_tensor("ps_A1", [128, 512], F32))
            ps_B1 = E(nc.psum_tensor("ps_B1", [128, 512], F32))
            ps_C = E(nc.psum_tensor("ps_C", [128, 512], F32))
            ps_D2 = E(nc.psum_tensor("ps_D2", [128, 512], F32))
            ps_E = E(nc.psum_tensor("ps_E", [128, 512], F32))
            sl = self.sem("sl")
            for th in range(2):
                for tg in range(2):
                    tok0 = th * 1024 + tg * 512
                    self.w(nc.sync, "pe", "ve")
                    v_xb = sl.inc(nc.sync.dma_start(
                        out=xbfb.ap().rearrange("p (c t) -> p c t", c=NCD),
                        in_=self.s_xb2.rearrange("(c p) t -> p c t", p=128)
                            [:, :, tok0:tok0 + 512]), 16)
                    for ei, e in enumerate([8] + list(range(8))):
                        self._p4_expert(th, tg, e, ei == 0, v_xb, acc, (w2b, w2c), hsl,
                                        xeb, xbfb, silub, wbsb,
                                        (self.g_wslab_a, self.g_wslab_b, wsc, wsd),
                                        (ps_A, ps_A1), (ps_B, ps_B1), ps_C, ps_D2, ps_E)
                    self._p5_quarter(th, tg, acc, xrow, rowf)
        so = self.sem("so")
        nc.sync.wait_ge(so.h, so.n)

    def _p4_expert(self, th, tg, e, first, v_xb, acc, w2p, hsl, xeb, xbfb, silub, wbsb,
                   wsl, ps_As, ps_Bs, ps_C, ps_D2, ps_E):
        nc = self.nc
        pe = self.sem("pe"); ve = self.sem("ve"); ae = self.sem("ae")
        sl = self.sem("sl")
        tok0 = th * 1024 + tg * 512
        shared = (e == 8)
        if not hasattr(self, "_p4_w13"):
            self._p4_w13 = {"tick": 0, "used": [0, 0], "q": []}
            self._p4_w2 = {"tick": 0, "used": [0, 0], "q": []}
            self._p4_hb = 0
            self._p4_ab_used = [0, 0]
            self._p4_silu_used = 0
        wsz = [512, 512, 512, 256]

        def spec13(ee, ws):
            base = ws * 512
            ln_ = wsz[ws]
            if ee == 8:
                return (self.i_sw1, self.i_sw3, None, base, ln_)
            return (self.i_w1, self.i_w3, ee, base, ln_)

        def load_w13(ee, ws):
            t = self._p4_w13
            b = t["tick"] % 2
            t["tick"] += 1
            wlx = self.sem(f"wA{b}")
            nc.gpsimd.wait_ge(pe.h, t["used"][b])
            a1, a3, ee_, base, ln_ = spec13(ee, ws)
            for ci in range(NCD):
                if ee_ is None:
                    s1ap = a1[ci * 128:(ci + 1) * 128, base:base + ln_]
                    s3ap = a3[ci * 128:(ci + 1) * 128, base:base + ln_]
                else:
                    s1ap = a1[ee_, ci * 128:(ci + 1) * 128, base:base + ln_]
                    s3ap = a3[ee_, ci * 128:(ci + 1) * 128, base:base + ln_]
                wlx.inc(nc.gpsimd.dma_start(out=wsl[2 * b][:, ci * 512:ci * 512 + ln_],
                                            in_=s1ap), 16)
                wlx.inc(nc.gpsimd.dma_start(out=wsl[2 * b + 1][:, ci * 512:ci * 512 + ln_],
                                            in_=s3ap), 16)
            t["q"].append((b, wlx, wlx.n))

        def load_w2(ee, dg):
            t = self._p4_w2
            b = t["tick"] % 2
            t["tick"] += 1
            wlx = self.sem(f"wB{b}")
            nc.gpsimd.wait_ge(pe.h, t["used"][b])
            for hc in range(NHC):
                if ee == 8:
                    w2ap = self.i_sw2[hc * 128:(hc + 1) * 128, dg * 512:(dg + 1) * 512]
                else:
                    w2ap = self.i_w2[ee, hc * 128:(hc + 1) * 128, dg * 512:(dg + 1) * 512]
                wlx.inc(nc.gpsimd.dma_start(out=w2p[b][:, hc * 512:(hc + 1) * 512],
                                            in_=w2ap), 16)
            t["q"].append((b, wlx, wlx.n))

        if first and tg == 0 and th == 0:
            load_w13(8, 0)
        if not shared:
            self.w(nc.tensor, "ve")
            pe.inc(nc.tensor.matmul(ps_E[:, :], self.g_sel8[:, e * 128:(e + 1) * 128],
                                    self.g_wtT8[:, tok0:tok0 + 512], start=True, stop=True))
            self.w(nc.vector, "pe")
            ve.inc(nc.vector.tensor_copy(wbsb[:, :], ps_E[:, :]))
            nc.vector.wait_ge(sl.h, v_xb)
            self.w(nc.vector, "ve")
            for ci in range(NCD):
                ve.inc(nc.vector.tensor_tensor(xeb[:, ci * 512:(ci + 1) * 512],
                                               xbfb[:, ci * 512:(ci + 1) * 512],
                                               wbsb[:, :], ALU.mult))
            xsrc = xeb
            v_xe = ve.n
        else:
            xsrc = xbfb
            v_xe = None
        for ws in range(4):
            wlen = wsz[ws]
            if ws + 1 < 4:
                load_w13(e, ws + 1)
            else:
                load_w2(e, 0)
            myb, dsem, dval = self._p4_w13["q"].pop(0)
            nc.tensor.wait_ge(dsem.h, dval)
            if v_xe is not None:
                nc.tensor.wait_ge(ve.h, v_xe)
            else:
                nc.tensor.wait_ge(sl.h, v_xb)
            for hcw in range(wlen // 128):
                hc = ws * 4 + hcw
                hb = self._p4_hb % 2
                self._p4_hb += 1
                pa = ps_As[hb]; pb = ps_Bs[hb]
                nc.tensor.wait_ge(ve.h, self._p4_ab_used[hb])
                for ci in range(NCD):
                    nc.tensor.matmul(pa[:, :],
                                     wsl[2 * myb][:, ci * 512 + hcw * 128: ci * 512 + (hcw + 1) * 128],
                                     xsrc[:, ci * 512:(ci + 1) * 512],
                                     start=(ci == 0), stop=(ci == NCD - 1))
                for ci in range(NCD):
                    mm = nc.tensor.matmul(pb[:, :],
                                          wsl[2 * myb + 1][:, ci * 512 + hcw * 128: ci * 512 + (hcw + 1) * 128],
                                          xsrc[:, ci * 512:(ci + 1) * 512],
                                          start=(ci == 0), stop=(ci == NCD - 1))
                pe.inc(mm)
                v_ab = pe.n
                nc.scalar.wait_ge(pe.h, v_ab)
                nc.scalar.wait_ge(ve.h, self._p4_silu_used)
                ae.inc(nc.scalar.activation(silub[:, :], pa[:, :], AF.Silu))
                self.w(nc.vector, "ae")
                ve.inc(nc.vector.tensor_tensor(hsl[:, hc * 512:(hc + 1) * 512],
                                               silub[:, :], pb[:, :], ALU.mult))
                self._p4_ab_used[hb] = ve.n
                self._p4_silu_used = ve.n
            self._p4_w13["used"][myb] = pe.n
        for dg in range(4):
            nxt = [8] + list(range(8))
            ei_cur = nxt.index(e)
            if dg + 1 < 4:
                load_w2(e, dg + 1)
            else:
                if ei_cur + 1 < 9:
                    load_w13(nxt[ei_cur + 1], 0)
                elif not (th == 1 and tg == 1):
                    load_w13(8, 0)
            myb, dsem, dval = self._p4_w2["q"].pop(0)
            nc.tensor.wait_ge(dsem.h, dval)
            for tsub in range(4):
                psw = (ps_C, ps_D2)[tsub % 2]
                self.w(nc.tensor, "ve")
                for hc in range(NHC):
                    mm = nc.tensor.matmul(
                        psw[:, :],
                        hsl[:, hc * 512 + tsub * 128: hc * 512 + (tsub + 1) * 128],
                        w2p[myb][:, hc * 512:(hc + 1) * 512],
                        start=(hc == 0), stop=(hc == NHC - 1))
                pe.inc(mm)
                self.w(nc.vector, "pe", "ae")
                a_sl = acc[:, tsub * D + dg * 512: tsub * D + (dg + 1) * 512]
                if first:
                    self.w(nc.vector, "so")
                    ve.inc(nc.vector.tensor_copy(a_sl, psw[:, :]))
                else:
                    ve.inc(nc.vector.tensor_tensor(a_sl, psw[:, :], a_sl, ALU.add))
            self._p4_w2["used"][myb] = pe.n

    def _p5_quarter(self, th, tg, acc, xrow, rowf):
        nc = self.nc
        ve = self.sem("ve"); ae = self.sem("ae"); xl = self.sem("xl"); so = self.sem("so")
        for tsub in range(4):
            tok0 = th * 1024 + tg * 512 + tsub * 128
            a_sl = acc[:, tsub * D:(tsub + 1) * D]
            self.w(nc.sync, "ve")
            v_x = xl.inc(nc.sync.dma_start(out=xrow[:, :],
                                           in_=self.s_x2[tok0:tok0 + 128, :]), 16)
            self.w(nc.vector, "ve")
            ve.inc(nc.vector.reduce_sum(self.col(16), a_sl, axis=AXX))
            self.w(nc.scalar, "ve")
            ae.inc(nc.scalar.activation(self.g_sqdump[:, :], a_sl, AF.Square,
                                        accum_out=self.col(17)))
            self._ln_cols(16, 17, 18, 19)
            self.w(nc.vector, "ae")
            ve.inc(nc.vector.tensor_scalar(rowf[:, :], a_sl, self.col(18), self.col(19),
                                           ALU.subtract, ALU.mult))
            nc.vector.wait_ge(xl.h, v_x)
            self.w(nc.vector, "ve", "so")
            ve.inc(nc.vector.tensor_tensor(a_sl, rowf[:, :], xrow[:, :], ALU.add))
            self.w(nc.sync, "ve")
            so.inc(nc.sync.dma_start(out=self.o_out[tok0:tok0 + 128, :],
                                     in_=a_sl), 16)

    def build(self):
        self.declare()
        self.alloc_globals()
        with self.nc.named_scope("prologue"):
            self.prologue()
        with self.nc.named_scope("p1"):
            self.p1()
        with self.nc.named_scope("p2p3"):
            self.p2p3()
        with self.nc.named_scope("p4p5"):
            self.p4p5()
        return self.nc


def make_masks(pos):
    t = np.arange(128)[:, None]
    o = np.arange(128)[None, :]
    m = np.zeros((32, 128, 512), np.float32)
    for g in range(4):
        for j in range(8):
            kt = 8 * g + j
            for s in range(4):
                p = pos[4 * g + s]
                m[g * 8 + j][:, s * 128:(s + 1) * 128] = (128 * kt + t <= 128 * p + o)
    return m.astype(BF)


def kernel(L0_attn_x, L0_attn_Wqkv, L0_attn_q_ln_scale, L0_attn_k_ln_scale,
           L0_attn_out_W, L0_ffn_gate_w, L0_ffn_w1, L0_ffn_w2, L0_ffn_w3,
           L0_ffn_sw1, L0_ffn_sw2, L0_ffn_sw3, L0_ffn_expert_bias):
    x = np.ascontiguousarray(np.asarray(L0_attn_x, dtype=np.float32))
    B, S, _ = x.shape
    b = Builder()
    nc = b.build()
    shared = {
        "wqkv": np.ascontiguousarray(np.asarray(L0_attn_Wqkv, np.float32)),
        "qln": np.ascontiguousarray(np.asarray(L0_attn_q_ln_scale, np.float32)),
        "kln": np.ascontiguousarray(np.asarray(L0_attn_k_ln_scale, np.float32)),
        "wout": np.ascontiguousarray(np.asarray(L0_attn_out_W, np.float32)),
        "gw": np.ascontiguousarray(np.asarray(L0_ffn_gate_w, np.float32)),
        "eb": np.ascontiguousarray(np.asarray(L0_ffn_expert_bias, np.float32)),
        "w1": np.ascontiguousarray(np.asarray(L0_ffn_w1, np.float32)),
        "w2": np.ascontiguousarray(np.asarray(L0_ffn_w2, np.float32)),
        "w3": np.ascontiguousarray(np.asarray(L0_ffn_w3, np.float32)),
        "sw1": np.ascontiguousarray(np.asarray(L0_ffn_sw1, np.float32)),
        "sw2": np.ascontiguousarray(np.asarray(L0_ffn_sw2, np.float32)),
        "sw3": np.ascontiguousarray(np.asarray(L0_ffn_sw3, np.float32)),
    }
    in_maps = []
    for c in range(8):
        bi = c // 2
        pos = positions_for_core(c)
        xb = x[bi]
        xq = np.ascontiguousarray(xb.reshape(32, 128, D)[pos].reshape(TOWN, D))
        m = {"xq": xq, "xk": np.ascontiguousarray(xb), "masks": make_masks(pos)}
        m.update(shared)
        in_maps.append(m)
    res = run_bass_kernel_spmd(nc, in_maps, core_ids=list(range(8)))
    out = np.empty((B, S, D), np.float32)
    for c in range(8):
        bi = c // 2
        pos = positions_for_core(c)
        o = res.results[c]["out"].reshape(16, 128, D)
        for j, p in enumerate(pos):
            out[bi, p * 128:(p + 1) * 128, :] = o[j]
    return out


# version 13
# speedup vs baseline: 1.0312x; 1.0312x over previous
import numpy as np
import ml_dtypes
from contextlib import ExitStack
import concourse.bass as bass
import concourse.mybir as mybir
from concourse.bass_utils import run_bass_kernel_spmd

F32 = mybir.dt.float32
BF16 = mybir.dt.bfloat16
F16 = mybir.dt.float16
AF = mybir.ActivationFunctionType
ALU = mybir.AluOpType
AXX = mybir.AxisListType.X
BF = ml_dtypes.bfloat16

D = 2048
NCD = 16
HID = 1792
NHC = 14
NH = 16
SEQ = 4096
TOWN = 2048
NKT = 32
QG_NKT = [8, 16, 24, 32]
RS2 = float(1.0 / np.sqrt(128.0))
EPS_LN = 1e-5
EPS_QK = 1e-6
INV_D = 1.0 / 2048.0
TP = 512
NT = TP // 128


def positions_for_core(c):
    if c % 2 == 0:
        return [i for i in range(32) if i % 4 in (0, 3)]
    return [i for i in range(32) if i % 4 in (1, 2)]


class Sem:
    def __init__(self, nc, name):
        self.h = nc.alloc_semaphore(name=name)
        self.n = 0

    def inc(self, inst, k=1):
        inst.then_inc(self.h, k)
        self.n += k
        return self.n


class Builder:
    def __init__(self):
        self.nc = bass.Bass("TRN2", target_bir_lowering=False)
        self.sems = {}

    def sem(self, name):
        if name not in self.sems:
            self.sems[name] = Sem(self.nc, name)
        return self.sems[name]

    def w(self, eng, *names):
        for n in names:
            s = self.sem(n)
            if s.n > 0:
                eng.wait_ge(s.h, s.n)

    def barrier(self):
        nc = self.nc
        b = self.sem("bar")
        engs = [nc.tensor, nc.vector, nc.scalar, nc.sync, nc.gpsimd]
        for nm, s in list(self.sems.items()):
            if nm != "bar" and s.n > 0:
                nc.sync.wait_ge(s.h, s.n)
        for e in engs:
            e.drain()
            e.sem_inc(b.h, 1)
        b.n += 5
        for e in engs:
            e.wait_ge(b.h, b.n)

    def declare(self):
        nc = self.nc
        ap = lambda t: t.ap()
        self.i_xq = ap(nc.dram_tensor("xq", [TOWN, D], F32, kind="ExternalInput"))
        self.i_xk = ap(nc.dram_tensor("xk", [SEQ, D], F32, kind="ExternalInput"))
        self.i_masks = ap(nc.dram_tensor("masks", [32, 128, 512], BF16, kind="ExternalInput"))
        self.i_wqkv = ap(nc.dram_tensor("wqkv", [D, 3 * D], F32, kind="ExternalInput"))
        self.i_qln = ap(nc.dram_tensor("qln", [D], F32, kind="ExternalInput"))
        self.i_kln = ap(nc.dram_tensor("kln", [D], F32, kind="ExternalInput"))
        self.i_wout = ap(nc.dram_tensor("wout", [D, D], F32, kind="ExternalInput"))
        self.i_gw = ap(nc.dram_tensor("gw", [D, 8], F32, kind="ExternalInput"))
        self.i_eb = ap(nc.dram_tensor("eb", [8], F32, kind="ExternalInput"))
        self.i_w1 = ap(nc.dram_tensor("w1", [8, D, HID], F32, kind="ExternalInput"))
        self.i_w2 = ap(nc.dram_tensor("w2", [8, HID, D], F32, kind="ExternalInput"))
        self.i_w3 = ap(nc.dram_tensor("w3", [8, D, HID], F32, kind="ExternalInput"))
        self.i_sw1 = ap(nc.dram_tensor("sw1", [D, HID], F32, kind="ExternalInput"))
        self.i_sw2 = ap(nc.dram_tensor("sw2", [HID, D], F32, kind="ExternalInput"))
        self.i_sw3 = ap(nc.dram_tensor("sw3", [D, HID], F32, kind="ExternalInput"))
        self.o_out = ap(nc.dram_tensor("out", [TOWN, D], F32, kind="ExternalOutput"))
        self.s_k = ap(nc.dram_tensor("kscr", [D, SEQ], BF16, kind="Internal"))
        self.s_kl = ap(nc.dram_tensor("kscrl", [D, SEQ], BF16, kind="Internal"))
        self.s_v = ap(nc.dram_tensor("vscr", [SEQ, D], F32, kind="Internal"))
        self.s_q = ap(nc.dram_tensor("qscr", [D, TOWN], BF16, kind="Internal"))
        self.s_ql = ap(nc.dram_tensor("qscrl", [D, TOWN], BF16, kind="Internal"))
        self.s_ctx = ap(nc.dram_tensor("ctxscr", [D, TOWN], F32, kind="Internal"))
        self.s_x2 = ap(nc.dram_tensor("x2scr", [TOWN, D], F32, kind="Internal"))
        self.s_xb2 = ap(nc.dram_tensor("xb2scr", [D, TOWN], BF16, kind="Internal"))

        self.c_eye_bf = ap(nc.inline_tensor(np.eye(128, dtype=BF), name="eyebf"))
        self.c_eye_f = ap(nc.inline_tensor(np.eye(128, dtype=np.float32), name="eyef"))
        self.c_ones_row_f = ap(nc.inline_tensor(np.ones((1, 128), np.float32), name="onesrowf"))
        self.c_ones_col_f = ap(nc.inline_tensor(np.ones((128, 1), np.float32), name="onescolf"))
        self.c_ones_col_h = ap(nc.inline_tensor(np.ones((128, 1), np.float16), name="onescolh"))
        eh = np.zeros((128, 256), np.float32)
        for kc in range(16):
            eh[:, 16 * kc + kc] = 1.0
        self.c_ehead = ap(nc.inline_tensor(eh.astype(np.float16), name="ehead"))
        s8 = np.zeros((8, 8 * 128), np.float32)
        for e in range(8):
            s8[e, e * 128:(e + 1) * 128] = 1.0
        self.c_sel8 = ap(nc.inline_tensor(s8.astype(BF), name="sel8"))
        cp = np.zeros((128, 4), np.float32)
        cp[:, 0] = EPS_LN; cp[:, 1] = EPS_QK; cp[:, 2] = 0.0; cp[:, 3] = 1.0
        self.c_pool = ap(nc.inline_tensor(cp, name="cpool"))

    def alloc_globals(self):
        nc = self.nc
        A = nc.alloc_sbuf_tensor
        self.g_wslab_a = A("wslab_a", [128, NCD * 512], BF16)
        self.g_wslab_b = A("wslab_b", [128, NCD * 512], BF16)
        self.g_sqdump = A("sqdump", [128, D], BF16)
        self.g_eye_bf = A("eye_bf", [128, 128], BF16)
        self.g_eye_f = A("eye_f", [128, 128], F32)
        self.g_ones_row_f = A("ones_row_f", [1, 128], F32)
        self.g_ones_col_f = A("ones_col_f", [128, 1], F32)
        self.g_ones_col_h = A("ones_col_h", [128, 1], F16)
        self.g_ehead = A("ehead_s", [128, 256], F16)
        self.g_sel8 = A("sel8_s", [8, 1024], BF16)
        self.g_pool = A("pool_s", [128, 4], F32)
        self.g_gq = A("gq_s", [128, 16], F32)
        self.g_gk = A("gk_s", [128, 16], F32)
        self.g_Bneg = A("Bneg", [128, 16], F32)
        self.g_wt = A("wtslab", [128, 16 * 8], F32)
        self.g_wtT8 = A("wtT8", [8, TOWN], BF16)
        self.g_ebrow = A("ebrow", [1, 8], F32)
        self.g_ebb = A("ebb", [128, 8], F32)
        self.g_ghi = A("ghi", [128, 16 * 8], BF16)
        self.g_glo = A("glo", [128, 16 * 8], BF16)
        self.g_cols = A("cols", [128, 32], F32)
        self.g_km2 = A("kmax2", [16, 2], F32)
        self.g_bh = A("bh16", [16, 1], F32)
        self.g_bhrow = A("bhrow", [1, 16], F32)
        self.g_gl = A("glg", [128, 8], F32)
        self.g_ge = A("geg", [128, 8], F32)
        self.g_pr = A("prg", [128, 8], F32)
        self.g_b8 = A("b8g", [128, 8], F32)
        self.g_s1 = A("s1g", [128, 8], F32)
        nc.const_aps.register(F32, 0.0, self.g_pool[:, 2:3])
        nc.const_aps.register(F32, 1.0, self.g_pool[:, 3:4])

    def col(self, i):
        return self.g_cols[:, i:i + 1]

    def prologue(self):
        nc = self.nc
        sl = self.sem("sl"); pe = self.sem("pe"); ve = self.sem("ve")
        ctx = ExitStack()
        gw16 = ctx.enter_context(nc.sbuf_tensor("gw16", [16, 1024], F32))
        gwhi16 = ctx.enter_context(nc.sbuf_tensor("gwhi16", [16, 1024], BF16))
        gwlo16 = ctx.enter_context(nc.sbuf_tensor("gwlo16", [16, 1024], BF16))
        g16t = ctx.enter_context(nc.sbuf_tensor("g16t", [16, 128], BF16))
        g16q = ctx.enter_context(nc.sbuf_tensor("g16q", [16, 128], F32))
        g16k = ctx.enter_context(nc.sbuf_tensor("g16k", [16, 128], F32))
        for dst, src in [
            (self.g_eye_bf, self.c_eye_bf), (self.g_eye_f, self.c_eye_f),
            (self.g_ones_row_f, self.c_ones_row_f), (self.g_ones_col_f, self.c_ones_col_f),
            (self.g_ones_col_h, self.c_ones_col_h),
            (self.g_ehead, self.c_ehead), (self.g_sel8, self.c_sel8),
            (self.g_pool, self.c_pool),
        ]:
            sl.inc(nc.sync.dma_start(out=dst[:, :], in_=src), 16)
        sl.inc(nc.sync.dma_start(out=g16q[:, :],
                                 in_=self.i_qln.rearrange("(c p) -> c p", p=128)), 16)
        sl.inc(nc.sync.dma_start(out=g16k[:, :],
                                 in_=self.i_kln.rearrange("(c p) -> c p", p=128)), 16)
        sl.inc(nc.sync.dma_start(out=gw16[:, :],
                                 in_=self.i_gw.rearrange("(c p) e -> c (p e)", p=128)), 16)
        sl.inc(nc.sync.dma_start(out=self.g_ebrow[:, :], in_=self.i_eb[None, :]), 16)
        with nc.psum_tensor("ps_pro", [128, 128], F32) as ps:
            self.w(nc.tensor, "sl")
            pe.inc(nc.tensor.transpose(ps[:, 0:16], g16q[:, :], self.g_eye_f[0:16, 0:16]))
            self.w(nc.vector, "pe")
            ve.inc(nc.vector.tensor_copy(self.g_gq[:, :], ps[:, 0:16]))
            self.w(nc.tensor, "ve")
            pe.inc(nc.tensor.transpose(ps[:, 0:16], g16k[:, :], self.g_eye_f[0:16, 0:16]))
            self.w(nc.vector, "pe")
            ve.inc(nc.vector.tensor_copy(self.g_gk[:, :], ps[:, 0:16]))
            self.w(nc.tensor, "ve")
            pe.inc(nc.tensor.matmul(ps[:, 0:8], self.g_ones_row_f[0:1, :],
                                    self.g_ebrow[0:1, :], start=True, stop=True))
            self.w(nc.vector, "pe")
            ve.inc(nc.vector.tensor_copy(self.g_ebb[:, :], ps[:, 0:8]))
        ve.inc(nc.vector.tensor_copy(gwhi16[:, :], gw16[:, :]))
        self.w(nc.vector, "ve")
        ve.inc(nc.vector.tensor_tensor(gwlo16[:, :], gw16[:, :], gwhi16[:, :], ALU.subtract))
        with nc.psum_tensor("ps_pg", [128, 128], BF16) as psb:
            for j in range(8):
                for (src16, dst) in [(gwhi16, self.g_ghi), (gwlo16, self.g_glo)]:
                    self.w(nc.vector, "pe")
                    ve.inc(nc.vector.tensor_copy(
                        g16t[:, :],
                        src16.ap().rearrange("c (p e) -> c p e", e=8)[:, :, j]))
                    self.w(nc.tensor, "ve")
                    pe.inc(nc.tensor.transpose(psb[:, 0:16], g16t[:, :],
                                               self.g_eye_bf[0:16, 0:16]))
                    self.w(nc.vector, "pe")
                    ve.inc(nc.vector.tensor_copy(
                        dst.ap().rearrange("p (c e) -> p c e", e=8)[:, :, j],
                        psb[:, 0:16]))
        self.barrier()
        ctx.close()

    def p1(self):
        nc = self.nc
        with ExitStack() as ctx:
            E = ctx.enter_context
            st = dict(
                xbTh=E(nc.sbuf_tensor("xbTh", [128, NCD * TP], BF16)),
                xbTl=E(nc.sbuf_tensor("xbTl", [128, NCD * TP], BF16)),
                kraw=E(nc.sbuf_tensor("kraw", [128, NCD * TP], F32)),
                wfh=E(nc.sbuf_tensor("wfh", [128, NCD * 512], BF16)),
                wfl=E(nc.sbuf_tensor("wfl", [128, NCD * 512], BF16)),
                xsh=E(nc.sbuf_tensor("xsh", [128, D], BF16)),
                xsl=E(nc.sbuf_tensor("xsl", [128, D], BF16)),
                khl=E(nc.sbuf_tensor("khl", [128, 512], BF16)),
                kll=E(nc.sbuf_tensor("kll", [128, 512], BF16)),
                xt0=E(nc.sbuf_tensor("xt0", [128, D], F32)),
                xt1=E(nc.sbuf_tensor("xt1", [128, D], F32)),
                xbtok=E(nc.sbuf_tensor("xbtok", [128, D], F32)),
                sq0=E(nc.sbuf_tensor("sq0", [128, 512], F16)),
                sq1=E(nc.sbuf_tensor("sq1", [128, 512], F16)),
                t1f=E(nc.sbuf_tensor("t1f", [128, 512], F32)),
                t2f=E(nc.sbuf_tensor("t2f", [128, 512], F32)),
                sA=E(nc.sbuf_tensor("sArow", [1, TP], F32)),
                sB=E(nc.sbuf_tensor("sBrow", [1, TP], F32)),
                mrow=E(nc.sbuf_tensor("mrow", [1, TP], F32)),
                rrow=E(nc.sbuf_tensor("rrow", [1, TP], F32)),
                trow=E(nc.sbuf_tensor("trow", [1, TP], F32)),
                kn2=E(nc.sbuf_tensor("kn2", [16, SEQ], F16)),
                qn2=E(nc.sbuf_tensor("qn2", [16, TOWN], F16)),
                vtmp=E(nc.sbuf_tensor("vtmp", [128, 512], F32)),
                vtok=E(nc.sbuf_tensor("vtok", [128, TP], F32)),
                ps_p0=E(nc.psum_tensor("ps_p0", [128, 512], F32)),
                ps_p1=E(nc.psum_tensor("ps_p1", [128, 512], F32)),
                ps_sA=E(nc.psum_tensor("ps_sA", [1, 512], F32)),
                ps_sB=E(nc.psum_tensor("ps_sB", [1, 512], F32)),
                ps_s16=E(nc.psum_tensor("ps_s16", [16, 512], F32)),
                ps_tb=E(nc.psum_tensor("ps_tb1", [128, 1024], BF16)),
                ps_b0=E(nc.psum_tensor("ps_b0", [128, 512], F32)),
                ps_b1=E(nc.psum_tensor("ps_b1", [128, 512], F32)),
            )
            self._wf = {"tick": 0, "used": [0, 0], "q": []}
            for p in range(8):
                self._p1_stepA(p, "kv", st)
                self._p1_proj(p, "k", st)
                self._p1_proj_v(p, st)
            for p in range(4):
                self._p1_stepA(p, "q", st)
                self._p1_proj(p, "q", st)
            self._bounds_ps = st["ps_b0"]
            self._p1_bounds(st["kn2"], st["qn2"])
        self.barrier()

    def _xbT_rhs(self, slab, ci):
        v = slab.ap()[:, :]
        return v.rearrange("p (i ct) -> p i ct", i=NT)[:, :, ci * 128:(ci + 1) * 128]

    def _p1_stepA(self, pidx, kind, st):
        nc = self.nc
        xl = self.sem("xl"); pe = self.sem("pe"); ve = self.sem("ve"); ae = self.sem("ae")
        src = self.i_xk if kind == "kv" else self.i_xq
        row0 = pidx * TP
        xbtok = st["xbtok"]
        for i in range(NT):
            xt = st[f"xt{i % 2}"]
            if i >= 2:
                self.w(nc.sync, "ve", "ae")
            xA = self.sem(f"xA{i % 2}")
            v_xl = xA.inc(nc.sync.dma_start(
                out=xt[:, :], in_=src[row0 + i * 128: row0 + (i + 1) * 128, :]), 16)
            nc.vector.wait_ge(xA.h, v_xl)
            ve.inc(nc.vector.reduce_sum(self.col(0 + (i % 2)), xt[:, :], axis=AXX))
            nc.scalar.wait_ge(xA.h, v_xl)
            self.w(nc.scalar, "ve")
            ae.inc(nc.scalar.activation(self.g_sqdump[:, :], xt[:, :], AF.Square,
                                        accum_out=self.col(2 + (i % 2))))
            c_m = self.col(4 + (i % 2)); c_v = self.col(6)
            c_r = self.col(8 + (i % 2)); c_t = self.col(7)
            self.w(nc.vector, "ae")
            ve.inc(nc.vector.tensor_scalar(c_m, self.col(0 + (i % 2)), INV_D, None, ALU.mult))
            ve.inc(nc.vector.tensor_scalar(c_t, self.col(2 + (i % 2)), INV_D, None, ALU.mult))
            self.w(nc.vector, "ve")
            ve.inc(nc.vector.tensor_tensor(self.col(12), c_m, c_m, ALU.mult))
            self.w(nc.vector, "ve")
            ve.inc(nc.vector.tensor_tensor(c_v, c_t, self.col(12), ALU.subtract))
            self.w(nc.scalar, "ve")
            ae.inc(nc.scalar.activation(self.col(13), c_v, AF.Ln,
                                        bias=self.g_pool[:, 0:1], scale=1.0))
            self.w(nc.scalar, "ae")
            ae.inc(nc.scalar.activation(c_r, self.col(13), AF.Exp, bias=0.0, scale=-0.5))
            self.w(nc.vector, "ae", "pe")
            ve.inc(nc.vector.tensor_scalar(xbtok[:, :], xt[:, :], c_m, c_r,
                                           ALU.subtract, ALU.mult))
            self.w(nc.vector, "ve")
            ve.inc(nc.vector.tensor_copy(st["xsh"][:, :], xbtok[:, :]))
            self.w(nc.vector, "ve")
            ve.inc(nc.vector.tensor_tensor(st["xsl"][:, :], xbtok[:, :], st["xsh"][:, :],
                                           ALU.subtract))
            nc.tensor.wait_ge(ve.h, ve.n)
            for (tok, slab) in [(st["xsh"], st["xbTh"]), (st["xsl"], st["xbTl"])]:
                for c4 in range(4):
                    self.w(nc.tensor, "ve")
                    for j in range(4):
                        ci = c4 * 4 + j
                        mm = nc.tensor.transpose(st["ps_tb"][:, j * 128:(j + 1) * 128],
                                                 tok[:, ci * 128:(ci + 1) * 128],
                                                 self.g_eye_bf[:, :])
                    pe.inc(mm)
                    nc.vector.wait_ge(pe.h, pe.n)
                    ve.inc(nc.vector.tensor_copy(
                        slab[:, i * D + c4 * 512: i * D + (c4 + 1) * 512],
                        st["ps_tb"][:, 0:512]))

    def _proj_half(self, st, colbase):
        nc = self.nc
        ve = self.sem("ve")
        t = self._wf
        b = t["tick"] % 2
        t["tick"] += 1
        ws = self.sem(f"wf{b}")
        nc.gpsimd.wait_ge(self.sem("pe").h, t["used"][b])
        wfh_half = st["wfh"].ap().rearrange("p (c u) -> p c u", u=512)[:, :, b * 256:(b + 1) * 256]
        wfl_half = st["wfl"].ap().rearrange("p (c u) -> p c u", u=512)[:, :, b * 256:(b + 1) * 256]
        wsrc = self.i_wqkv[:, colbase:colbase + 256].rearrange("(c p) u -> p c u", p=128)
        for hlf in range(2):
            ws.inc(nc.gpsimd.dma_start(out=wfh_half[:, hlf * 8:(hlf + 1) * 8, :],
                                       in_=wsrc[:, hlf * 8:(hlf + 1) * 8, :]), 16)
        v_hi = ws.n
        nc.vector.wait_ge(ws.h, v_hi)
        v_ng = ve.inc(nc.vector.tensor_scalar(wfl_half, wfh_half, -1.0, None, ALU.mult))
        nc.gpsimd.wait_ge(ve.h, v_ng)
        for hlf in range(2):
            ws.inc(nc.gpsimd.dma_start(out=wfl_half[:, hlf * 8:(hlf + 1) * 8, :],
                                       in_=wsrc[:, hlf * 8:(hlf + 1) * 8, :],
                                       accum_op=ALU.add), 16)
        t["q"].append((b, ws, ws.n))

    def _p1_proj(self, pidx, kind, st):
        nc = self.nc
        pe = self.sem("pe"); ve = self.sem("ve"); ae = self.sem("ae")
        kraw = st["kraw"]
        colb0 = D if kind == "k" else 0
        self._proj_half(st, colb0)
        evac_done = {}
        for hw in range(8):
            if hw + 1 < 8:
                self._proj_half(st, colb0 + (hw + 1) * 256)
            b, ws, vw = self._wf["q"].pop(0)
            nc.tensor.wait_ge(ws.h, vw)
            for oc2 in range(2):
                kc = hw * 2 + oc2
                psp = (st["ps_p0"], st["ps_p1"])[kc % 2]
                if kc >= 2:
                    nc.tensor.wait_ge(ve.h, evac_done[kc - 2])
                nterm = 3 * NCD
                ti = 0
                for (wsl_, xsl_) in [(st["wfh"], st["xbTh"]), (st["wfh"], st["xbTl"]),
                                     (st["wfl"], st["xbTh"])]:
                    for ci in range(NCD):
                        mm = nc.tensor.matmul(
                            psp[:, :],
                            wsl_[:, ci * 512 + b * 256 + oc2 * 128: ci * 512 + b * 256 + (oc2 + 1) * 128],
                            self._xbT_rhs(xsl_, ci),
                            start=(ti == 0), stop=(ti == nterm - 1))
                        ti += 1
                pe.inc(mm)
                self.w(nc.vector, "pe", "sl")
                dst = kraw[:, kc * TP:(kc + 1) * TP]
                ve.inc(nc.vector.tensor_copy(dst, psp[:, :]))
                evac_done[kc] = ve.n
                if oc2 == 1:
                    self._wf["used"][b] = pe.n
                self.w(nc.scalar, "ve")
                nc.scalar.wait_ge(pe.h, pe.n)
                sq = st[f"sq{kc % 2}"]
                ae.inc(nc.scalar.activation(sq[:, :], dst, AF.Square))
                self.w(nc.tensor, "ae")
                pe.inc(nc.tensor.matmul(st["ps_sA"][:, :], self.g_ones_col_f[:, 0:1], dst,
                                        start=(kc == 0), stop=(kc == NCD - 1)))
                pe.inc(nc.tensor.matmul(st["ps_sB"][:, :], self.g_ones_col_h[:, 0:1],
                                        sq[:, :], start=(kc == 0), stop=(kc == NCD - 1)))
        if True:
            pass
        self.w(nc.vector, "pe")
        ve.inc(nc.vector.tensor_copy(st["sA"][0:1, :], st["ps_sA"][:, :]))
        ve.inc(nc.vector.tensor_copy(st["sB"][0:1, :], st["ps_sB"][:, :]))
        self.w(nc.vector, "ve")
        ve.inc(nc.vector.tensor_scalar(st["mrow"][0:1, :], st["sA"][0:1, :], INV_D, None, ALU.mult))
        ve.inc(nc.vector.tensor_scalar(st["sB"][0:1, :], st["sB"][0:1, :], INV_D, None, ALU.mult))
        self.w(nc.vector, "ve")
        ve.inc(nc.vector.tensor_tensor(st["trow"][0:1, :], st["mrow"][0:1, :],
                                       st["mrow"][0:1, :], ALU.mult))
        self.w(nc.vector, "ve")
        ve.inc(nc.vector.tensor_tensor(st["rrow"][0:1, :], st["sB"][0:1, :],
                                       st["trow"][0:1, :], ALU.subtract))
        self.w(nc.scalar, "ve")
        ae.inc(nc.scalar.activation(st["trow"][0:1, :], st["rrow"][0:1, :], AF.Ln,
                                    bias=self.g_pool[0:1, 1:2], scale=1.0))
        self.w(nc.scalar, "ae")
        ae.inc(nc.scalar.activation(st["rrow"][0:1, :], st["trow"][0:1, :], AF.Exp,
                                    bias=0.0, scale=-0.5))
        self._p1_normalize_flush(pidx, kind, st)

    def _p1_normalize_flush(self, pidx, kind, st):
        nc = self.nc
        pe = self.sem("pe"); ve = self.sem("ve"); ae = self.sem("ae"); sl = self.sem("sl")
        kraw = st["kraw"]; t1f = st["t1f"]; t2f = st["t2f"]
        gam = self.g_gk if kind == "k" else self.g_gq
        scr = (self.s_k, self.s_kl) if kind == "k" else (self.s_q, self.s_ql)
        n2sb = st["kn2"] if kind == "k" else st["qn2"]
        row0 = pidx * TP
        self.w(nc.tensor, "ae", "ve")
        pe.inc(nc.tensor.matmul(st["ps_b0"][:, :], self.g_ones_row_f[0:1, :],
                                st["mrow"][0:1, :], start=True, stop=True))
        pe.inc(nc.tensor.matmul(st["ps_b1"][:, :], self.g_ones_row_f[0:1, :],
                                st["rrow"][0:1, :], start=True, stop=True))
        v_bc = pe.n
        for kc in range(NCD):
            sli = kraw[:, kc * TP:(kc + 1) * TP]
            nc.vector.wait_ge(pe.h, v_bc)
            ve.inc(nc.vector.tensor_tensor(t1f[:, :], sli, st["ps_b0"][:, :], ALU.subtract))
            self.w(nc.vector, "ve")
            ve.inc(nc.vector.tensor_tensor(t2f[:, :], t1f[:, :], st["ps_b1"][:, :], ALU.mult))
            self.w(nc.vector, "ve")
            if kind == "k":
                ve.inc(nc.vector.tensor_scalar(sli, t2f[:, :], gam[:, kc:kc + 1], None, ALU.mult))
            else:
                ve.inc(nc.vector.tensor_scalar(sli, t2f[:, :], gam[:, kc:kc + 1], RS2,
                                               ALU.mult, ALU.mult))
            self.w(nc.scalar, "ve")
            nc.scalar.wait_ge(pe.h, pe.n)
            sq = st[f"sq{kc % 2}"]
            ae.inc(nc.scalar.activation(sq[:, :], sli, AF.Square))
            nc.tensor.wait_ge(ae.h, ae.n)
            pe.inc(nc.tensor.matmul(st["ps_s16"][:, :],
                                    self.g_ehead[:, kc * 16:(kc + 1) * 16],
                                    sq[:, :], start=(kc == 0), stop=(kc == NCD - 1)))
            self.w(nc.vector, "sl")
            ve.inc(nc.vector.tensor_copy(st["khl"][:, :], sli))
            self.w(nc.vector, "ve")
            ve.inc(nc.vector.tensor_tensor(st["kll"][:, :], sli, st["khl"][:, :],
                                           ALU.subtract))
            self.w(nc.sync, "ve")
            sl.inc(nc.sync.dma_start(
                out=scr[0][kc * 128:(kc + 1) * 128, row0:row0 + TP],
                in_=st["khl"][:, :]), 16)
            sl.inc(nc.sync.dma_start(
                out=scr[1][kc * 128:(kc + 1) * 128, row0:row0 + TP],
                in_=st["kll"][:, :]), 16)
        self.w(nc.vector, "pe")
        ve.inc(nc.vector.tensor_copy(n2sb[:, row0:row0 + TP], st["ps_s16"][:, :]))

    def _p1_proj_v(self, pidx, st):
        nc = self.nc
        pe = self.sem("pe"); ve = self.sem("ve"); wl = self.sem("wl"); sl = self.sem("sl")
        row0 = pidx * TP
        vtmp = st["vtmp"]; vtok = st["vtok"]
        self._proj_half(st, 2 * D)
        for hw in range(8):
            if hw + 1 < 8:
                self._proj_half(st, 2 * D + (hw + 1) * 256)
            b, ws, vw = self._wf["q"].pop(0)
            nc.tensor.wait_ge(ws.h, vw)
            for oc2 in range(2):
                vc = hw * 2 + oc2
                psp = (st["ps_p0"], st["ps_p1"])[vc % 2]
                self.w(nc.tensor, "ve")
                nterm = 3 * NCD
                ti = 0
                for (wsl_, xsl_) in [(st["wfh"], st["xbTh"]), (st["wfh"], st["xbTl"]),
                                     (st["wfl"], st["xbTh"])]:
                    for ci in range(NCD):
                        mm = nc.tensor.matmul(
                            psp[:, :],
                            wsl_[:, ci * 512 + b * 256 + oc2 * 128: ci * 512 + b * 256 + (oc2 + 1) * 128],
                            self._xbT_rhs(xsl_, ci),
                            start=(ti == 0), stop=(ti == nterm - 1))
                        ti += 1
                pe.inc(mm)
                if oc2 == 1:
                    self._wf["used"][b] = pe.n
                self.w(nc.vector, "pe")
                ve.inc(nc.vector.tensor_copy(vtmp[:, :], psp[:, :]))
                nc.tensor.wait_ge(ve.h, ve.n)
                for j in range(4):
                    mm = nc.tensor.transpose(st["ps_b0"][:, j * 128:(j + 1) * 128],
                                             vtmp[:, j * 128:(j + 1) * 128],
                                             self.g_eye_f[:, :])
                pe.inc(mm)
                self.w(nc.vector, "pe")
                nc.vector.wait_ge(sl.h, sl.n)
                ve.inc(nc.vector.tensor_copy(vtok[:, :], st["ps_b0"][:, :]))
                self.w(nc.sync, "ve")
                sl.inc(nc.sync.dma_start(
                    out=self.s_v[row0:row0 + TP, vc * 128:(vc + 1) * 128]
                        .rearrange("(tt p) c -> p tt c", p=128),
                    in_=vtok.ap().rearrange("p (tt c) -> p tt c", c=128)), 16)

    def _p1_bounds(self, kn2, qn2):
        nc = self.nc
        pe = self.sem("pe"); ve = self.sem("ve"); ae = self.sem("ae")
        self.w(nc.vector, "pe", "ae", "sl")
        km = self.g_km2
        ve.inc(nc.vector.reduce_max(km[:, 0:1], kn2[:, :], axis=AXX))
        ve.inc(nc.vector.reduce_max(km[:, 1:2], qn2[:, :], axis=AXX))
        self.w(nc.scalar, "ve")
        ae.inc(nc.scalar.activation(km[:, 0:1], km[:, 0:1], AF.Sqrt))
        self.w(nc.scalar, "ae")
        ae.inc(nc.scalar.activation(km[:, 1:2], km[:, 1:2], AF.Sqrt))
        self.w(nc.vector, "ae")
        ve.inc(nc.vector.tensor_tensor(self.g_bh[:, :], km[:, 0:1], km[:, 1:2], ALU.mult))
        psb = self._bounds_ps
        self.w(nc.tensor, "ve")
        pe.inc(nc.tensor.transpose(psb[0:1, 0:16], self.g_bh[:, :],
                                   self.g_eye_f[0:16, 0:16]))
        self.w(nc.vector, "pe")
        ve.inc(nc.vector.tensor_scalar(self.g_bhrow[0:1, :], psb[0:1, 0:16],
                                       -1.0, None, ALU.mult))
        self.w(nc.tensor, "ve")
        pe.inc(nc.tensor.matmul(psb[:, 0:16], self.g_ones_row_f[0:1, :],
                                self.g_bhrow[0:1, :], start=True, stop=True))
        self.w(nc.vector, "pe")
        ve.inc(nc.vector.tensor_copy(self.g_Bneg[:, :], psb[:, 0:16]))

    def p2p3(self):
        nc = self.nc
        with ExitStack() as ctx:
            E = ctx.enter_context
            st = dict(
                masksl=E(nc.sbuf_tensor("masksl", [128, 8 * 512], BF16)),
                qh0=E(nc.sbuf_tensor("qh0", [128, 512], BF16)),
                qh1=E(nc.sbuf_tensor("qh1", [128, 512], BF16)),
                ql0=E(nc.sbuf_tensor("ql0_t", [128, 512], BF16)),
                ql1=E(nc.sbuf_tensor("ql1_t", [128, 512], BF16)),
                kh0=E(nc.sbuf_tensor("kh0", [128, 8 * 128], BF16)),
                kh1=E(nc.sbuf_tensor("kh1", [128, 8 * 128], BF16)),
                kl0=E(nc.sbuf_tensor("kl0", [128, 8 * 128], BF16)),
                kl1=E(nc.sbuf_tensor("kl1", [128, 8 * 128], BF16)),
                vh0=E(nc.sbuf_tensor("vh0", [128, 8 * 128], F32)),
                vh1=E(nc.sbuf_tensor("vh1", [128, 8 * 128], F32)),
                pf0=E(nc.sbuf_tensor("pf0", [128, 1024], F32)),
                pf1=E(nc.sbuf_tensor("pf1", [128, 1024], F32)),
                pbm0=E(nc.sbuf_tensor("pbm0", [128, 512], F32)),
                pbm1=E(nc.sbuf_tensor("pbm1", [128, 512], F32)),
                pbm2=E(nc.sbuf_tensor("pbm2", [128, 512], F32)),
                pbm3=E(nc.sbuf_tensor("pbm3", [128, 512], F32)),
                dsum0=E(nc.sbuf_tensor("dsum0", [128, 512], F32)),
                dsum1=E(nc.sbuf_tensor("dsum1", [128, 512], F32)),
                rbsb=E(nc.sbuf_tensor("rbsb", [128, 512], F32)),
                drow=E(nc.sbuf_tensor("drow", [1, 512], F32)),
                ctxn0=E(nc.sbuf_tensor("ctxn0", [128, 512], F32)),
                ctxn1=E(nc.sbuf_tensor("ctxn1", [128, 512], F32)),
                ctxb0=E(nc.sbuf_tensor("ctxb0", [128, 128], F32)),
                ctxb1=E(nc.sbuf_tensor("ctxb1", [128, 128], F32)),
                wob0=E(nc.sbuf_tensor("wob0", [128, 512], F32)),
                wob1=E(nc.sbuf_tensor("wob1", [128, 512], F32)),
                wob2=E(nc.sbuf_tensor("wob2", [128, 512], F32)),
                wob3=E(nc.sbuf_tensor("wob3", [128, 512], F32)),
                ao=E(nc.sbuf_tensor("aorow", [128, D], F32)),
                rowf=E(nc.sbuf_tensor("rowf", [128, D], F32)),
                xhiT=E(nc.sbuf_tensor("xhiT", [128, D], BF16)),
                xloT=E(nc.sbuf_tensor("xloT", [128, D], BF16)),
                xhitok=E(nc.sbuf_tensor("xhitok", [128, D], BF16)),
                xlotok=E(nc.sbuf_tensor("xlotok", [128, D], BF16)),
                ps_S0=E(nc.psum_tensor("ps_S0", [128, 1024], F32)),
                ps_S1=E(nc.psum_tensor("ps_S1", [128, 1024], F32)),
                ps_ctx=E(nc.psum_tensor("ps_ctx", [128, 512], F32)),
                ps_D=E(nc.psum_tensor("ps_D", [1, 512], F32)),
                ps_tb=E(nc.psum_tensor("ps_tb", [128, 1024], BF16)),
                ps_g=E(nc.psum_tensor("ps_g", [128, 512], F32)),
            )
            self._p3_prev_last = 0
            for g in range(4):
                self._p2_group(g, st)
                self._p3_group(g, st)
        self.barrier()

    def _p2_load_q(self, g, h, st):
        nc = self.nc
        ql = self.sem("ql"); pe = self.sem("pe")
        if not hasattr(self, "_q_loaded"):
            self._q_loaded = [0, 0]
            self._kv_queue = []
            self._kv_tick = 0
            self._pf_cons = [0, 0]
            self._ctx_flushed = [0, 0]
            self._kv_first = True
            self._q_first = True
        if not (self._q_first and h < 2):
            nc.sync.wait_ge(pe.h, pe.n)
        if h >= 1:
            self._q_first = False
        qs = self.sem(f"ql{h % 2}")
        qs.inc(nc.sync.dma_start(
            out=st[f"qh{h % 2}"][:, :],
            in_=self.s_q[h * 128:(h + 1) * 128, g * 512:(g + 1) * 512]), 16)
        self._q_loaded[h % 2] = qs.inc(nc.sync.dma_start(
            out=st[f"ql{h % 2}"][:, :],
            in_=self.s_ql[h * 128:(h + 1) * 128, g * 512:(g + 1) * 512]), 16)

    def _p2_load_kv(self, g, h, cidx, st, nkt):
        nc = self.nc
        hl = self.sem("hl"); pe = self.sem("pe")
        if not self._kv_first:
            nc.sync.wait_ge(pe.h, pe.n)
        self._kv_first = False
        n = min(8, nkt - cidx * 8)
        b = self._kv_tick % 2
        self._kv_tick += 1
        hs = self.sem(f"hl{b}")
        hs.inc(nc.sync.dma_start(
            out=st[f"kh{b}"][:, 0:n * 128],
            in_=self.s_k[h * 128:(h + 1) * 128, cidx * 1024: cidx * 1024 + n * 128]), 16)
        hs.inc(nc.sync.dma_start(
            out=st[f"kl{b}"][:, 0:n * 128],
            in_=self.s_kl[h * 128:(h + 1) * 128, cidx * 1024: cidx * 1024 + n * 128]), 16)
        self._kv_queue.append((b, hs.inc(nc.sync.dma_start(
            out=st[f"vh{b}"].ap()[:, 0:n * 128].rearrange("p (kt c) -> p kt c", c=128),
            in_=self.s_v[cidx * 1024: cidx * 1024 + n * 128, h * 128:(h + 1) * 128]
                .rearrange("(kt p) c -> p kt c", p=128)), 16)))

    def _p2_group(self, g, st):
        nc = self.nc
        nkt = QG_NKT[g]
        pe = self.sem("pe"); ve = self.sem("ve"); ae = self.sem("ae")
        sl = self.sem("sl"); hl = self.sem("hl"); ql = self.sem("ql")
        self.w(nc.sync, "ve", "pe")
        for j in range(8):
            sl.inc(nc.sync.dma_start(out=st["masksl"][:, j * 512:(j + 1) * 512],
                                     in_=self.i_masks[g * 8 + j]), 16)
        v_m = sl.n
        nchk = (nkt + 7) // 8
        self._p2_load_q(g, 0, st)
        self._p2_load_kv(g, 0, 0, st, nkt)
        for h in range(NH):
            if h + 1 < NH:
                self._p2_load_q(g, h + 1, st)
            qh = st[f"qh{h % 2}"]; qlo = st[f"ql{h % 2}"]
            nc.tensor.wait_ge(self.sem(f"ql{h % 2}").h, self._q_loaded[h % 2])
            nc.tensor.wait_ge(sl.h, v_m)
            self.w(nc.tensor, "ve")
            exp_done = []
            cons_done = {}
            presum_done = {}
            band0 = 8 * g
            npair = nkt // 2
            pend = None

            def emit_cons(jj, srcs, ds, koff_, vh_):
                nc.tensor.wait_ge(ve.h, presum_done[jj])
                for half in range(2):
                    kt = jj * 2 + half
                    nc.tensor.matmul(st["ps_ctx"][:, :],
                                     vh_[:, (koff_ + half) * 128:(koff_ + half + 1) * 128],
                                     srcs[half],
                                     start=(kt == 0), stop=(kt == nkt - 1))
                mm_ = nc.tensor.matmul(st["ps_D"][:, :], self.g_ones_col_f[:, 0:1],
                                       ds[:, :], start=(jj == 0), stop=(jj == npair - 1))
                pe.inc(mm_)
                cons_done[jj] = pe.n

            for jp in range(npair):
                cidx = jp // 4
                if jp % 4 == 0:
                    if pend is not None:
                        emit_cons(*pend)
                        pend = None
                    if cidx + 1 < nchk:
                        self._p2_load_kv(g, h, cidx + 1, st, nkt)
                    elif h + 1 < NH:
                        self._p2_load_kv(g, h + 1, 0, st, QG_NKT[g])
                    self._kv_cur = self._kv_queue.pop(0)
                    nc.tensor.wait_ge(self.sem(f"hl{self._kv_cur[0]}").h, self._kv_cur[1])
                kh = st[f"kh{self._kv_cur[0]}"]; vh = st[f"vh{self._kv_cur[0]}"]
                klo = st[f"kl{self._kv_cur[0]}"]
                koff = (jp % 4) * 2
                psS = (st["ps_S0"], st["ps_S1"])[jp % 2]
                if jp >= 2:
                    nc.tensor.wait_ge(ae.h, exp_done[jp - 2])
                for half in range(2):
                    sliA = slice((koff + half) * 128, (koff + half + 1) * 128)
                    for ti, (kk, qq) in enumerate([(kh, qh), (kh, qlo), (klo, qh)]):
                        mm = nc.tensor.matmul(
                            psS[:, half * 512:(half + 1) * 512],
                            kk[:, sliA], qq[:, :], start=(ti == 0), stop=(ti == 2))
                pe.inc(mm)
                v_S = pe.n
                if pend is not None:
                    emit_cons(*pend)
                    pend = None
                nc.scalar.wait_ge(pe.h, v_S)
                if jp >= 2:
                    nc.scalar.wait_ge(ve.h, self._pf_cons[jp % 2])
                pf = st[f"pf{jp % 2}"]
                ae.inc(nc.scalar.activation(pf[:, :], psS[:, :], AF.Exp,
                                            bias=self.g_Bneg[:, h:h + 1], scale=1.0))
                exp_done.append(ae.n)
                srcs = []
                nc.vector.wait_ge(ae.h, exp_done[jp])
                if jp >= 2:
                    nc.vector.wait_ge(pe.h, cons_done[jp - 2])
                for half in range(2):
                    kt = jp * 2 + half
                    src_p = pf[:, half * 512:(half + 1) * 512]
                    if kt >= band0:
                        pbm = st[f"pbm{(jp % 2) * 2 + half}"]
                        ve.inc(nc.vector.tensor_tensor(
                            pbm[:, :], src_p,
                            st["masksl"][:, (kt - band0) * 512:(kt - band0 + 1) * 512],
                            ALU.mult))
                        src_p = pbm[:, :]
                    srcs.append(src_p)
                self.w(nc.vector, "ve")
                ds = st[f"dsum{jp % 2}"]
                ve.inc(nc.vector.tensor_tensor(ds[:, :], srcs[0], srcs[1], ALU.add))
                self._pf_cons[jp % 2] = ve.n
                presum_done[jp] = ve.n
                pend = (jp, srcs, ds, koff, vh)
            emit_cons(*pend)
            self.w(nc.vector, "pe")
            ve.inc(nc.vector.reciprocal(st["drow"][0:1, :], st["ps_D"][:, :]))
            self.w(nc.tensor, "ve")
            pe.inc(nc.tensor.matmul(st["ps_g"][:, 0:512], self.g_ones_row_f[0:1, :],
                                    st["drow"][0:1, :], start=True, stop=True))
            self.w(nc.vector, "pe")
            ve.inc(nc.vector.tensor_copy(st["rbsb"][:, :], st["ps_g"][:, 0:512]))
            self.w(nc.vector, "ve")
            cn = st[f"ctxn{h % 2}"]
            cf = self.sem(f"cf{h % 2}")
            nc.vector.wait_ge(cf.h, self._ctx_flushed[h % 2])
            ve.inc(nc.vector.tensor_tensor(cn[:, :], st["ps_ctx"][:, :], st["rbsb"][:, :],
                                           ALU.mult))
            self.w(nc.sync, "ve")
            self._ctx_flushed[h % 2] = cf.inc(nc.sync.dma_start(
                out=self.s_ctx[h * 128:(h + 1) * 128, g * 512:(g + 1) * 512],
                in_=cn[:, :]), 16)

    def _p3_group(self, g, st):
        nc = self.nc
        pe = self.sem("pe"); ve = self.sem("ve"); ae = self.sem("ae")
        sl = self.sem("sl"); xl = self.sem("xl"); cl = self.sem("cl")
        ao = st["ao"]; rowf = st["rowf"]
        for s in range(4):
            tok0 = g * 512 + s * 128
            views = [st["ps_S0"][:, 0:512], st["ps_S0"][:, 512:1024],
                     st["ps_S1"][:, 0:512], st["ps_S1"][:, 512:1024]]
            self.w(nc.tensor, "ae", "ve")
            pv = {}
            v_w = {}
            v_c = {}
            base_pe = pe.n

            def load_wob(t):
                ci_, eg_ = t // 4, t % 4
                wbs = self.sem(f"wb{t % 4}")
                nc.sync.wait_ge(pe.h, pv.get(t - 4, self._p3_prev_last))
                v_w[t] = (wbs, wbs.inc(nc.sync.dma_start(
                    out=st[f"wob{t % 4}"][:, :],
                    in_=self.i_wout[ci_ * 128:(ci_ + 1) * 128,
                                    eg_ * 512:(eg_ + 1) * 512]), 16))

            def load_cb(ci_):
                cbs = self.sem(f"cb{ci_ % 2}")
                nc.sync.wait_ge(pe.h, pv.get((ci_ - 2) * 4 + 3, self._p3_prev_last))
                v_c[ci_] = (cbs, cbs.inc(nc.sync.dma_start(
                    out=st[f"ctxb{ci_ % 2}"][:, :],
                    in_=self.s_ctx[ci_ * 128:(ci_ + 1) * 128, tok0:tok0 + 128]), 16))

            load_cb(0); load_cb(1)
            for t0 in range(3):
                load_wob(t0)
            for t in range(64):
                ci, eg = t // 4, t % 4
                if t + 3 < 64:
                    load_wob(t + 3)
                sw, vw = v_w[t]
                nc.tensor.wait_ge(sw.h, vw)
                sc, vc = v_c[ci]
                nc.tensor.wait_ge(sc.h, vc)
                pe.inc(nc.tensor.matmul(views[eg], st[f"ctxb{ci % 2}"][:, :],
                                        st[f"wob{t % 4}"][:, :],
                                        start=(ci == 0), stop=(ci == NCD - 1)))
                pv[t] = pe.n
                if eg == 3 and ci + 2 < NCD:
                    load_cb(ci + 2)
            self._p3_prev_last = pe.n
            self.w(nc.vector, "pe")
            for eg in range(4):
                ve.inc(nc.vector.tensor_copy(ao[:, eg * 512:(eg + 1) * 512], views[eg]))
            self.w(nc.vector, "ve")
            ve.inc(nc.vector.reduce_sum(self.col(16), ao[:, :], axis=AXX))
            self.w(nc.scalar, "ve")
            ae.inc(nc.scalar.activation(self.g_sqdump[:, :], ao[:, :], AF.Square,
                                        accum_out=self.col(17)))
            self._ln_cols(16, 17, 18, 19)
            self.w(nc.vector, "ae")
            ve.inc(nc.vector.tensor_scalar(rowf[:, :], ao[:, :], self.col(18), self.col(19),
                                           ALU.subtract, ALU.mult))
            v_ln2 = ve.n
            nc.sync.wait_ge(ve.h, v_ln2)
            v_x = xl.inc(nc.sync.dma_start(out=ao[:, :],
                                           in_=self.i_xq[tok0:tok0 + 128, :]), 16)
            nc.vector.wait_ge(xl.h, v_x)
            self.w(nc.vector, "ve")
            ve.inc(nc.vector.tensor_tensor(ao[:, :], rowf[:, :], ao[:, :], ALU.add))
            self.w(nc.sync, "ve")
            sl.inc(nc.sync.dma_start(out=self.s_x2[tok0:tok0 + 128, :], in_=ao[:, :]), 16)
            self.w(nc.vector, "ve")
            ve.inc(nc.vector.reduce_sum(self.col(20), ao[:, :], axis=AXX))
            self.w(nc.scalar, "ve")
            ae.inc(nc.scalar.activation(self.g_sqdump[:, :], ao[:, :], AF.Square,
                                        accum_out=self.col(21)))
            self._ln_cols(20, 21, 22, 23)
            self.w(nc.vector, "ae")
            ve.inc(nc.vector.tensor_scalar(rowf[:, :], ao[:, :], self.col(22), self.col(23),
                                           ALU.subtract, ALU.mult))
            self.w(nc.vector, "ve")
            ve.inc(nc.vector.tensor_copy(st["xhitok"][:, :], rowf[:, :]))
            self.w(nc.vector, "ve")
            ve.inc(nc.vector.tensor_tensor(st["xlotok"][:, :], rowf[:, :],
                                           st["xhitok"][:, :], ALU.subtract))
            v_hl = ve.n
            nc.tensor.wait_ge(ve.h, v_hl)
            for (tok, dstT) in [(st["xhitok"], st["xhiT"]), (st["xlotok"], st["xloT"])]:
                for c4 in range(4):
                    if not (tok is st["xhitok"] and c4 == 0):
                        nc.tensor.wait_ge(ve.h, ve.n)
                    for j in range(4):
                        ci = c4 * 4 + j
                        mm = nc.tensor.transpose(st["ps_tb"][:, j * 128:(j + 1) * 128],
                                                 tok[:, ci * 128:(ci + 1) * 128],
                                                 self.g_eye_bf[:, :])
                    pe.inc(mm)
                    self.w(nc.vector, "pe")
                    ve.inc(nc.vector.tensor_copy(dstT[:, c4 * 512:(c4 + 1) * 512],
                                                 st["ps_tb"][:, 0:512]))
            self.w(nc.sync, "ve")
            sl.inc(nc.sync.dma_start(
                out=self.s_xb2.rearrange("(c p) t -> p c t", p=128)[:, :, tok0:tok0 + 128],
                in_=st["xhiT"].ap().rearrange("p (c t) -> p c t", c=NCD)), 16)
            self.w(nc.tensor, "ve")
            first = True
            for (xT, gt) in [(st["xhiT"], self.g_ghi), (st["xhiT"], self.g_glo),
                             (st["xloT"], self.g_ghi)]:
                last_op = (xT is st["xloT"])
                for ci in range(NCD):
                    mm = nc.tensor.matmul(
                        st["ps_g"][:, 0:8],
                        xT[:, ci * 128:(ci + 1) * 128],
                        gt[:, ci * 8:(ci + 1) * 8],
                        start=first, stop=(last_op and ci == NCD - 1))
                    first = False
            pe.inc(mm)
            self._p3_gate(g * 4 + s, st["ps_g"])

    def _ln_cols(self, c_sum, c_ss, c_mean, c_rstd):
        nc = self.nc
        ve = self.sem("ve"); ae = self.sem("ae")
        self.w(nc.vector, "ae")
        ve.inc(nc.vector.tensor_scalar(self.col(c_mean), self.col(c_sum), INV_D, None, ALU.mult))
        ve.inc(nc.vector.tensor_scalar(self.col(24), self.col(c_ss), INV_D, None, ALU.mult))
        self.w(nc.vector, "ve")
        ve.inc(nc.vector.tensor_tensor(self.col(25), self.col(c_mean), self.col(c_mean), ALU.mult))
        self.w(nc.vector, "ve")
        ve.inc(nc.vector.tensor_tensor(self.col(26), self.col(24), self.col(25), ALU.subtract))
        self.w(nc.scalar, "ve")
        ae.inc(nc.scalar.activation(self.col(27), self.col(26), AF.Ln,
                                    bias=self.g_pool[:, 0:1], scale=1.0))
        self.w(nc.scalar, "ae")
        ae.inc(nc.scalar.activation(self.col(c_rstd), self.col(27), AF.Exp, bias=0.0, scale=-0.5))

    def _p3_gate(self, qsub, ps_g):
        nc = self.nc
        pe = self.sem("pe"); ve = self.sem("ve"); ae = self.sem("ae")
        gl, ge, pr, b8, s1 = self.g_gl, self.g_ge, self.g_pr, self.g_b8, self.g_s1
        self.w(nc.vector, "pe")
        ve.inc(nc.vector.tensor_copy(gl[:, :], ps_g[:, 0:8]))
        self.w(nc.vector, "ve")
        ve.inc(nc.vector.reduce_max(self.col(28), gl[:, :], axis=AXX))
        self.w(nc.vector, "ve")
        ve.inc(nc.vector.tensor_scalar(self.col(29), self.col(28), -1.0, None, ALU.mult))
        self.w(nc.scalar, "ve")
        ae.inc(nc.scalar.activation(ge[:, :], gl[:, :], AF.Exp, bias=self.col(29), scale=1.0,
                                    accum_out=self.col(30)))
        self.w(nc.vector, "ae")
        ve.inc(nc.vector.reciprocal(self.col(31), self.col(30)))
        self.w(nc.vector, "ve")
        ve.inc(nc.vector.tensor_scalar(pr[:, :], ge[:, :], self.col(31), None, ALU.mult))
        self.w(nc.vector, "ve")
        ve.inc(nc.vector.tensor_tensor(b8[:, :], pr[:, :], self.g_ebb[:, :], ALU.add))
        self.w(nc.vector, "ve")
        ve.inc(nc.vector.reduce_max(self.col(28), b8[:, :], axis=AXX))
        self.w(nc.vector, "ve")
        ve.inc(nc.vector.tensor_scalar(s1[:, :], b8[:, :], self.col(28), None, ALU.is_ge))
        self.w(nc.vector, "ve")
        ve.inc(nc.vector.scalar_tensor_tensor(ge[:, :], s1[:, :], -1e30, b8[:, :],
                                              ALU.mult, ALU.add))
        self.w(nc.vector, "ve")
        ve.inc(nc.vector.reduce_max(self.col(29), ge[:, :], axis=AXX))
        self.w(nc.vector, "ve")
        ve.inc(nc.vector.tensor_scalar(s1[:, :], b8[:, :], self.col(29), None, ALU.is_ge))
        self.w(nc.vector, "ve")
        ve.inc(nc.vector.tensor_tensor(self.g_wt[:, qsub * 8:(qsub + 1) * 8],
                                       pr[:, :], s1[:, :], ALU.mult))
        self.w(nc.tensor, "ve")
        psw = ps_g
        pe.inc(nc.tensor.transpose(psw[0:8, 0:128],
                                   self.g_wt[:, qsub * 8:(qsub + 1) * 8],
                                   self.g_eye_f[:, :]))
        self.w(nc.vector, "pe")
        ve.inc(nc.vector.tensor_copy(self.g_wtT8[:, qsub * 128:(qsub + 1) * 128],
                                     psw[0:8, 0:128]))

    def p4p5(self):
        nc = self.nc
        with ExitStack() as ctx:
            E = ctx.enter_context
            acc = E(nc.sbuf_tensor("acc", [128, 4 * D], F32))
            w2b = E(nc.sbuf_tensor("w2b", [128, NHC * 512], BF16))
            w2c = E(nc.sbuf_tensor("w2c", [128, NHC * 512], BF16))
            wsc = E(nc.sbuf_tensor("wslab_c", [128, NCD * 512], BF16))
            wsd = E(nc.sbuf_tensor("wslab_d", [128, NCD * 512], BF16))
            hsl = E(nc.sbuf_tensor("hsl", [128, NHC * 512], BF16))
            xeb = E(nc.sbuf_tensor("xeb", [128, NCD * 512], BF16))
            xbfb = E(nc.sbuf_tensor("xbfb", [128, NCD * 512], BF16))
            silub = E(nc.sbuf_tensor("silub", [128, 512], BF16))
            wbsb = E(nc.sbuf_tensor("wbsb", [128, 512], BF16))
            xrow = E(nc.sbuf_tensor("p5x", [128, D], F32))
            rowf = E(nc.sbuf_tensor("p5f", [128, D], F32))
            ps_A = E(nc.psum_tensor("ps_A", [128, 512], F32))
            ps_B = E(nc.psum_tensor("ps_B", [128, 512], F32))
            ps_A1 = E(nc.psum_tensor("ps_A1", [128, 512], F32))
            ps_B1 = E(nc.psum_tensor("ps_B1", [128, 512], F32))
            ps_C = E(nc.psum_tensor("ps_C", [128, 512], F32))
            ps_D2 = E(nc.psum_tensor("ps_D2", [128, 512], F32))
            ps_E = E(nc.psum_tensor("ps_E", [128, 512], F32))
            sl = self.sem("sl")
            for th in range(2):
                for tg in range(2):
                    tok0 = th * 1024 + tg * 512
                    self.w(nc.sync, "pe", "ve")
                    v_xb = sl.inc(nc.sync.dma_start(
                        out=xbfb.ap().rearrange("p (c t) -> p c t", c=NCD),
                        in_=self.s_xb2.rearrange("(c p) t -> p c t", p=128)
                            [:, :, tok0:tok0 + 512]), 16)
                    for ei, e in enumerate([8] + list(range(8))):
                        self._p4_expert(th, tg, e, ei == 0, v_xb, acc, (w2b, w2c), hsl,
                                        xeb, xbfb, silub, wbsb,
                                        (self.g_wslab_a, self.g_wslab_b, wsc, wsd),
                                        (ps_A, ps_A1), (ps_B, ps_B1), ps_C, ps_D2, ps_E)
                    self._p5_quarter(th, tg, acc, xrow, rowf)
        so = self.sem("so")
        nc.sync.wait_ge(so.h, so.n)

    def _p4_expert(self, th, tg, e, first, v_xb, acc, w2p, hsl, xeb, xbfb, silub, wbsb,
                   wsl, ps_As, ps_Bs, ps_C, ps_D2, ps_E):
        nc = self.nc
        pe = self.sem("pe"); ve = self.sem("ve"); ae = self.sem("ae")
        sl = self.sem("sl")
        tok0 = th * 1024 + tg * 512
        shared = (e == 8)
        if not hasattr(self, "_p4_w13"):
            self._p4_w13 = {"tick": 0, "used": [0, 0], "q": []}
            self._p4_w2 = {"tick": 0, "used": [0, 0], "q": []}
            self._p4_hb = 0
            self._p4_ab_used = [0, 0]
            self._p4_silu_used = 0
        wsz = [512, 512, 512, 256]

        def spec13(ee, ws):
            base = ws * 512
            ln_ = wsz[ws]
            if ee == 8:
                return (self.i_sw1, self.i_sw3, None, base, ln_)
            return (self.i_w1, self.i_w3, ee, base, ln_)

        def load_w13(ee, ws):
            t = self._p4_w13
            b = t["tick"] % 2
            t["tick"] += 1
            wlx = self.sem(f"wA{b}")
            nc.gpsimd.wait_ge(pe.h, t["used"][b])
            a1, a3, ee_, base, ln_ = spec13(ee, ws)
            if ee_ is None:
                s1ap = a1[:, base:base + ln_].rearrange("(c p) u -> p c u", p=128)
                s3ap = a3[:, base:base + ln_].rearrange("(c p) u -> p c u", p=128)
            else:
                s1ap = a1[ee_][:, base:base + ln_].rearrange("(c p) u -> p c u", p=128)
                s3ap = a3[ee_][:, base:base + ln_].rearrange("(c p) u -> p c u", p=128)
            d1 = wsl[2 * b].ap().rearrange("p (c u) -> p c u", u=512)[:, :, 0:ln_]
            d3 = wsl[2 * b + 1].ap().rearrange("p (c u) -> p c u", u=512)[:, :, 0:ln_]
            for hlf in range(2):
                wlx.inc(nc.gpsimd.dma_start(out=d1[:, hlf * 8:(hlf + 1) * 8, :],
                                            in_=s1ap[:, hlf * 8:(hlf + 1) * 8, :]), 16)
                wlx.inc(nc.gpsimd.dma_start(out=d3[:, hlf * 8:(hlf + 1) * 8, :],
                                            in_=s3ap[:, hlf * 8:(hlf + 1) * 8, :]), 16)
            t["q"].append((b, wlx, wlx.n))

        def load_w2(ee, dg):
            t = self._p4_w2
            b = t["tick"] % 2
            t["tick"] += 1
            wlx = self.sem(f"wB{b}")
            nc.gpsimd.wait_ge(pe.h, t["used"][b])
            if ee == 8:
                w2ap = self.i_sw2[:, dg * 512:(dg + 1) * 512].rearrange(
                    "(c p) u -> p c u", p=128)
            else:
                w2ap = self.i_w2[ee][:, dg * 512:(dg + 1) * 512].rearrange(
                    "(c p) u -> p c u", p=128)
            d2 = w2p[b].ap().rearrange("p (c u) -> p c u", u=512)
            for hlf in range(2):
                wlx.inc(nc.gpsimd.dma_start(out=d2[:, hlf * 7:(hlf + 1) * 7, :],
                                            in_=w2ap[:, hlf * 7:(hlf + 1) * 7, :]), 16)
            t["q"].append((b, wlx, wlx.n))

        if first and tg == 0 and th == 0:
            load_w13(8, 0)
        if not shared:
            self.w(nc.tensor, "ve")
            pe.inc(nc.tensor.matmul(ps_E[:, :], self.g_sel8[:, e * 128:(e + 1) * 128],
                                    self.g_wtT8[:, tok0:tok0 + 512], start=True, stop=True))
            self.w(nc.vector, "pe")
            ve.inc(nc.vector.tensor_copy(wbsb[:, :], ps_E[:, :]))
            nc.vector.wait_ge(sl.h, v_xb)
            self.w(nc.vector, "ve")
            for ci in range(NCD):
                ve.inc(nc.vector.tensor_tensor(xeb[:, ci * 512:(ci + 1) * 512],
                                               xbfb[:, ci * 512:(ci + 1) * 512],
                                               wbsb[:, :], ALU.mult))
            xsrc = xeb
            v_xe = ve.n
        else:
            xsrc = xbfb
            v_xe = None
        for ws in range(4):
            wlen = wsz[ws]
            if ws + 1 < 4:
                load_w13(e, ws + 1)
            else:
                load_w2(e, 0)
            myb, dsem, dval = self._p4_w13["q"].pop(0)
            nc.tensor.wait_ge(dsem.h, dval)
            if v_xe is not None:
                nc.tensor.wait_ge(ve.h, v_xe)
            else:
                nc.tensor.wait_ge(sl.h, v_xb)
            for hcw in range(wlen // 128):
                hc = ws * 4 + hcw
                hb = self._p4_hb % 2
                self._p4_hb += 1
                pa = ps_As[hb]; pb = ps_Bs[hb]
                nc.tensor.wait_ge(ve.h, self._p4_ab_used[hb])
                for ci in range(NCD):
                    nc.tensor.matmul(pa[:, :],
                                     wsl[2 * myb][:, ci * 512 + hcw * 128: ci * 512 + (hcw + 1) * 128],
                                     xsrc[:, ci * 512:(ci + 1) * 512],
                                     start=(ci == 0), stop=(ci == NCD - 1))
                for ci in range(NCD):
                    mm = nc.tensor.matmul(pb[:, :],
                                          wsl[2 * myb + 1][:, ci * 512 + hcw * 128: ci * 512 + (hcw + 1) * 128],
                                          xsrc[:, ci * 512:(ci + 1) * 512],
                                          start=(ci == 0), stop=(ci == NCD - 1))
                pe.inc(mm)
                v_ab = pe.n
                nc.scalar.wait_ge(pe.h, v_ab)
                nc.scalar.wait_ge(ve.h, self._p4_silu_used)
                ae.inc(nc.scalar.activation(silub[:, :], pa[:, :], AF.Silu))
                self.w(nc.vector, "ae")
                ve.inc(nc.vector.tensor_tensor(hsl[:, hc * 512:(hc + 1) * 512],
                                               silub[:, :], pb[:, :], ALU.mult))
                self._p4_ab_used[hb] = ve.n
                self._p4_silu_used = ve.n
            self._p4_w13["used"][myb] = pe.n
        for dg in range(4):
            nxt = [8] + list(range(8))
            ei_cur = nxt.index(e)
            if dg + 1 < 4:
                load_w2(e, dg + 1)
            else:
                if ei_cur + 1 < 9:
                    load_w13(nxt[ei_cur + 1], 0)
                elif not (th == 1 and tg == 1):
                    load_w13(8, 0)
            myb, dsem, dval = self._p4_w2["q"].pop(0)
            nc.tensor.wait_ge(dsem.h, dval)
            for tsub in range(4):
                psw = (ps_C, ps_D2)[tsub % 2]
                self.w(nc.tensor, "ve")
                for hc in range(NHC):
                    mm = nc.tensor.matmul(
                        psw[:, :],
                        hsl[:, hc * 512 + tsub * 128: hc * 512 + (tsub + 1) * 128],
                        w2p[myb][:, hc * 512:(hc + 1) * 512],
                        start=(hc == 0), stop=(hc == NHC - 1))
                pe.inc(mm)
                self.w(nc.vector, "pe", "ae")
                a_sl = acc[:, tsub * D + dg * 512: tsub * D + (dg + 1) * 512]
                if first:
                    self.w(nc.vector, "so")
                    ve.inc(nc.vector.tensor_copy(a_sl, psw[:, :]))
                else:
                    ve.inc(nc.vector.tensor_tensor(a_sl, psw[:, :], a_sl, ALU.add))
            self._p4_w2["used"][myb] = pe.n

    def _p5_quarter(self, th, tg, acc, xrow, rowf):
        nc = self.nc
        ve = self.sem("ve"); ae = self.sem("ae"); xl = self.sem("xl"); so = self.sem("so")
        for tsub in range(4):
            tok0 = th * 1024 + tg * 512 + tsub * 128
            a_sl = acc[:, tsub * D:(tsub + 1) * D]
            self.w(nc.sync, "ve")
            v_x = xl.inc(nc.sync.dma_start(out=xrow[:, :],
                                           in_=self.s_x2[tok0:tok0 + 128, :]), 16)
            self.w(nc.vector, "ve")
            ve.inc(nc.vector.reduce_sum(self.col(16), a_sl, axis=AXX))
            self.w(nc.scalar, "ve")
            ae.inc(nc.scalar.activation(self.g_sqdump[:, :], a_sl, AF.Square,
                                        accum_out=self.col(17)))
            self._ln_cols(16, 17, 18, 19)
            self.w(nc.vector, "ae")
            ve.inc(nc.vector.tensor_scalar(rowf[:, :], a_sl, self.col(18), self.col(19),
                                           ALU.subtract, ALU.mult))
            nc.vector.wait_ge(xl.h, v_x)
            self.w(nc.vector, "ve", "so")
            ve.inc(nc.vector.tensor_tensor(a_sl, rowf[:, :], xrow[:, :], ALU.add))
            self.w(nc.sync, "ve")
            so.inc(nc.sync.dma_start(out=self.o_out[tok0:tok0 + 128, :],
                                     in_=a_sl), 16)

    def build(self):
        self.declare()
        self.alloc_globals()
        with self.nc.named_scope("prologue"):
            self.prologue()
        with self.nc.named_scope("p1"):
            self.p1()
        with self.nc.named_scope("p2p3"):
            self.p2p3()
        with self.nc.named_scope("p4p5"):
            self.p4p5()
        return self.nc


def make_masks(pos):
    t = np.arange(128)[:, None]
    o = np.arange(128)[None, :]
    m = np.zeros((32, 128, 512), np.float32)
    for g in range(4):
        for j in range(8):
            kt = 8 * g + j
            for s in range(4):
                p = pos[4 * g + s]
                m[g * 8 + j][:, s * 128:(s + 1) * 128] = (128 * kt + t <= 128 * p + o)
    return m.astype(BF)


def kernel(L0_attn_x, L0_attn_Wqkv, L0_attn_q_ln_scale, L0_attn_k_ln_scale,
           L0_attn_out_W, L0_ffn_gate_w, L0_ffn_w1, L0_ffn_w2, L0_ffn_w3,
           L0_ffn_sw1, L0_ffn_sw2, L0_ffn_sw3, L0_ffn_expert_bias):
    x = np.ascontiguousarray(np.asarray(L0_attn_x, dtype=np.float32))
    B, S, _ = x.shape
    b = Builder()
    nc = b.build()
    shared = {
        "wqkv": np.ascontiguousarray(np.asarray(L0_attn_Wqkv, np.float32)),
        "qln": np.ascontiguousarray(np.asarray(L0_attn_q_ln_scale, np.float32)),
        "kln": np.ascontiguousarray(np.asarray(L0_attn_k_ln_scale, np.float32)),
        "wout": np.ascontiguousarray(np.asarray(L0_attn_out_W, np.float32)),
        "gw": np.ascontiguousarray(np.asarray(L0_ffn_gate_w, np.float32)),
        "eb": np.ascontiguousarray(np.asarray(L0_ffn_expert_bias, np.float32)),
        "w1": np.ascontiguousarray(np.asarray(L0_ffn_w1, np.float32)),
        "w2": np.ascontiguousarray(np.asarray(L0_ffn_w2, np.float32)),
        "w3": np.ascontiguousarray(np.asarray(L0_ffn_w3, np.float32)),
        "sw1": np.ascontiguousarray(np.asarray(L0_ffn_sw1, np.float32)),
        "sw2": np.ascontiguousarray(np.asarray(L0_ffn_sw2, np.float32)),
        "sw3": np.ascontiguousarray(np.asarray(L0_ffn_sw3, np.float32)),
    }
    in_maps = []
    for c in range(8):
        bi = c // 2
        pos = positions_for_core(c)
        xb = x[bi]
        xq = np.ascontiguousarray(xb.reshape(32, 128, D)[pos].reshape(TOWN, D))
        m = {"xq": xq, "xk": np.ascontiguousarray(xb), "masks": make_masks(pos)}
        m.update(shared)
        in_maps.append(m)
    res = run_bass_kernel_spmd(nc, in_maps, core_ids=list(range(8)))
    out = np.empty((B, S, D), np.float32)
    for c in range(8):
        bi = c // 2
        pos = positions_for_core(c)
        o = res.results[c]["out"].reshape(16, 128, D)
        for j, p in enumerate(pos):
            out[bi, p * 128:(p + 1) * 128, :] = o[j]
    return out


# version 14
# speedup vs baseline: 1.1062x; 1.0727x over previous
import numpy as np
import ml_dtypes
from contextlib import ExitStack
import concourse.bass as bass
import concourse.mybir as mybir
from concourse.bass_utils import run_bass_kernel_spmd

F32 = mybir.dt.float32
BF16 = mybir.dt.bfloat16
F16 = mybir.dt.float16
AF = mybir.ActivationFunctionType
ALU = mybir.AluOpType
AXX = mybir.AxisListType.X
BF = ml_dtypes.bfloat16

D = 2048
NCD = 16
HID = 1792
NHC = 14
NH = 16
SEQ = 4096
TOWN = 2048
NKT = 32
QG_NKT = [8, 16, 24, 32]
RS2 = float(1.0 / np.sqrt(128.0))
EPS_LN = 1e-5
EPS_QK = 1e-6
INV_D = 1.0 / 2048.0
TP = 512
NT = TP // 128


def positions_for_core(c):
    if c % 2 == 0:
        return [i for i in range(32) if i % 4 in (0, 3)]
    return [i for i in range(32) if i % 4 in (1, 2)]


class Sem:
    def __init__(self, nc, name):
        self.h = nc.alloc_semaphore(name=name)
        self.n = 0

    def inc(self, inst, k=1):
        inst.then_inc(self.h, k)
        self.n += k
        return self.n


class Builder:
    def __init__(self):
        self.nc = bass.Bass("TRN2", target_bir_lowering=False)
        self.sems = {}

    def sem(self, name):
        if name not in self.sems:
            self.sems[name] = Sem(self.nc, name)
        return self.sems[name]

    def w(self, eng, *names):
        for n in names:
            s = self.sem(n)
            if s.n > 0:
                eng.wait_ge(s.h, s.n)

    def barrier(self):
        nc = self.nc
        b = self.sem("bar")
        engs = [nc.tensor, nc.vector, nc.scalar, nc.sync, nc.gpsimd]
        for nm, s in list(self.sems.items()):
            if nm != "bar" and s.n > 0:
                nc.sync.wait_ge(s.h, s.n)
        for e in engs:
            e.drain()
            e.sem_inc(b.h, 1)
        b.n += 5
        for e in engs:
            e.wait_ge(b.h, b.n)

    def declare(self):
        nc = self.nc
        ap = lambda t: t.ap()
        self.i_xq = ap(nc.dram_tensor("xq", [TOWN, D], F32, kind="ExternalInput"))
        self.i_xk = ap(nc.dram_tensor("xk", [SEQ, D], F32, kind="ExternalInput"))
        self.i_masks = ap(nc.dram_tensor("masks", [32, 128, 512], BF16, kind="ExternalInput"))
        self.i_wqkv = ap(nc.dram_tensor("wqkv", [D, 3 * D], F32, kind="ExternalInput"))
        self.i_qln = ap(nc.dram_tensor("qln", [D], F32, kind="ExternalInput"))
        self.i_kln = ap(nc.dram_tensor("kln", [D], F32, kind="ExternalInput"))
        self.i_wout = ap(nc.dram_tensor("wout", [D, D], F32, kind="ExternalInput"))
        self.i_gw = ap(nc.dram_tensor("gw", [D, 8], F32, kind="ExternalInput"))
        self.i_eb = ap(nc.dram_tensor("eb", [8], F32, kind="ExternalInput"))
        self.i_w1 = ap(nc.dram_tensor("w1", [8, D, HID], F32, kind="ExternalInput"))
        self.i_w2 = ap(nc.dram_tensor("w2", [8, HID, D], F32, kind="ExternalInput"))
        self.i_w3 = ap(nc.dram_tensor("w3", [8, D, HID], F32, kind="ExternalInput"))
        self.i_sw1 = ap(nc.dram_tensor("sw1", [D, HID], F32, kind="ExternalInput"))
        self.i_sw2 = ap(nc.dram_tensor("sw2", [HID, D], F32, kind="ExternalInput"))
        self.i_sw3 = ap(nc.dram_tensor("sw3", [D, HID], F32, kind="ExternalInput"))
        self.o_out = ap(nc.dram_tensor("out", [TOWN, D], F32, kind="ExternalOutput"))
        self.s_k = ap(nc.dram_tensor("kscr", [D, SEQ], BF16, kind="Internal"))
        self.s_kl = ap(nc.dram_tensor("kscrl", [D, SEQ], BF16, kind="Internal"))
        self.s_v = ap(nc.dram_tensor("vscr", [SEQ, D], F32, kind="Internal"))
        self.s_q = ap(nc.dram_tensor("qscr", [D, TOWN], BF16, kind="Internal"))
        self.s_ql = ap(nc.dram_tensor("qscrl", [D, TOWN], BF16, kind="Internal"))
        self.s_ctx = ap(nc.dram_tensor("ctxscr", [D, TOWN], F32, kind="Internal"))
        self.s_x2 = ap(nc.dram_tensor("x2scr", [TOWN, D], F32, kind="Internal"))
        self.s_xb2 = ap(nc.dram_tensor("xb2scr", [D, TOWN], BF16, kind="Internal"))

        self.c_eye_bf = ap(nc.inline_tensor(np.eye(128, dtype=BF), name="eyebf"))
        self.c_eye_f = ap(nc.inline_tensor(np.eye(128, dtype=np.float32), name="eyef"))
        self.c_ones_row_f = ap(nc.inline_tensor(np.ones((1, 128), np.float32), name="onesrowf"))
        self.c_ones_col_f = ap(nc.inline_tensor(np.ones((128, 1), np.float32), name="onescolf"))
        self.c_ones_col_h = ap(nc.inline_tensor(np.ones((128, 1), np.float16), name="onescolh"))
        eh = np.zeros((128, 256), np.float32)
        for kc in range(16):
            eh[:, 16 * kc + kc] = 1.0
        self.c_ehead = ap(nc.inline_tensor(eh.astype(np.float16), name="ehead"))
        s8 = np.zeros((8, 8 * 128), np.float32)
        for e in range(8):
            s8[e, e * 128:(e + 1) * 128] = 1.0
        self.c_sel8 = ap(nc.inline_tensor(s8.astype(BF), name="sel8"))
        cp = np.zeros((128, 4), np.float32)
        cp[:, 0] = EPS_LN; cp[:, 1] = EPS_QK; cp[:, 2] = 0.0; cp[:, 3] = 1.0
        self.c_pool = ap(nc.inline_tensor(cp, name="cpool"))

    def alloc_globals(self):
        nc = self.nc
        A = nc.alloc_sbuf_tensor
        self.g_wslab_a = A("wslab_a", [128, NCD * 512], BF16)
        self.g_wslab_b = A("wslab_b", [128, NCD * 512], BF16)
        self.g_sqdump = A("sqdump", [128, D], BF16)
        self.g_eye_bf = A("eye_bf", [128, 128], BF16)
        self.g_eye_f = A("eye_f", [128, 128], F32)
        self.g_ones_row_f = A("ones_row_f", [1, 128], F32)
        self.g_ones_col_f = A("ones_col_f", [128, 1], F32)
        self.g_ones_col_h = A("ones_col_h", [128, 1], F16)
        self.g_ehead = A("ehead_s", [128, 256], F16)
        self.g_sel8 = A("sel8_s", [8, 1024], BF16)
        self.g_pool = A("pool_s", [128, 4], F32)
        self.g_gq = A("gq_s", [128, 16], F32)
        self.g_gk = A("gk_s", [128, 16], F32)
        self.g_Bneg = A("Bneg", [128, 16], F32)
        self.g_wt = A("wtslab", [128, 16 * 8], F32)
        self.g_wtT8 = A("wtT8", [8, TOWN], BF16)
        self.g_ebrow = A("ebrow", [1, 8], F32)
        self.g_ebb = A("ebb", [128, 8], F32)
        self.g_ghi = A("ghi", [128, 16 * 8], BF16)
        self.g_glo = A("glo", [128, 16 * 8], BF16)
        self.g_cols = A("cols", [128, 32], F32)
        self.g_km2 = A("kmax2", [16, 2], F32)
        self.g_bh = A("bh16", [16, 1], F32)
        self.g_bhrow = A("bhrow", [1, 16], F32)
        self.g_gl = A("glg", [128, 8], F32)
        self.g_ge = A("geg", [128, 8], F32)
        self.g_pr = A("prg", [128, 8], F32)
        self.g_b8 = A("b8g", [128, 8], F32)
        self.g_s1 = A("s1g", [128, 8], F32)
        nc.const_aps.register(F32, 0.0, self.g_pool[:, 2:3])
        nc.const_aps.register(F32, 1.0, self.g_pool[:, 3:4])

    def col(self, i):
        return self.g_cols[:, i:i + 1]

    def prologue(self):
        nc = self.nc
        sl = self.sem("sl"); pe = self.sem("pe"); ve = self.sem("ve")
        ctx = ExitStack()
        gw16 = ctx.enter_context(nc.sbuf_tensor("gw16", [16, 1024], F32))
        gwhi16 = ctx.enter_context(nc.sbuf_tensor("gwhi16", [16, 1024], BF16))
        gwlo16 = ctx.enter_context(nc.sbuf_tensor("gwlo16", [16, 1024], BF16))
        g16t = ctx.enter_context(nc.sbuf_tensor("g16t", [16, 128], BF16))
        g16q = ctx.enter_context(nc.sbuf_tensor("g16q", [16, 128], F32))
        g16k = ctx.enter_context(nc.sbuf_tensor("g16k", [16, 128], F32))
        for dst, src in [
            (self.g_eye_bf, self.c_eye_bf), (self.g_eye_f, self.c_eye_f),
            (self.g_ones_row_f, self.c_ones_row_f), (self.g_ones_col_f, self.c_ones_col_f),
            (self.g_ones_col_h, self.c_ones_col_h),
            (self.g_ehead, self.c_ehead), (self.g_sel8, self.c_sel8),
            (self.g_pool, self.c_pool),
        ]:
            sl.inc(nc.sync.dma_start(out=dst[:, :], in_=src), 16)
        sl.inc(nc.sync.dma_start(out=g16q[:, :],
                                 in_=self.i_qln.rearrange("(c p) -> c p", p=128)), 16)
        sl.inc(nc.sync.dma_start(out=g16k[:, :],
                                 in_=self.i_kln.rearrange("(c p) -> c p", p=128)), 16)
        sl.inc(nc.sync.dma_start(out=gw16[:, :],
                                 in_=self.i_gw.rearrange("(c p) e -> c (p e)", p=128)), 16)
        sl.inc(nc.sync.dma_start(out=self.g_ebrow[:, :], in_=self.i_eb[None, :]), 16)
        with nc.psum_tensor("ps_pro", [128, 128], F32) as ps:
            self.w(nc.tensor, "sl")
            pe.inc(nc.tensor.transpose(ps[:, 0:16], g16q[:, :], self.g_eye_f[0:16, 0:16]))
            self.w(nc.vector, "pe")
            ve.inc(nc.vector.tensor_copy(self.g_gq[:, :], ps[:, 0:16]))
            self.w(nc.tensor, "ve")
            pe.inc(nc.tensor.transpose(ps[:, 0:16], g16k[:, :], self.g_eye_f[0:16, 0:16]))
            self.w(nc.vector, "pe")
            ve.inc(nc.vector.tensor_copy(self.g_gk[:, :], ps[:, 0:16]))
            self.w(nc.tensor, "ve")
            pe.inc(nc.tensor.matmul(ps[:, 0:8], self.g_ones_row_f[0:1, :],
                                    self.g_ebrow[0:1, :], start=True, stop=True))
            self.w(nc.vector, "pe")
            ve.inc(nc.vector.tensor_copy(self.g_ebb[:, :], ps[:, 0:8]))
        ve.inc(nc.vector.tensor_copy(gwhi16[:, :], gw16[:, :]))
        self.w(nc.vector, "ve")
        ve.inc(nc.vector.tensor_tensor(gwlo16[:, :], gw16[:, :], gwhi16[:, :], ALU.subtract))
        with nc.psum_tensor("ps_pg", [128, 128], BF16) as psb:
            for j in range(8):
                for (src16, dst) in [(gwhi16, self.g_ghi), (gwlo16, self.g_glo)]:
                    self.w(nc.vector, "pe")
                    ve.inc(nc.vector.tensor_copy(
                        g16t[:, :],
                        src16.ap().rearrange("c (p e) -> c p e", e=8)[:, :, j]))
                    self.w(nc.tensor, "ve")
                    pe.inc(nc.tensor.transpose(psb[:, 0:16], g16t[:, :],
                                               self.g_eye_bf[0:16, 0:16]))
                    self.w(nc.vector, "pe")
                    ve.inc(nc.vector.tensor_copy(
                        dst.ap().rearrange("p (c e) -> p c e", e=8)[:, :, j],
                        psb[:, 0:16]))
        self.barrier()
        ctx.close()

    def p1(self):
        nc = self.nc
        with ExitStack() as ctx:
            E = ctx.enter_context
            st = dict(
                xbTh=E(nc.sbuf_tensor("xbTh", [128, NCD * TP], BF16)),
                xbTl=E(nc.sbuf_tensor("xbTl", [128, NCD * TP], BF16)),
                kraw=E(nc.sbuf_tensor("kraw", [128, NCD * TP], F32)),
                wfh=E(nc.sbuf_tensor("wfh", [128, NCD * 512], BF16)),
                wfl=E(nc.sbuf_tensor("wfl", [128, NCD * 512], BF16)),
                xsh=E(nc.sbuf_tensor("xsh", [128, D], BF16)),
                xsl=E(nc.sbuf_tensor("xsl", [128, D], BF16)),
                khl=E(nc.sbuf_tensor("khl", [128, 512], BF16)),
                kll=E(nc.sbuf_tensor("kll", [128, 512], BF16)),
                xt0=E(nc.sbuf_tensor("xt0", [128, D], F32)),
                xt1=E(nc.sbuf_tensor("xt1", [128, D], F32)),
                xbtok=E(nc.sbuf_tensor("xbtok", [128, D], F32)),
                sq0=E(nc.sbuf_tensor("sq0", [128, 512], F16)),
                sq1=E(nc.sbuf_tensor("sq1", [128, 512], F16)),
                t1f=E(nc.sbuf_tensor("t1f", [128, 512], F32)),
                t2f=E(nc.sbuf_tensor("t2f", [128, 512], F32)),
                sA=E(nc.sbuf_tensor("sArow", [1, TP], F32)),
                sB=E(nc.sbuf_tensor("sBrow", [1, TP], F32)),
                mrow=E(nc.sbuf_tensor("mrow", [1, TP], F32)),
                rrow=E(nc.sbuf_tensor("rrow", [1, TP], F32)),
                trow=E(nc.sbuf_tensor("trow", [1, TP], F32)),
                kn2=E(nc.sbuf_tensor("kn2", [16, SEQ], F16)),
                qn2=E(nc.sbuf_tensor("qn2", [16, TOWN], F16)),
                vtmp=E(nc.sbuf_tensor("vtmp", [128, 512], F32)),
                vtok=E(nc.sbuf_tensor("vtok", [128, TP], F32)),
                ps_p0=E(nc.psum_tensor("ps_p0", [128, 512], F32)),
                ps_p1=E(nc.psum_tensor("ps_p1", [128, 512], F32)),
                ps_sA=E(nc.psum_tensor("ps_sA", [1, 512], F32)),
                ps_sB=E(nc.psum_tensor("ps_sB", [1, 512], F32)),
                ps_s16=E(nc.psum_tensor("ps_s16", [16, 512], F32)),
                ps_tb=E(nc.psum_tensor("ps_tb1", [128, 1024], BF16)),
                ps_b0=E(nc.psum_tensor("ps_b0", [128, 512], F32)),
                ps_b1=E(nc.psum_tensor("ps_b1", [128, 512], F32)),
            )
            self._wf = {"tick": 0, "used": [0, 0], "q": []}
            for p in range(8):
                self._p1_stepA(p, "kv", st)
                self._p1_proj(p, "k", st)
                self._p1_proj_v(p, st)
            for p in range(4):
                self._p1_stepA(p, "q", st)
                self._p1_proj(p, "q", st)
            self._bounds_ps = st["ps_b0"]
            self._p1_bounds(st["kn2"], st["qn2"])
        self.barrier()

    def _xbT_rhs(self, slab, ci):
        v = slab.ap()[:, :]
        return v.rearrange("p (i ct) -> p i ct", i=NT)[:, :, ci * 128:(ci + 1) * 128]

    def _p1_stepA(self, pidx, kind, st):
        nc = self.nc
        xl = self.sem("xl"); pe = self.sem("pe"); ve = self.sem("ve"); ae = self.sem("ae")
        src = self.i_xk if kind == "kv" else self.i_xq
        row0 = pidx * TP
        xbtok = st["xbtok"]
        for i in range(NT):
            xt = st[f"xt{i % 2}"]
            if i >= 2:
                self.w(nc.sync, "ve", "ae")
            xA = self.sem(f"xA{i % 2}")
            v_xl = xA.inc(nc.sync.dma_start(
                out=xt[:, :], in_=src[row0 + i * 128: row0 + (i + 1) * 128, :]), 16)
            nc.vector.wait_ge(xA.h, v_xl)
            ve.inc(nc.vector.reduce_sum(self.col(0 + (i % 2)), xt[:, :], axis=AXX))
            nc.scalar.wait_ge(xA.h, v_xl)
            self.w(nc.scalar, "ve")
            ae.inc(nc.scalar.activation(self.g_sqdump[:, :], xt[:, :], AF.Square,
                                        accum_out=self.col(2 + (i % 2))))
            c_m = self.col(4 + (i % 2)); c_v = self.col(6)
            c_r = self.col(8 + (i % 2)); c_t = self.col(7)
            self.w(nc.vector, "ae")
            ve.inc(nc.vector.tensor_scalar(c_m, self.col(0 + (i % 2)), INV_D, None, ALU.mult))
            ve.inc(nc.vector.tensor_scalar(c_t, self.col(2 + (i % 2)), INV_D, None, ALU.mult))
            self.w(nc.vector, "ve")
            ve.inc(nc.vector.tensor_tensor(self.col(12), c_m, c_m, ALU.mult))
            self.w(nc.vector, "ve")
            ve.inc(nc.vector.tensor_tensor(c_v, c_t, self.col(12), ALU.subtract))
            self.w(nc.scalar, "ve")
            ae.inc(nc.scalar.activation(self.col(13), c_v, AF.Ln,
                                        bias=self.g_pool[:, 0:1], scale=1.0))
            self.w(nc.scalar, "ae")
            ae.inc(nc.scalar.activation(c_r, self.col(13), AF.Exp, bias=0.0, scale=-0.5))
            self.w(nc.vector, "ae", "pe")
            ve.inc(nc.vector.tensor_scalar(xbtok[:, :], xt[:, :], c_m, c_r,
                                           ALU.subtract, ALU.mult))
            self.w(nc.vector, "ve")
            ve.inc(nc.vector.tensor_copy(st["xsh"][:, :], xbtok[:, :]))
            self.w(nc.vector, "ve")
            ve.inc(nc.vector.tensor_tensor(st["xsl"][:, :], xbtok[:, :], st["xsh"][:, :],
                                           ALU.subtract))
            nc.tensor.wait_ge(ve.h, ve.n)
            for (tok, slab) in [(st["xsh"], st["xbTh"]), (st["xsl"], st["xbTl"])]:
                for c4 in range(4):
                    self.w(nc.tensor, "ve")
                    for j in range(4):
                        ci = c4 * 4 + j
                        mm = nc.tensor.transpose(st["ps_tb"][:, j * 128:(j + 1) * 128],
                                                 tok[:, ci * 128:(ci + 1) * 128],
                                                 self.g_eye_bf[:, :])
                    pe.inc(mm)
                    nc.vector.wait_ge(pe.h, pe.n)
                    ve.inc(nc.vector.tensor_copy(
                        slab[:, i * D + c4 * 512: i * D + (c4 + 1) * 512],
                        st["ps_tb"][:, 0:512]))

    def _proj_half(self, st, colbase):
        nc = self.nc
        ve = self.sem("ve")
        t = self._wf
        b = t["tick"] % 2
        t["tick"] += 1
        ws = self.sem(f"wf{b}")
        nc.gpsimd.wait_ge(self.sem("pe").h, t["used"][b])
        wfh_half = st["wfh"].ap().rearrange("p (c u) -> p c u", u=512)[:, :, b * 256:(b + 1) * 256]
        wfl_half = st["wfl"].ap().rearrange("p (c u) -> p c u", u=512)[:, :, b * 256:(b + 1) * 256]
        wsrc = self.i_wqkv[:, colbase:colbase + 256].rearrange("(c p) u -> p c u", p=128)
        for hlf in range(2):
            ws.inc(nc.gpsimd.dma_start(out=wfh_half[:, hlf * 8:(hlf + 1) * 8, :],
                                       in_=wsrc[:, hlf * 8:(hlf + 1) * 8, :]), 16)
        v_hi = ws.n
        nc.vector.wait_ge(ws.h, v_hi)
        v_ng = ve.inc(nc.vector.tensor_scalar(wfl_half, wfh_half, -1.0, None, ALU.mult))
        nc.gpsimd.wait_ge(ve.h, v_ng)
        for hlf in range(2):
            ws.inc(nc.gpsimd.dma_start(out=wfl_half[:, hlf * 8:(hlf + 1) * 8, :],
                                       in_=wsrc[:, hlf * 8:(hlf + 1) * 8, :],
                                       accum_op=ALU.add), 16)
        t["q"].append((b, ws, ws.n))

    def _p1_proj(self, pidx, kind, st):
        nc = self.nc
        pe = self.sem("pe"); ve = self.sem("ve"); ae = self.sem("ae")
        kraw = st["kraw"]
        colb0 = D if kind == "k" else 0
        self._proj_half(st, colb0)
        evac_done = {}
        for hw in range(8):
            if hw + 1 < 8:
                self._proj_half(st, colb0 + (hw + 1) * 256)
            b, ws, vw = self._wf["q"].pop(0)
            nc.tensor.wait_ge(ws.h, vw)
            for oc2 in range(2):
                kc = hw * 2 + oc2
                psp = (st["ps_p0"], st["ps_p1"])[kc % 2]
                if kc >= 2:
                    nc.tensor.wait_ge(ve.h, evac_done[kc - 2])
                nterm = 3 * NCD
                ti = 0
                for (wsl_, xsl_) in [(st["wfh"], st["xbTh"]), (st["wfh"], st["xbTl"]),
                                     (st["wfl"], st["xbTh"])]:
                    for ci in range(NCD):
                        mm = nc.tensor.matmul(
                            psp[:, :],
                            wsl_[:, ci * 512 + b * 256 + oc2 * 128: ci * 512 + b * 256 + (oc2 + 1) * 128],
                            self._xbT_rhs(xsl_, ci),
                            start=(ti == 0), stop=(ti == nterm - 1))
                        ti += 1
                pe.inc(mm)
                v_mm = pe.n
                if oc2 == 1:
                    self._wf["used"][b] = pe.n
                nc.vector.wait_ge(pe.h, v_mm)
                self.w(nc.vector, "sl")
                dst = kraw[:, kc * TP:(kc + 1) * TP]
                ve.inc(nc.vector.tensor_copy(dst, psp[:, :]))
                evac_done[kc] = ve.n
                self.w(nc.scalar, "ve")
                nc.scalar.wait_ge(pe.h, pe.n)
                sq = st[f"sq{kc % 2}"]
                ae.inc(nc.scalar.activation(sq[:, :], dst, AF.Square))
                self.w(nc.tensor, "ae")
                pe.inc(nc.tensor.matmul(st["ps_sA"][:, :], self.g_ones_col_f[:, 0:1], dst,
                                        start=(kc == 0), stop=(kc == NCD - 1)))
                pe.inc(nc.tensor.matmul(st["ps_sB"][:, :], self.g_ones_col_h[:, 0:1],
                                        sq[:, :], start=(kc == 0), stop=(kc == NCD - 1)))
        self._k_evac_last = (evac_done[NCD - 2], evac_done[NCD - 1])
        self.w(nc.vector, "pe")
        ve.inc(nc.vector.tensor_copy(st["sA"][0:1, :], st["ps_sA"][:, :]))
        ve.inc(nc.vector.tensor_copy(st["sB"][0:1, :], st["ps_sB"][:, :]))
        self.w(nc.vector, "ve")
        ve.inc(nc.vector.tensor_scalar(st["mrow"][0:1, :], st["sA"][0:1, :], INV_D, None, ALU.mult))
        ve.inc(nc.vector.tensor_scalar(st["sB"][0:1, :], st["sB"][0:1, :], INV_D, None, ALU.mult))
        self.w(nc.vector, "ve")
        ve.inc(nc.vector.tensor_tensor(st["trow"][0:1, :], st["mrow"][0:1, :],
                                       st["mrow"][0:1, :], ALU.mult))
        self.w(nc.vector, "ve")
        ve.inc(nc.vector.tensor_tensor(st["rrow"][0:1, :], st["sB"][0:1, :],
                                       st["trow"][0:1, :], ALU.subtract))
        self.w(nc.scalar, "ve")
        ae.inc(nc.scalar.activation(st["trow"][0:1, :], st["rrow"][0:1, :], AF.Ln,
                                    bias=self.g_pool[0:1, 1:2], scale=1.0))
        self.w(nc.scalar, "ae")
        ae.inc(nc.scalar.activation(st["rrow"][0:1, :], st["trow"][0:1, :], AF.Exp,
                                    bias=0.0, scale=-0.5))
        self._p1_normalize_flush(pidx, kind, st)

    def _p1_normalize_flush(self, pidx, kind, st):
        nc = self.nc
        pe = self.sem("pe"); ve = self.sem("ve"); ae = self.sem("ae"); sl = self.sem("sl")
        kraw = st["kraw"]; t1f = st["t1f"]; t2f = st["t2f"]
        gam = self.g_gk if kind == "k" else self.g_gq
        scr = (self.s_k, self.s_kl) if kind == "k" else (self.s_q, self.s_ql)
        n2sb = st["kn2"] if kind == "k" else st["qn2"]
        row0 = pidx * TP
        self.w(nc.tensor, "ae", "ve")
        pe.inc(nc.tensor.matmul(st["ps_b0"][:, :], self.g_ones_row_f[0:1, :],
                                st["mrow"][0:1, :], start=True, stop=True))
        pe.inc(nc.tensor.matmul(st["ps_b1"][:, :], self.g_ones_row_f[0:1, :],
                                st["rrow"][0:1, :], start=True, stop=True))
        v_bc = pe.n
        for kc in range(NCD):
            sli = kraw[:, kc * TP:(kc + 1) * TP]
            nc.vector.wait_ge(pe.h, v_bc)
            ve.inc(nc.vector.tensor_tensor(t1f[:, :], sli, st["ps_b0"][:, :], ALU.subtract))
            self.w(nc.vector, "ve")
            ve.inc(nc.vector.tensor_tensor(t2f[:, :], t1f[:, :], st["ps_b1"][:, :], ALU.mult))
            self.w(nc.vector, "ve")
            if kind == "k":
                ve.inc(nc.vector.tensor_scalar(sli, t2f[:, :], gam[:, kc:kc + 1], None, ALU.mult))
            else:
                ve.inc(nc.vector.tensor_scalar(sli, t2f[:, :], gam[:, kc:kc + 1], RS2,
                                               ALU.mult, ALU.mult))
            self.w(nc.scalar, "ve")
            nc.scalar.wait_ge(pe.h, pe.n)
            sq = st[f"sq{kc % 2}"]
            ae.inc(nc.scalar.activation(sq[:, :], sli, AF.Square))
            nc.tensor.wait_ge(ae.h, ae.n)
            pe.inc(nc.tensor.matmul(st["ps_s16"][:, :],
                                    self.g_ehead[:, kc * 16:(kc + 1) * 16],
                                    sq[:, :], start=(kc == 0), stop=(kc == NCD - 1)))
            self.w(nc.vector, "sl")
            ve.inc(nc.vector.tensor_copy(st["khl"][:, :], sli))
            self.w(nc.vector, "ve")
            ve.inc(nc.vector.tensor_tensor(st["kll"][:, :], sli, st["khl"][:, :],
                                           ALU.subtract))
            self.w(nc.sync, "ve")
            sl.inc(nc.sync.dma_start(
                out=scr[0][kc * 128:(kc + 1) * 128, row0:row0 + TP],
                in_=st["khl"][:, :]), 16)
            sl.inc(nc.sync.dma_start(
                out=scr[1][kc * 128:(kc + 1) * 128, row0:row0 + TP],
                in_=st["kll"][:, :]), 16)
        self.w(nc.vector, "pe")
        ve.inc(nc.vector.tensor_copy(n2sb[:, row0:row0 + TP], st["ps_s16"][:, :]))

    def _p1_proj_v(self, pidx, st):
        nc = self.nc
        pe = self.sem("pe"); ve = self.sem("ve"); wl = self.sem("wl"); sl = self.sem("sl")
        row0 = pidx * TP
        vtmp = st["vtmp"]; vtok = st["vtok"]
        self._proj_half(st, 2 * D)
        evd = {}
        tpd = {}
        fld = {}
        for hw in range(8):
            if hw + 1 < 8:
                self._proj_half(st, 2 * D + (hw + 1) * 256)
            b, ws, vw = self._wf["q"].pop(0)
            nc.tensor.wait_ge(ws.h, vw)
            for oc2 in range(2):
                vc = hw * 2 + oc2
                psp = (st["ps_p0"], st["ps_p1"])[vc % 2]
                if vc >= 2:
                    nc.tensor.wait_ge(ve.h, evd[vc - 2])
                else:
                    nc.tensor.wait_ge(ve.h, self._k_evac_last[vc])
                nterm = 3 * NCD
                ti = 0
                for (wsl_, xsl_) in [(st["wfh"], st["xbTh"]), (st["wfh"], st["xbTl"]),
                                     (st["wfl"], st["xbTh"])]:
                    for ci in range(NCD):
                        mm = nc.tensor.matmul(
                            psp[:, :],
                            wsl_[:, ci * 512 + b * 256 + oc2 * 128: ci * 512 + b * 256 + (oc2 + 1) * 128],
                            self._xbT_rhs(xsl_, ci),
                            start=(ti == 0), stop=(ti == nterm - 1))
                        ti += 1
                pe.inc(mm)
                v_mm = pe.n
                if oc2 == 1:
                    self._wf["used"][b] = pe.n
                nc.vector.wait_ge(pe.h, v_mm)
                ve.inc(nc.vector.tensor_copy(vtmp[:, :], psp[:, :]))
                evd[vc] = ve.n
                nc.tensor.wait_ge(ve.h, evd[vc])
                if vc >= 1:
                    nc.tensor.wait_ge(ve.h, tpd[vc - 1])
                for j in range(4):
                    mm = nc.tensor.transpose(st["ps_b0"][:, j * 128:(j + 1) * 128],
                                             vtmp[:, j * 128:(j + 1) * 128],
                                             self.g_eye_f[:, :])
                pe.inc(mm)
                v_tp = pe.n
                nc.vector.wait_ge(pe.h, v_tp)
                if vc >= 1:
                    nc.vector.wait_ge(sl.h, fld[vc - 1])
                ve.inc(nc.vector.tensor_copy(vtok[:, :], st["ps_b0"][:, :]))
                tpd[vc] = ve.n
                nc.sync.wait_ge(ve.h, tpd[vc])
                sl.inc(nc.sync.dma_start(
                    out=self.s_v[row0:row0 + TP, vc * 128:(vc + 1) * 128]
                        .rearrange("(tt p) c -> p tt c", p=128),
                    in_=vtok.ap().rearrange("p (tt c) -> p tt c", c=128)), 16)
                fld[vc] = sl.n

    def _p1_bounds(self, kn2, qn2):
        nc = self.nc
        pe = self.sem("pe"); ve = self.sem("ve"); ae = self.sem("ae")
        self.w(nc.vector, "pe", "ae", "sl")
        km = self.g_km2
        ve.inc(nc.vector.reduce_max(km[:, 0:1], kn2[:, :], axis=AXX))
        ve.inc(nc.vector.reduce_max(km[:, 1:2], qn2[:, :], axis=AXX))
        self.w(nc.scalar, "ve")
        ae.inc(nc.scalar.activation(km[:, 0:1], km[:, 0:1], AF.Sqrt))
        self.w(nc.scalar, "ae")
        ae.inc(nc.scalar.activation(km[:, 1:2], km[:, 1:2], AF.Sqrt))
        self.w(nc.vector, "ae")
        ve.inc(nc.vector.tensor_tensor(self.g_bh[:, :], km[:, 0:1], km[:, 1:2], ALU.mult))
        psb = self._bounds_ps
        self.w(nc.tensor, "ve")
        pe.inc(nc.tensor.transpose(psb[0:1, 0:16], self.g_bh[:, :],
                                   self.g_eye_f[0:16, 0:16]))
        self.w(nc.vector, "pe")
        ve.inc(nc.vector.tensor_scalar(self.g_bhrow[0:1, :], psb[0:1, 0:16],
                                       -1.0, None, ALU.mult))
        self.w(nc.tensor, "ve")
        pe.inc(nc.tensor.matmul(psb[:, 0:16], self.g_ones_row_f[0:1, :],
                                self.g_bhrow[0:1, :], start=True, stop=True))
        self.w(nc.vector, "pe")
        ve.inc(nc.vector.tensor_copy(self.g_Bneg[:, :], psb[:, 0:16]))

    def p2p3(self):
        nc = self.nc
        with ExitStack() as ctx:
            E = ctx.enter_context
            st = dict(
                masksl=E(nc.sbuf_tensor("masksl", [128, 8 * 512], BF16)),
                qh0=E(nc.sbuf_tensor("qh0", [128, 512], BF16)),
                qh1=E(nc.sbuf_tensor("qh1", [128, 512], BF16)),
                ql0=E(nc.sbuf_tensor("ql0_t", [128, 512], BF16)),
                ql1=E(nc.sbuf_tensor("ql1_t", [128, 512], BF16)),
                kh0=E(nc.sbuf_tensor("kh0", [128, 8 * 128], BF16)),
                kh1=E(nc.sbuf_tensor("kh1", [128, 8 * 128], BF16)),
                kl0=E(nc.sbuf_tensor("kl0", [128, 8 * 128], BF16)),
                kl1=E(nc.sbuf_tensor("kl1", [128, 8 * 128], BF16)),
                vh0=E(nc.sbuf_tensor("vh0", [128, 8 * 128], F32)),
                vh1=E(nc.sbuf_tensor("vh1", [128, 8 * 128], F32)),
                pf0=E(nc.sbuf_tensor("pf0", [128, 1024], F32)),
                pf1=E(nc.sbuf_tensor("pf1", [128, 1024], F32)),
                pbm0=E(nc.sbuf_tensor("pbm0", [128, 512], F32)),
                pbm1=E(nc.sbuf_tensor("pbm1", [128, 512], F32)),
                pbm2=E(nc.sbuf_tensor("pbm2", [128, 512], F32)),
                pbm3=E(nc.sbuf_tensor("pbm3", [128, 512], F32)),
                dsum0=E(nc.sbuf_tensor("dsum0", [128, 512], F32)),
                dsum1=E(nc.sbuf_tensor("dsum1", [128, 512], F32)),
                rbsb=E(nc.sbuf_tensor("rbsb", [128, 512], F32)),
                drow=E(nc.sbuf_tensor("drow", [1, 512], F32)),
                ctxn0=E(nc.sbuf_tensor("ctxn0", [128, 512], F32)),
                ctxn1=E(nc.sbuf_tensor("ctxn1", [128, 512], F32)),
                ctxb0=E(nc.sbuf_tensor("ctxb0", [128, 128], F32)),
                ctxb1=E(nc.sbuf_tensor("ctxb1", [128, 128], F32)),
                wob0=E(nc.sbuf_tensor("wob0", [128, 512], F32)),
                wob1=E(nc.sbuf_tensor("wob1", [128, 512], F32)),
                wob2=E(nc.sbuf_tensor("wob2", [128, 512], F32)),
                wob3=E(nc.sbuf_tensor("wob3", [128, 512], F32)),
                ao=E(nc.sbuf_tensor("aorow", [128, D], F32)),
                rowf=E(nc.sbuf_tensor("rowf", [128, D], F32)),
                xhiT=E(nc.sbuf_tensor("xhiT", [128, D], BF16)),
                xloT=E(nc.sbuf_tensor("xloT", [128, D], BF16)),
                xhitok=E(nc.sbuf_tensor("xhitok", [128, D], BF16)),
                xlotok=E(nc.sbuf_tensor("xlotok", [128, D], BF16)),
                ps_S0=E(nc.psum_tensor("ps_S0", [128, 1024], F32)),
                ps_S1=E(nc.psum_tensor("ps_S1", [128, 1024], F32)),
                ps_ctx=E(nc.psum_tensor("ps_ctx", [128, 512], F32)),
                ps_D=E(nc.psum_tensor("ps_D", [1, 512], F32)),
                ps_tb=E(nc.psum_tensor("ps_tb", [128, 1024], BF16)),
                ps_g=E(nc.psum_tensor("ps_g", [128, 512], F32)),
            )
            self._p3_prev_last = 0
            for g in range(4):
                self._p2_group(g, st)
                self._p3_group(g, st)
        self.barrier()

    def _p2_load_q(self, g, h, st):
        nc = self.nc
        ql = self.sem("ql"); pe = self.sem("pe")
        if not hasattr(self, "_q_loaded"):
            self._q_loaded = [0, 0]
            self._kv_queue = []
            self._kv_tick = 0
            self._pf_cons = [0, 0]
            self._ctx_flushed = [0, 0]
            self._kv_first = True
            self._q_first = True
        if not (self._q_first and h < 2):
            nc.sync.wait_ge(pe.h, pe.n)
        if h >= 1:
            self._q_first = False
        qs = self.sem(f"ql{h % 2}")
        qs.inc(nc.sync.dma_start(
            out=st[f"qh{h % 2}"][:, :],
            in_=self.s_q[h * 128:(h + 1) * 128, g * 512:(g + 1) * 512]), 16)
        self._q_loaded[h % 2] = qs.inc(nc.sync.dma_start(
            out=st[f"ql{h % 2}"][:, :],
            in_=self.s_ql[h * 128:(h + 1) * 128, g * 512:(g + 1) * 512]), 16)

    def _p2_load_kv(self, g, h, cidx, st, nkt):
        nc = self.nc
        hl = self.sem("hl"); pe = self.sem("pe")
        if not self._kv_first:
            nc.sync.wait_ge(pe.h, pe.n)
        self._kv_first = False
        n = min(8, nkt - cidx * 8)
        b = self._kv_tick % 2
        self._kv_tick += 1
        hs = self.sem(f"hl{b}")
        hs.inc(nc.sync.dma_start(
            out=st[f"kh{b}"][:, 0:n * 128],
            in_=self.s_k[h * 128:(h + 1) * 128, cidx * 1024: cidx * 1024 + n * 128]), 16)
        hs.inc(nc.sync.dma_start(
            out=st[f"kl{b}"][:, 0:n * 128],
            in_=self.s_kl[h * 128:(h + 1) * 128, cidx * 1024: cidx * 1024 + n * 128]), 16)
        self._kv_queue.append((b, hs.inc(nc.sync.dma_start(
            out=st[f"vh{b}"].ap()[:, 0:n * 128].rearrange("p (kt c) -> p kt c", c=128),
            in_=self.s_v[cidx * 1024: cidx * 1024 + n * 128, h * 128:(h + 1) * 128]
                .rearrange("(kt p) c -> p kt c", p=128)), 16)))

    def _p2_group(self, g, st):
        nc = self.nc
        nkt = QG_NKT[g]
        pe = self.sem("pe"); ve = self.sem("ve"); ae = self.sem("ae")
        sl = self.sem("sl"); hl = self.sem("hl"); ql = self.sem("ql")
        self.w(nc.sync, "ve", "pe")
        for j in range(8):
            sl.inc(nc.sync.dma_start(out=st["masksl"][:, j * 512:(j + 1) * 512],
                                     in_=self.i_masks[g * 8 + j]), 16)
        v_m = sl.n
        nchk = (nkt + 7) // 8
        self._p2_load_q(g, 0, st)
        self._p2_load_kv(g, 0, 0, st, nkt)
        for h in range(NH):
            if h + 1 < NH:
                self._p2_load_q(g, h + 1, st)
            qh = st[f"qh{h % 2}"]; qlo = st[f"ql{h % 2}"]
            nc.tensor.wait_ge(self.sem(f"ql{h % 2}").h, self._q_loaded[h % 2])
            nc.tensor.wait_ge(sl.h, v_m)
            self.w(nc.tensor, "ve")
            exp_done = []
            cons_done = {}
            presum_done = {}
            band0 = 8 * g
            npair = nkt // 2
            pend = None

            def emit_cons(jj, srcs, ds, koff_, vh_):
                nc.tensor.wait_ge(ve.h, presum_done[jj])
                for half in range(2):
                    kt = jj * 2 + half
                    nc.tensor.matmul(st["ps_ctx"][:, :],
                                     vh_[:, (koff_ + half) * 128:(koff_ + half + 1) * 128],
                                     srcs[half],
                                     start=(kt == 0), stop=(kt == nkt - 1))
                mm_ = nc.tensor.matmul(st["ps_D"][:, :], self.g_ones_col_f[:, 0:1],
                                       ds[:, :], start=(jj == 0), stop=(jj == npair - 1))
                pe.inc(mm_)
                cons_done[jj] = pe.n

            for jp in range(npair):
                cidx = jp // 4
                if jp % 4 == 0:
                    if pend is not None:
                        emit_cons(*pend)
                        pend = None
                    if cidx + 1 < nchk:
                        self._p2_load_kv(g, h, cidx + 1, st, nkt)
                    elif h + 1 < NH:
                        self._p2_load_kv(g, h + 1, 0, st, QG_NKT[g])
                    self._kv_cur = self._kv_queue.pop(0)
                    nc.tensor.wait_ge(self.sem(f"hl{self._kv_cur[0]}").h, self._kv_cur[1])
                kh = st[f"kh{self._kv_cur[0]}"]; vh = st[f"vh{self._kv_cur[0]}"]
                klo = st[f"kl{self._kv_cur[0]}"]
                koff = (jp % 4) * 2
                psS = (st["ps_S0"], st["ps_S1"])[jp % 2]
                if jp >= 2:
                    nc.tensor.wait_ge(ae.h, exp_done[jp - 2])
                for half in range(2):
                    sliA = slice((koff + half) * 128, (koff + half + 1) * 128)
                    for ti, (kk, qq) in enumerate([(kh, qh), (kh, qlo), (klo, qh)]):
                        mm = nc.tensor.matmul(
                            psS[:, half * 512:(half + 1) * 512],
                            kk[:, sliA], qq[:, :], start=(ti == 0), stop=(ti == 2))
                pe.inc(mm)
                v_S = pe.n
                if pend is not None:
                    emit_cons(*pend)
                    pend = None
                nc.scalar.wait_ge(pe.h, v_S)
                if jp >= 2:
                    nc.scalar.wait_ge(ve.h, self._pf_cons[jp % 2])
                pf = st[f"pf{jp % 2}"]
                ae.inc(nc.scalar.activation(pf[:, :], psS[:, :], AF.Exp,
                                            bias=self.g_Bneg[:, h:h + 1], scale=1.0))
                exp_done.append(ae.n)
                srcs = []
                nc.vector.wait_ge(ae.h, exp_done[jp])
                if jp >= 2:
                    nc.vector.wait_ge(pe.h, cons_done[jp - 2])
                for half in range(2):
                    kt = jp * 2 + half
                    src_p = pf[:, half * 512:(half + 1) * 512]
                    if kt >= band0:
                        pbm = st[f"pbm{(jp % 2) * 2 + half}"]
                        ve.inc(nc.vector.tensor_tensor(
                            pbm[:, :], src_p,
                            st["masksl"][:, (kt - band0) * 512:(kt - band0 + 1) * 512],
                            ALU.mult))
                        src_p = pbm[:, :]
                    srcs.append(src_p)
                self.w(nc.vector, "ve")
                ds = st[f"dsum{jp % 2}"]
                ve.inc(nc.vector.tensor_tensor(ds[:, :], srcs[0], srcs[1], ALU.add))
                self._pf_cons[jp % 2] = ve.n
                presum_done[jp] = ve.n
                pend = (jp, srcs, ds, koff, vh)
            emit_cons(*pend)
            self.w(nc.vector, "pe")
            ve.inc(nc.vector.reciprocal(st["drow"][0:1, :], st["ps_D"][:, :]))
            self.w(nc.tensor, "ve")
            pe.inc(nc.tensor.matmul(st["ps_g"][:, 0:512], self.g_ones_row_f[0:1, :],
                                    st["drow"][0:1, :], start=True, stop=True))
            self.w(nc.vector, "pe")
            ve.inc(nc.vector.tensor_copy(st["rbsb"][:, :], st["ps_g"][:, 0:512]))
            self.w(nc.vector, "ve")
            cn = st[f"ctxn{h % 2}"]
            cf = self.sem(f"cf{h % 2}")
            nc.vector.wait_ge(cf.h, self._ctx_flushed[h % 2])
            ve.inc(nc.vector.tensor_tensor(cn[:, :], st["ps_ctx"][:, :], st["rbsb"][:, :],
                                           ALU.mult))
            self.w(nc.sync, "ve")
            self._ctx_flushed[h % 2] = cf.inc(nc.sync.dma_start(
                out=self.s_ctx[h * 128:(h + 1) * 128, g * 512:(g + 1) * 512],
                in_=cn[:, :]), 16)

    def _p3_group(self, g, st):
        nc = self.nc
        pe = self.sem("pe"); ve = self.sem("ve"); ae = self.sem("ae")
        sl = self.sem("sl"); xl = self.sem("xl"); cl = self.sem("cl")
        ao = st["ao"]; rowf = st["rowf"]
        for s in range(4):
            tok0 = g * 512 + s * 128
            views = [st["ps_S0"][:, 0:512], st["ps_S0"][:, 512:1024],
                     st["ps_S1"][:, 0:512], st["ps_S1"][:, 512:1024]]
            self.w(nc.tensor, "ae", "ve")
            pv = {}
            v_w = {}
            v_c = {}
            base_pe = pe.n

            def load_wob(t):
                ci_, eg_ = t // 4, t % 4
                wbs = self.sem(f"wb{t % 4}")
                nc.sync.wait_ge(pe.h, pv.get(t - 4, self._p3_prev_last))
                v_w[t] = (wbs, wbs.inc(nc.sync.dma_start(
                    out=st[f"wob{t % 4}"][:, :],
                    in_=self.i_wout[ci_ * 128:(ci_ + 1) * 128,
                                    eg_ * 512:(eg_ + 1) * 512]), 16))

            def load_cb(ci_):
                cbs = self.sem(f"cb{ci_ % 2}")
                nc.sync.wait_ge(pe.h, pv.get((ci_ - 2) * 4 + 3, self._p3_prev_last))
                v_c[ci_] = (cbs, cbs.inc(nc.sync.dma_start(
                    out=st[f"ctxb{ci_ % 2}"][:, :],
                    in_=self.s_ctx[ci_ * 128:(ci_ + 1) * 128, tok0:tok0 + 128]), 16))

            load_cb(0); load_cb(1)
            for t0 in range(3):
                load_wob(t0)
            for t in range(64):
                ci, eg = t // 4, t % 4
                if t + 3 < 64:
                    load_wob(t + 3)
                sw, vw = v_w[t]
                nc.tensor.wait_ge(sw.h, vw)
                sc, vc = v_c[ci]
                nc.tensor.wait_ge(sc.h, vc)
                pe.inc(nc.tensor.matmul(views[eg], st[f"ctxb{ci % 2}"][:, :],
                                        st[f"wob{t % 4}"][:, :],
                                        start=(ci == 0), stop=(ci == NCD - 1)))
                pv[t] = pe.n
                if eg == 3 and ci + 2 < NCD:
                    load_cb(ci + 2)
            self._p3_prev_last = pe.n
            self.w(nc.vector, "pe")
            for eg in range(4):
                ve.inc(nc.vector.tensor_copy(ao[:, eg * 512:(eg + 1) * 512], views[eg]))
            self.w(nc.vector, "ve")
            ve.inc(nc.vector.reduce_sum(self.col(16), ao[:, :], axis=AXX))
            self.w(nc.scalar, "ve")
            ae.inc(nc.scalar.activation(self.g_sqdump[:, :], ao[:, :], AF.Square,
                                        accum_out=self.col(17)))
            self._ln_cols(16, 17, 18, 19)
            self.w(nc.vector, "ae")
            ve.inc(nc.vector.tensor_scalar(rowf[:, :], ao[:, :], self.col(18), self.col(19),
                                           ALU.subtract, ALU.mult))
            v_ln2 = ve.n
            nc.sync.wait_ge(ve.h, v_ln2)
            v_x = xl.inc(nc.sync.dma_start(out=ao[:, :],
                                           in_=self.i_xq[tok0:tok0 + 128, :]), 16)
            nc.vector.wait_ge(xl.h, v_x)
            self.w(nc.vector, "ve")
            ve.inc(nc.vector.tensor_tensor(ao[:, :], rowf[:, :], ao[:, :], ALU.add))
            self.w(nc.sync, "ve")
            sl.inc(nc.sync.dma_start(out=self.s_x2[tok0:tok0 + 128, :], in_=ao[:, :]), 16)
            self.w(nc.vector, "ve")
            ve.inc(nc.vector.reduce_sum(self.col(20), ao[:, :], axis=AXX))
            self.w(nc.scalar, "ve")
            ae.inc(nc.scalar.activation(self.g_sqdump[:, :], ao[:, :], AF.Square,
                                        accum_out=self.col(21)))
            self._ln_cols(20, 21, 22, 23)
            self.w(nc.vector, "ae")
            ve.inc(nc.vector.tensor_scalar(rowf[:, :], ao[:, :], self.col(22), self.col(23),
                                           ALU.subtract, ALU.mult))
            self.w(nc.vector, "ve")
            ve.inc(nc.vector.tensor_copy(st["xhitok"][:, :], rowf[:, :]))
            self.w(nc.vector, "ve")
            ve.inc(nc.vector.tensor_tensor(st["xlotok"][:, :], rowf[:, :],
                                           st["xhitok"][:, :], ALU.subtract))
            v_hl = ve.n
            nc.tensor.wait_ge(ve.h, v_hl)
            for (tok, dstT) in [(st["xhitok"], st["xhiT"]), (st["xlotok"], st["xloT"])]:
                for c4 in range(4):
                    if not (tok is st["xhitok"] and c4 == 0):
                        nc.tensor.wait_ge(ve.h, ve.n)
                    for j in range(4):
                        ci = c4 * 4 + j
                        mm = nc.tensor.transpose(st["ps_tb"][:, j * 128:(j + 1) * 128],
                                                 tok[:, ci * 128:(ci + 1) * 128],
                                                 self.g_eye_bf[:, :])
                    pe.inc(mm)
                    self.w(nc.vector, "pe")
                    ve.inc(nc.vector.tensor_copy(dstT[:, c4 * 512:(c4 + 1) * 512],
                                                 st["ps_tb"][:, 0:512]))
            self.w(nc.sync, "ve")
            sl.inc(nc.sync.dma_start(
                out=self.s_xb2.rearrange("(c p) t -> p c t", p=128)[:, :, tok0:tok0 + 128],
                in_=st["xhiT"].ap().rearrange("p (c t) -> p c t", c=NCD)), 16)
            self.w(nc.tensor, "ve")
            first = True
            for (xT, gt) in [(st["xhiT"], self.g_ghi), (st["xhiT"], self.g_glo),
                             (st["xloT"], self.g_ghi)]:
                last_op = (xT is st["xloT"])
                for ci in range(NCD):
                    mm = nc.tensor.matmul(
                        st["ps_g"][:, 0:8],
                        xT[:, ci * 128:(ci + 1) * 128],
                        gt[:, ci * 8:(ci + 1) * 8],
                        start=first, stop=(last_op and ci == NCD - 1))
                    first = False
            pe.inc(mm)
            self._p3_gate(g * 4 + s, st["ps_g"])

    def _ln_cols(self, c_sum, c_ss, c_mean, c_rstd):
        nc = self.nc
        ve = self.sem("ve"); ae = self.sem("ae")
        self.w(nc.vector, "ae")
        ve.inc(nc.vector.tensor_scalar(self.col(c_mean), self.col(c_sum), INV_D, None, ALU.mult))
        ve.inc(nc.vector.tensor_scalar(self.col(24), self.col(c_ss), INV_D, None, ALU.mult))
        self.w(nc.vector, "ve")
        ve.inc(nc.vector.tensor_tensor(self.col(25), self.col(c_mean), self.col(c_mean), ALU.mult))
        self.w(nc.vector, "ve")
        ve.inc(nc.vector.tensor_tensor(self.col(26), self.col(24), self.col(25), ALU.subtract))
        self.w(nc.scalar, "ve")
        ae.inc(nc.scalar.activation(self.col(27), self.col(26), AF.Ln,
                                    bias=self.g_pool[:, 0:1], scale=1.0))
        self.w(nc.scalar, "ae")
        ae.inc(nc.scalar.activation(self.col(c_rstd), self.col(27), AF.Exp, bias=0.0, scale=-0.5))

    def _p3_gate(self, qsub, ps_g):
        nc = self.nc
        pe = self.sem("pe"); ve = self.sem("ve"); ae = self.sem("ae")
        gl, ge, pr, b8, s1 = self.g_gl, self.g_ge, self.g_pr, self.g_b8, self.g_s1
        self.w(nc.vector, "pe")
        ve.inc(nc.vector.tensor_copy(gl[:, :], ps_g[:, 0:8]))
        self.w(nc.vector, "ve")
        ve.inc(nc.vector.reduce_max(self.col(28), gl[:, :], axis=AXX))
        self.w(nc.vector, "ve")
        ve.inc(nc.vector.tensor_scalar(self.col(29), self.col(28), -1.0, None, ALU.mult))
        self.w(nc.scalar, "ve")
        ae.inc(nc.scalar.activation(ge[:, :], gl[:, :], AF.Exp, bias=self.col(29), scale=1.0,
                                    accum_out=self.col(30)))
        self.w(nc.vector, "ae")
        ve.inc(nc.vector.reciprocal(self.col(31), self.col(30)))
        self.w(nc.vector, "ve")
        ve.inc(nc.vector.tensor_scalar(pr[:, :], ge[:, :], self.col(31), None, ALU.mult))
        self.w(nc.vector, "ve")
        ve.inc(nc.vector.tensor_tensor(b8[:, :], pr[:, :], self.g_ebb[:, :], ALU.add))
        self.w(nc.vector, "ve")
        ve.inc(nc.vector.reduce_max(self.col(28), b8[:, :], axis=AXX))
        self.w(nc.vector, "ve")
        ve.inc(nc.vector.tensor_scalar(s1[:, :], b8[:, :], self.col(28), None, ALU.is_ge))
        self.w(nc.vector, "ve")
        ve.inc(nc.vector.scalar_tensor_tensor(ge[:, :], s1[:, :], -1e30, b8[:, :],
                                              ALU.mult, ALU.add))
        self.w(nc.vector, "ve")
        ve.inc(nc.vector.reduce_max(self.col(29), ge[:, :], axis=AXX))
        self.w(nc.vector, "ve")
        ve.inc(nc.vector.tensor_scalar(s1[:, :], b8[:, :], self.col(29), None, ALU.is_ge))
        self.w(nc.vector, "ve")
        ve.inc(nc.vector.tensor_tensor(self.g_wt[:, qsub * 8:(qsub + 1) * 8],
                                       pr[:, :], s1[:, :], ALU.mult))
        self.w(nc.tensor, "ve")
        psw = ps_g
        pe.inc(nc.tensor.transpose(psw[0:8, 0:128],
                                   self.g_wt[:, qsub * 8:(qsub + 1) * 8],
                                   self.g_eye_f[:, :]))
        self.w(nc.vector, "pe")
        ve.inc(nc.vector.tensor_copy(self.g_wtT8[:, qsub * 128:(qsub + 1) * 128],
                                     psw[0:8, 0:128]))

    def p4p5(self):
        nc = self.nc
        with ExitStack() as ctx:
            E = ctx.enter_context
            acc = E(nc.sbuf_tensor("acc", [128, 4 * D], F32))
            w2b = E(nc.sbuf_tensor("w2b", [128, NHC * 512], BF16))
            w2c = E(nc.sbuf_tensor("w2c", [128, NHC * 512], BF16))
            wsc = E(nc.sbuf_tensor("wslab_c", [128, NCD * 512], BF16))
            wsd = E(nc.sbuf_tensor("wslab_d", [128, NCD * 512], BF16))
            hsl = E(nc.sbuf_tensor("hsl", [128, NHC * 512], BF16))
            xeb = E(nc.sbuf_tensor("xeb", [128, NCD * 512], BF16))
            xbfb = E(nc.sbuf_tensor("xbfb", [128, NCD * 512], BF16))
            silub = E(nc.sbuf_tensor("silub", [128, 512], BF16))
            wbsb = E(nc.sbuf_tensor("wbsb", [128, 512], BF16))
            xrow = E(nc.sbuf_tensor("p5x", [128, D], F32))
            rowf = E(nc.sbuf_tensor("p5f", [128, D], F32))
            ps_A = E(nc.psum_tensor("ps_A", [128, 512], F32))
            ps_B = E(nc.psum_tensor("ps_B", [128, 512], F32))
            ps_A1 = E(nc.psum_tensor("ps_A1", [128, 512], F32))
            ps_B1 = E(nc.psum_tensor("ps_B1", [128, 512], F32))
            ps_C = E(nc.psum_tensor("ps_C", [128, 512], F32))
            ps_D2 = E(nc.psum_tensor("ps_D2", [128, 512], F32))
            ps_E = E(nc.psum_tensor("ps_E", [128, 512], F32))
            sl = self.sem("sl")
            for th in range(2):
                for tg in range(2):
                    tok0 = th * 1024 + tg * 512
                    self.w(nc.sync, "pe", "ve")
                    v_xb = sl.inc(nc.sync.dma_start(
                        out=xbfb.ap().rearrange("p (c t) -> p c t", c=NCD),
                        in_=self.s_xb2.rearrange("(c p) t -> p c t", p=128)
                            [:, :, tok0:tok0 + 512]), 16)
                    for ei, e in enumerate([8] + list(range(8))):
                        self._p4_expert(th, tg, e, ei == 0, v_xb, acc, (w2b, w2c), hsl,
                                        xeb, xbfb, silub, wbsb,
                                        (self.g_wslab_a, self.g_wslab_b, wsc, wsd),
                                        (ps_A, ps_A1), (ps_B, ps_B1), ps_C, ps_D2, ps_E)
                    self._p5_quarter(th, tg, acc, xrow, rowf)
        so = self.sem("so")
        nc.sync.wait_ge(so.h, so.n)

    def _p4_expert(self, th, tg, e, first, v_xb, acc, w2p, hsl, xeb, xbfb, silub, wbsb,
                   wsl, ps_As, ps_Bs, ps_C, ps_D2, ps_E):
        nc = self.nc
        pe = self.sem("pe"); ve = self.sem("ve"); ae = self.sem("ae")
        sl = self.sem("sl")
        tok0 = th * 1024 + tg * 512
        shared = (e == 8)
        if not hasattr(self, "_p4_w13"):
            self._p4_w13 = {"tick": 0, "used": [0, 0], "q": []}
            self._p4_w2 = {"tick": 0, "used": [0, 0], "q": []}
            self._p4_hb = 0
            self._p4_ab_used = [0, 0]
            self._p4_silu_used = 0
        wsz = [512, 512, 512, 256]

        def spec13(ee, ws):
            base = ws * 512
            ln_ = wsz[ws]
            if ee == 8:
                return (self.i_sw1, self.i_sw3, None, base, ln_)
            return (self.i_w1, self.i_w3, ee, base, ln_)

        def load_w13(ee, ws):
            t = self._p4_w13
            b = t["tick"] % 2
            t["tick"] += 1
            wlx = self.sem(f"wA{b}")
            nc.gpsimd.wait_ge(pe.h, t["used"][b])
            a1, a3, ee_, base, ln_ = spec13(ee, ws)
            if ee_ is None:
                s1ap = a1[:, base:base + ln_].rearrange("(c p) u -> p c u", p=128)
                s3ap = a3[:, base:base + ln_].rearrange("(c p) u -> p c u", p=128)
            else:
                s1ap = a1[ee_][:, base:base + ln_].rearrange("(c p) u -> p c u", p=128)
                s3ap = a3[ee_][:, base:base + ln_].rearrange("(c p) u -> p c u", p=128)
            d1 = wsl[2 * b].ap().rearrange("p (c u) -> p c u", u=512)[:, :, 0:ln_]
            d3 = wsl[2 * b + 1].ap().rearrange("p (c u) -> p c u", u=512)[:, :, 0:ln_]
            for hlf in range(2):
                wlx.inc(nc.gpsimd.dma_start(out=d1[:, hlf * 8:(hlf + 1) * 8, :],
                                            in_=s1ap[:, hlf * 8:(hlf + 1) * 8, :]), 16)
                wlx.inc(nc.gpsimd.dma_start(out=d3[:, hlf * 8:(hlf + 1) * 8, :],
                                            in_=s3ap[:, hlf * 8:(hlf + 1) * 8, :]), 16)
            t["q"].append((b, wlx, wlx.n))

        def load_w2(ee, dg):
            t = self._p4_w2
            b = t["tick"] % 2
            t["tick"] += 1
            wlx = self.sem(f"wB{b}")
            nc.gpsimd.wait_ge(pe.h, t["used"][b])
            if ee == 8:
                w2ap = self.i_sw2[:, dg * 512:(dg + 1) * 512].rearrange(
                    "(c p) u -> p c u", p=128)
            else:
                w2ap = self.i_w2[ee][:, dg * 512:(dg + 1) * 512].rearrange(
                    "(c p) u -> p c u", p=128)
            d2 = w2p[b].ap().rearrange("p (c u) -> p c u", u=512)
            for hlf in range(2):
                wlx.inc(nc.gpsimd.dma_start(out=d2[:, hlf * 7:(hlf + 1) * 7, :],
                                            in_=w2ap[:, hlf * 7:(hlf + 1) * 7, :]), 16)
            t["q"].append((b, wlx, wlx.n))

        if first and tg == 0 and th == 0:
            load_w13(8, 0)
        if not shared:
            self.w(nc.tensor, "ve")
            pe.inc(nc.tensor.matmul(ps_E[:, :], self.g_sel8[:, e * 128:(e + 1) * 128],
                                    self.g_wtT8[:, tok0:tok0 + 512], start=True, stop=True))
            self.w(nc.vector, "pe")
            ve.inc(nc.vector.tensor_copy(wbsb[:, :], ps_E[:, :]))
            nc.vector.wait_ge(sl.h, v_xb)
            self.w(nc.vector, "ve")
            for ci in range(NCD):
                ve.inc(nc.vector.tensor_tensor(xeb[:, ci * 512:(ci + 1) * 512],
                                               xbfb[:, ci * 512:(ci + 1) * 512],
                                               wbsb[:, :], ALU.mult))
            xsrc = xeb
            v_xe = ve.n
        else:
            xsrc = xbfb
            v_xe = None
        for ws in range(4):
            wlen = wsz[ws]
            if ws + 1 < 4:
                load_w13(e, ws + 1)
            else:
                load_w2(e, 0)
            myb, dsem, dval = self._p4_w13["q"].pop(0)
            nc.tensor.wait_ge(dsem.h, dval)
            if v_xe is not None:
                nc.tensor.wait_ge(ve.h, v_xe)
            else:
                nc.tensor.wait_ge(sl.h, v_xb)
            for hcw in range(wlen // 128):
                hc = ws * 4 + hcw
                hb = self._p4_hb % 2
                self._p4_hb += 1
                pa = ps_As[hb]; pb = ps_Bs[hb]
                nc.tensor.wait_ge(ve.h, self._p4_ab_used[hb])
                for ci in range(NCD):
                    nc.tensor.matmul(pa[:, :],
                                     wsl[2 * myb][:, ci * 512 + hcw * 128: ci * 512 + (hcw + 1) * 128],
                                     xsrc[:, ci * 512:(ci + 1) * 512],
                                     start=(ci == 0), stop=(ci == NCD - 1))
                for ci in range(NCD):
                    mm = nc.tensor.matmul(pb[:, :],
                                          wsl[2 * myb + 1][:, ci * 512 + hcw * 128: ci * 512 + (hcw + 1) * 128],
                                          xsrc[:, ci * 512:(ci + 1) * 512],
                                          start=(ci == 0), stop=(ci == NCD - 1))
                pe.inc(mm)
                v_ab = pe.n
                nc.scalar.wait_ge(pe.h, v_ab)
                nc.scalar.wait_ge(ve.h, self._p4_silu_used)
                ae.inc(nc.scalar.activation(silub[:, :], pa[:, :], AF.Silu))
                self.w(nc.vector, "ae")
                ve.inc(nc.vector.tensor_tensor(hsl[:, hc * 512:(hc + 1) * 512],
                                               silub[:, :], pb[:, :], ALU.mult))
                self._p4_ab_used[hb] = ve.n
                self._p4_silu_used = ve.n
            self._p4_w13["used"][myb] = pe.n
        for dg in range(4):
            nxt = [8] + list(range(8))
            ei_cur = nxt.index(e)
            if dg + 1 < 4:
                load_w2(e, dg + 1)
            else:
                if ei_cur + 1 < 9:
                    load_w13(nxt[ei_cur + 1], 0)
                elif not (th == 1 and tg == 1):
                    load_w13(8, 0)
            myb, dsem, dval = self._p4_w2["q"].pop(0)
            nc.tensor.wait_ge(dsem.h, dval)
            for tsub in range(4):
                psw = (ps_C, ps_D2)[tsub % 2]
                self.w(nc.tensor, "ve")
                for hc in range(NHC):
                    mm = nc.tensor.matmul(
                        psw[:, :],
                        hsl[:, hc * 512 + tsub * 128: hc * 512 + (tsub + 1) * 128],
                        w2p[myb][:, hc * 512:(hc + 1) * 512],
                        start=(hc == 0), stop=(hc == NHC - 1))
                pe.inc(mm)
                self.w(nc.vector, "pe", "ae")
                a_sl = acc[:, tsub * D + dg * 512: tsub * D + (dg + 1) * 512]
                if first:
                    self.w(nc.vector, "so")
                    ve.inc(nc.vector.tensor_copy(a_sl, psw[:, :]))
                else:
                    ve.inc(nc.vector.tensor_tensor(a_sl, psw[:, :], a_sl, ALU.add))
            self._p4_w2["used"][myb] = pe.n

    def _p5_quarter(self, th, tg, acc, xrow, rowf):
        nc = self.nc
        ve = self.sem("ve"); ae = self.sem("ae"); xl = self.sem("xl"); so = self.sem("so")
        for tsub in range(4):
            tok0 = th * 1024 + tg * 512 + tsub * 128
            a_sl = acc[:, tsub * D:(tsub + 1) * D]
            self.w(nc.sync, "ve")
            v_x = xl.inc(nc.sync.dma_start(out=xrow[:, :],
                                           in_=self.s_x2[tok0:tok0 + 128, :]), 16)
            self.w(nc.vector, "ve")
            ve.inc(nc.vector.reduce_sum(self.col(16), a_sl, axis=AXX))
            self.w(nc.scalar, "ve")
            ae.inc(nc.scalar.activation(self.g_sqdump[:, :], a_sl, AF.Square,
                                        accum_out=self.col(17)))
            self._ln_cols(16, 17, 18, 19)
            self.w(nc.vector, "ae")
            ve.inc(nc.vector.tensor_scalar(rowf[:, :], a_sl, self.col(18), self.col(19),
                                           ALU.subtract, ALU.mult))
            nc.vector.wait_ge(xl.h, v_x)
            self.w(nc.vector, "ve", "so")
            ve.inc(nc.vector.tensor_tensor(a_sl, rowf[:, :], xrow[:, :], ALU.add))
            self.w(nc.sync, "ve")
            so.inc(nc.sync.dma_start(out=self.o_out[tok0:tok0 + 128, :],
                                     in_=a_sl), 16)

    def build(self):
        self.declare()
        self.alloc_globals()
        with self.nc.named_scope("prologue"):
            self.prologue()
        with self.nc.named_scope("p1"):
            self.p1()
        with self.nc.named_scope("p2p3"):
            self.p2p3()
        with self.nc.named_scope("p4p5"):
            self.p4p5()
        return self.nc


def make_masks(pos):
    t = np.arange(128)[:, None]
    o = np.arange(128)[None, :]
    m = np.zeros((32, 128, 512), np.float32)
    for g in range(4):
        for j in range(8):
            kt = 8 * g + j
            for s in range(4):
                p = pos[4 * g + s]
                m[g * 8 + j][:, s * 128:(s + 1) * 128] = (128 * kt + t <= 128 * p + o)
    return m.astype(BF)


def kernel(L0_attn_x, L0_attn_Wqkv, L0_attn_q_ln_scale, L0_attn_k_ln_scale,
           L0_attn_out_W, L0_ffn_gate_w, L0_ffn_w1, L0_ffn_w2, L0_ffn_w3,
           L0_ffn_sw1, L0_ffn_sw2, L0_ffn_sw3, L0_ffn_expert_bias):
    x = np.ascontiguousarray(np.asarray(L0_attn_x, dtype=np.float32))
    B, S, _ = x.shape
    b = Builder()
    nc = b.build()
    shared = {
        "wqkv": np.ascontiguousarray(np.asarray(L0_attn_Wqkv, np.float32)),
        "qln": np.ascontiguousarray(np.asarray(L0_attn_q_ln_scale, np.float32)),
        "kln": np.ascontiguousarray(np.asarray(L0_attn_k_ln_scale, np.float32)),
        "wout": np.ascontiguousarray(np.asarray(L0_attn_out_W, np.float32)),
        "gw": np.ascontiguousarray(np.asarray(L0_ffn_gate_w, np.float32)),
        "eb": np.ascontiguousarray(np.asarray(L0_ffn_expert_bias, np.float32)),
        "w1": np.ascontiguousarray(np.asarray(L0_ffn_w1, np.float32)),
        "w2": np.ascontiguousarray(np.asarray(L0_ffn_w2, np.float32)),
        "w3": np.ascontiguousarray(np.asarray(L0_ffn_w3, np.float32)),
        "sw1": np.ascontiguousarray(np.asarray(L0_ffn_sw1, np.float32)),
        "sw2": np.ascontiguousarray(np.asarray(L0_ffn_sw2, np.float32)),
        "sw3": np.ascontiguousarray(np.asarray(L0_ffn_sw3, np.float32)),
    }
    in_maps = []
    for c in range(8):
        bi = c // 2
        pos = positions_for_core(c)
        xb = x[bi]
        xq = np.ascontiguousarray(xb.reshape(32, 128, D)[pos].reshape(TOWN, D))
        m = {"xq": xq, "xk": np.ascontiguousarray(xb), "masks": make_masks(pos)}
        m.update(shared)
        in_maps.append(m)
    res = run_bass_kernel_spmd(nc, in_maps, core_ids=list(range(8)))
    out = np.empty((B, S, D), np.float32)
    for c in range(8):
        bi = c // 2
        pos = positions_for_core(c)
        o = res.results[c]["out"].reshape(16, 128, D)
        for j, p in enumerate(pos):
            out[bi, p * 128:(p + 1) * 128, :] = o[j]
    return out
